# version 08
# speedup vs baseline: 1.2044x; 1.0307x over previous
import math
import numpy as np
from contextlib import ExitStack

import concourse.bass as bass
import concourse.mybir as mybir
from concourse.bass_utils import run_bass_kernel_spmd

F32 = mybir.dt.float32
BF16 = mybir.dt.bfloat16
AF = mybir.ActivationFunctionType
ALU = mybir.AluOpType
AX = mybir.AxisListType

D = 2048
NK = 16
TL = 512
NTC = 4
NTT = 16
NH = 16
HD = 128
EPS_LN = 1e-5
RS128 = 1.0 / math.sqrt(128.0)



def _tile_w(w):
    din, dout = w.shape
    return np.ascontiguousarray(
        w.reshape(din // 128, 128, dout // 128, 128).transpose(2, 1, 0, 3)
    ).astype(np.float32, copy=False)


def _tile_rhs_v(w):
    return np.ascontiguousarray(
        np.asarray(w).reshape(16, 128, 4, 512).transpose(2, 1, 0, 3)
    ).astype(np.float32, copy=False)


def _perpart(v):
    return np.ascontiguousarray(np.asarray(v).reshape(NK, 128).T).astype(np.float32, copy=False)


def _core_token_idx(h):
    j = np.arange(NTT)
    return ((2 * j + h)[:, None] * 128 + np.arange(128)[None, :]).reshape(-1)



class Ctr:
    def __init__(self, h):
        self.h = h
        self.v = 0

    def inc(self, inst, n=1):
        inst.then_inc(self.h, n)
        self.v += n
        return self.v


class Bld:

    def __init__(self, name="mk"):
        self.nc = bass.Bass("TRN2", target_bir_lowering=False, name=name, num_swdge_queues=2)
        self.es = ExitStack()
        self._sem = {}
        self._block = None

    def sem(self, name) -> Ctr:
        if name not in self._sem:
            self._sem[name] = Ctr(self.es.enter_context(self.nc.semaphore(name)))
        return self._sem[name]

    def sb(self, name, shape, dt):
        return self.es.enter_context(self.nc.sbuf_tensor(name, shape, dt))

    def ps(self, name, shape, dt=F32):
        return self.es.enter_context(self.nc.psum_tensor(name, shape, dt))

    def ring(self, name, n):
        return [self.sem(f"{name}{i}") for i in range(n)]

    def fence(self, eng, inst):
        s = self.sem("s_f")
        s.inc(inst)
        eng.wait_ge(s.h, s.v)

    def start(self):
        self._block = self.nc.Block()
        self.block = self._block.__enter__()
        return self

    def on(self, eng, fn):
        getattr(self.block, eng)(fn)

    def finish(self):
        self._block.__exit__(None, None, None)
        self.es.close()
        return self.nc


def alloc_common(b: Bld):
    b.xn = b.sb("xn_buf", [128, 32768], BF16)
    b.aux = b.sb("aux_buf", [128, 32768], BF16)
    b.xtmp = b.sb("xtmp", [128, 4096], F32)
    b.wring = b.sb("wring", [128, 8192], BF16)
    b.stat = b.sb("stat", [128, 2048], F32)
    b.wbc = b.sb("wbc", [128, 2048], BF16)
    b.vec = b.sb("vecbuf", [128, 64], F32)
    b.b1v = b.sb("b1v", [128, 128], F32)
    b.rt = b.sb("rtbuf", [128, 128], BF16)
    b.gwb = b.sb("gwb", [128, 64], BF16)
    b.ovl = b.sb("ovl", [128, 8192], BF16)
    b.wrow = b.ovl[0:1, :]
    b.pp = b.sb("pp", [128, 4096], BF16)
    b.io = b.sb("io", [128, 4096], BF16)
    b.ones = b.sb("ones128", [128, 128], BF16)
    b.onesrow = b.sb("onesrow", [1, 128], BF16)
    b.idf = b.sb("idf32", [128, 128], F32)
    b.psA = b.ps("psA", [128, 1024])
    b.psB = b.ps("psB", [128, 1024])
    b.psBig = b.ps("psBig", [128, 2048])
    b.psC = b.psBig[:, 0:512]
    b.psD = b.psBig[:, 512:1024]
    b.psE = b.psBig[:, 1024:1536]
    b.psF = b.psBig[:, 1536:2048]

    b.rx = b.ring("rx", 4)
    b.rs = b.ring("rs", 4)
    b.rw = b.ring("rw", 4)
    b.ri = b.ring("ri", 8)
    b.rk4 = b.ring("rk4", 4)
    b.rv4 = b.ring("rv4", 4)
    b.rk = b.ring("rk", 2)
    b.rv = b.ring("rv", 2)
    b.rq = b.ring("rq", 2)
    s_ps = b.sem("s_ps")

    def g_init(g):
        g.memset(b.ones[:], 1.0)
        g.memset(b.onesrow[:], 1.0)
        g.memset(b.vec[:, 62:63], EPS_LN)
        inst = g.memset(b.vec[:, 63:64], 1e-6)
        s_ps.inc(inst)

    b.on("gpsimd", g_init)
    b.init_done = s_ps.v


class WCache:

    SUB = 16

    def __init__(self, b: Bld):
        self.b = b
        self.gsem = {}
        self.group_done = {}
        self.counts = {}

    def fill(self, items, gate=None):
        ords = dict(self.counts)
        plan = []
        for g, srcap, dstap in items:
            o = ords.get(g, 0)
            ords[g] = o + 1
            sub = o // self.SUB
            key = (g, sub)
            if key not in self.gsem:
                self.gsem[key] = self.b.sem(f"wc_{g}_{sub}")
            plan.append((key, srcap, dstap))
        self.counts.update(ords)

        def gfn(gp):
            if gate is not None:
                gp.wait_ge(gate[0].h, gate[1])
            for key, srcap, dstap in plan:
                inst = gp.dma_start(dstap, srcap)
                self.gsem[key].inc(inst, 16)
                self.group_done[key] = self.gsem[key].v
                self.group_done[key[0]] = None
        self.b.on("gpsimd", gfn)

    def wait(self, eng, grp, ordinal=None):
        if ordinal is not None:
            key = (grp, ordinal // self.SUB)
            eng.wait_ge(self.gsem[key].h, self.group_done[key])
        else:
            n = self.counts[grp]
            for sub in range((n + self.SUB - 1) // self.SUB):
                key = (grp, sub)
                eng.wait_ge(self.gsem[key].h, self.group_done[key])



def emit_ln(b: Bld, x_dram, xbase_dram=None, xscale=0.5, xn_dram=None):
    s_pe, s_dv, s_ac, s_ps = (b.sem(n) for n in ("s_pe", "s_dv", "s_ac", "s_ps"))
    s_xn = b.sem("s_xn") if xn_dram is not None else None
    mean = b.stat[:, 0:2048]
    rstd = b.pp[:, :].bitcast(F32)[:, 0:2048]
    m2buf = b.aux[:, 8192:16384].bitcast(F32)[:, 0:2048]

    def slot(k):
        return b.xtmp[:, (k % 2) * 2048:(k % 2 + 1) * 2048]

    def xb_ap(k):
        return b.aux[:, (k % 2) * 2048:(k % 2 + 1) * 2048]

    def x2_ap(k):
        return b.aux[:, 4096 + (k % 2) * 2048: 4096 + (k % 2 + 1) * 2048]

    prev_norm = getattr(b, '_ln_prev_norm', None)
    prev_store = getattr(b, '_ln_prev_store', None)
    ld1 = {}
    cast1 = {}
    sq1 = {}
    mm1 = {}
    for k in range(NK):
        def sy1(sy, k=k):
            if k >= 2:
                sy.wait_ge(s_dv.h, cast1[k - 2])
                sy.wait_ge(s_ac.h, sq1[k - 2])
            else:
                if prev_norm is not None:
                    sy.wait_ge(s_dv.h, prev_norm)
                if prev_store is not None:
                    for p in range(4):
                        sy.wait_ge(b.rs[p].h, prev_store[p])
            inst = sy.dma_start(slot(k), x_dram[k * 128:(k + 1) * 128, :])
            ld1[k] = b.rx[k % 2].inc(inst, 16)
        b.on("sync", sy1)

        def dv1(v, k=k):
            v.wait_ge(b.rx[k % 2].h, ld1[k])
            if k == 0 and prev_norm is not None:
                v.wait_ge(s_pe.h, b._ln_prev_pe) if hasattr(b, '_ln_prev_pe') else None
            inst = v.tensor_copy(xb_ap(k), slot(k))
            cast1[k] = s_dv.inc(inst)
        b.on("vector", dv1)

        def ac1(s, k=k):
            s.wait_ge(b.rx[k % 2].h, ld1[k])
            inst = s.activation(x2_ap(k), slot(k), AF.Square)
            sq1[k] = s_ac.inc(inst)
        b.on("scalar", ac1)

        def pe1(t, k=k):
            if k == 0:
                t.wait_ge(s_ps.h, b.init_done)
            t.wait_ge(s_dv.h, cast1[k])
            t.wait_ge(s_ac.h, sq1[k])
            for tc in range(NTC):
                t.matmul(b.psBig[:, tc * TL:(tc + 1) * TL], b.ones[:],
                         xb_ap(k)[:, tc * TL:(tc + 1) * TL],
                         start=(k == 0), stop=(k == NK - 1))
                psq = (b.psA if tc < 2 else b.psB)[:, (tc % 2) * TL:(tc % 2) * TL + TL]
                inst = t.matmul(psq, b.ones[:],
                                x2_ap(k)[:, tc * TL:(tc + 1) * TL],
                                start=(k == 0), stop=(k == NK - 1))
            s_pe.inc(inst)
            mm1[k] = s_pe.v
        b.on("tensor", pe1)
    b.ln_aux_free = mm1[NK - 1]

    def dv_stat(v):
        v.wait_ge(s_pe.h, mm1[NK - 1])
        v.tensor_scalar_mul(mean, b.psBig[:, 0:2048], 1.0 / D)
        v.tensor_scalar_mul(rstd[:, 0:1024], b.psA[:, 0:1024], 1.0 / D)
        inst = v.tensor_scalar_mul(rstd[:, 1024:2048], b.psB[:, 0:1024], 1.0 / D)
        b.fence(v, inst)
        inst = v.tensor_tensor(m2buf, mean, mean, ALU.mult)
        b.fence(v, inst)
        inst = v.tensor_tensor(rstd, rstd, m2buf, ALU.subtract)
        s_dv.inc(inst)
    b.on("vector", dv_stat)
    vard = s_dv.v
    b.ln_psCD_free = vard

    def ac_rstd(s):
        s.wait_ge(s_dv.h, vard)
        inst = s.activation(rstd, rstd, AF.Ln, bias=b.vec[:, 62:63])
        b.fence(s, inst)
        inst = s.activation(rstd, rstd, AF.Exp, scale=-0.5)
        s_ac.inc(inst)
    b.on("scalar", ac_rstd)
    rstdd = s_ac.v

    ld2 = {}
    norm2 = {}
    st2 = {}
    stn = {}
    for k in range(NK):
        def sy2(sy, k=k):
            if k >= 2:
                sy.wait_ge(s_dv.h, norm2[k - 2])
                if xbase_dram is not None:
                    sy.wait_ge(b.rs[k % 2].h, st2[k - 2])
            else:
                sy.wait_ge(s_dv.h, cast1[NK - 1])
                sy.wait_ge(s_ac.h, sq1[NK - 1])
            inst = sy.dma_start(slot(k), x_dram[k * 128:(k + 1) * 128, :])
            ld2[k] = b.rx[k % 2].inc(inst, 16)
        b.on("sync", sy2)

        def dv2(v, k=k):
            if k == 0:
                v.wait_ge(s_ac.h, rstdd)
            v.wait_ge(b.rx[k % 2].h, ld2[k])
            xnk = b.xn[:, k * 2048:(k + 1) * 2048]
            inst = v.tensor_tensor(m2buf, slot(k), mean, ALU.subtract)
            b.fence(v, inst)
            inst = v.tensor_tensor(xnk, m2buf, rstd, ALU.mult)
            if xbase_dram is not None:
                b.fence(v, inst)
                inst = v.scalar_tensor_tensor(slot(k), xnk, xscale, slot(k),
                                              ALU.mult, ALU.add)
            norm2[k] = s_dv.inc(inst)
        b.on("vector", dv2)

        def sy2b(sy, k=k):
            sy.wait_ge(s_dv.h, norm2[k])
            if xbase_dram is not None:
                inst = sy.dma_start(xbase_dram[k * 128:(k + 1) * 128, :], slot(k))
                st2[k] = b.rs[k % 2].inc(inst, 16)
            if xn_dram is not None:
                inst = sy.dma_start(xn_dram[k * 128:(k + 1) * 128, :],
                                    b.xn[:, k * 2048:(k + 1) * 2048])
                s_xn.inc(inst, 16)
        if xbase_dram is not None or xn_dram is not None:
            b.on("sync", sy2b)

    b.ln_norm_done = norm2[NK - 1]
    b.ln_xtmp_free = norm2[NK - 1]
    b.ln_store_ring = ({p: (b.rs[p % 2], st2[14 + p % 2]) for p in range(4)}
                       if xbase_dram is not None else None)
    b._ln_prev_norm = None
    b._ln_prev_store = None



def emit_l0_ffn(b: Bld, dr, wc: WCache, dbg=None):
    s_x, s_pe, s_dv, s_ac, s_st = (b.sem(n) for n in
                                   ("s_x", "s_pe", "s_dv", "s_ac", "s_st"))
    s_k = b.sem("s_k")
    s_g = b.sem("s_g")

    u_all = b.stat[:, 0:64]
    cnt = b.stat[:, 64:128]
    wu = b.stat[:, 128:192]
    s2 = b.stat[:, 192:208]
    r2 = b.stat[:, 208:224]
    wfin = b.stat[:, 256:320]

    def g_small(g):
        g.wait_ge(b.sem("s_dv").h, b.ln_norm_done)
        inst = g.dma_start(b.gwb[:, :].rearrange("p (k c) -> p k c", k=16),
                           dr['gwt'].rearrange("k p c -> p k c"))
        s_g.inc(inst, 16)
        inst = g.dma_start(b.b1v[:, 0:64], dr['b1r'])
        s_g.inc(inst, 16)
        inst = g.dma_start(b.pp[0:1, 0:4096], dr['b2row'][0:1, 0:4096])
        s_g.inc(inst, 16)
        inst = g.dma_start(b.io[0:1, 0:4096], dr['b2row'][0:1, 4096:8192])
        s_g.inc(inst, 16)
        inst = g.dma_start(b.idf[:, :], dr['idf'])
        s_g.inc(inst, 16)
    b.on("gpsimd", g_small)
    g_done = s_g.v

    def pe_router(t):
        t.wait_ge(s_g.h, g_done)
        t.wait_ge(s_dv.h, b.ln_norm_done)
        for tt in range(NTT):
            for k in range(NK):
                inst = t.matmul(b.psE[:, tt * 4:(tt + 1) * 4],
                                b.xn[:, k * 2048 + tt * 128: k * 2048 + (tt + 1) * 128],
                                b.gwb[:, k * 4:(k + 1) * 4],
                                start=(k == 0), stop=(k == NK - 1))
        s_pe.inc(inst)
    b.on("tensor", pe_router)
    rlogit = s_pe.v

    def dv_rmax(v):
        v.wait_ge(s_pe.h, rlogit)
        for tt in range(NTT):
            inst = v.reduce_max(b.vec[:, tt:tt + 1], b.psE[:, tt * 4:(tt + 1) * 4], axis=AX.X)
        b.fence(v, inst)
        inst = v.tensor_scalar_mul(b.vec[:, 0:16], b.vec[:, 0:16], -1.0)
        s_dv.inc(inst)
    b.on("vector", dv_rmax)
    rmax = s_dv.v

    def ac_uexp(s):
        s.wait_ge(s_dv.h, rmax)
        for tt in range(NTT):
            inst = s.activation(u_all[:, tt * 4:(tt + 1) * 4],
                                b.psE[:, tt * 4:(tt + 1) * 4], AF.Exp,
                                bias=b.vec[:, tt:tt + 1])
        s_ac.inc(inst)
    b.on("scalar", ac_uexp)
    uexp = s_ac.v

    gts = b.stat[:, 320:512]
    te = b.stat[:, 512:576]

    def dv_top2(v):
        v.wait_ge(s_ac.h, uexp)
        gi = 0
        for e in range(4):
            for j in range(4):
                if j == e:
                    continue
                inst = v.tensor_tensor(gts[:, gi * 16:(gi + 1) * 16],
                                       u_all[:, j::4], u_all[:, e::4], ALU.is_gt)
                gi += 1
        b.fence(v, inst)
        for e in range(4):
            inst = v.tensor_tensor(te[:, e * 16:(e + 1) * 16],
                                   gts[:, (3 * e) * 16:(3 * e + 1) * 16],
                                   gts[:, (3 * e + 1) * 16:(3 * e + 2) * 16], ALU.add)
        b.fence(v, inst)
        for e in range(4):
            inst = v.tensor_tensor(cnt[:, e::4], te[:, e * 16:(e + 1) * 16],
                                   gts[:, (3 * e + 2) * 16:(3 * e + 3) * 16], ALU.add)
        b.fence(v, inst)
        inst = v.tensor_scalar(cnt[:, :], cnt[:, :], 2.0, None, ALU.is_lt)
        b.fence(v, inst)
        inst = v.tensor_tensor(wu[:, :], cnt[:, :], u_all[:, :], ALU.mult)
        b.fence(v, inst)
        v.tensor_tensor(s2, wu[:, 0::4], wu[:, 1::4], ALU.add)
        inst = v.tensor_tensor(r2, wu[:, 2::4], wu[:, 3::4], ALU.add)
        b.fence(v, inst)
        inst = v.tensor_tensor(s2, s2, r2, ALU.add)
        b.fence(v, inst)
        inst = v.reciprocal(r2, s2)
        b.fence(v, inst)
        for e in range(4):
            inst = v.tensor_tensor(wfin[:, e::4], wu[:, e::4], r2, ALU.mult)
        s_dv.inc(inst)
    b.on("vector", dv_top2)
    wfin_done = s_dv.v
    if dbg is not None:
        def sy_dbg2(sy):
            sy.wait_ge(s_dv.h, wfin_done)
            inst = sy.dma_start(dbg['dbg_wf'], b.stat[:, 0:512])
            s_st.inc(inst, 16)
        b.on("sync", sy_dbg2)

    rowcp = {}
    prev = None
    for e in range(4):
        for grp in range(4):
            def pe_tr(t, e=e, grp=grp, prev=prev):
                if prev is None:
                    t.wait_ge(s_dv.h, wfin_done)
                else:
                    t.wait_ge(s_dv.h, prev)
                for q in range(4):
                    tt = grp * 4 + q
                    inst = t.transpose(b.psF[0:1, q * 128:(q + 1) * 128],
                                       wfin[:, tt * 4 + e: tt * 4 + e + 1],
                                       b.idf[:, :])
                s_pe.inc(inst)
            b.on("tensor", pe_tr)
            trd = s_pe.v

            def dv_row(v, e=e, grp=grp, trd=trd):
                v.wait_ge(s_pe.h, trd)
                inst = v.tensor_copy(
                    b.wrow[:, e * 2048 + grp * TL: e * 2048 + (grp + 1) * TL],
                    b.psF[0:1, 0:TL])
                s_dv.inc(inst)
            b.on("vector", dv_row)
            prev = s_dv.v
    wrow_done = prev
    if dbg is not None:
        def sy_dbg3(sy):
            sy.wait_ge(s_dv.h, wrow_done)
            inst = sy.dma_start(dbg['dbg_wr'], b.wrow[:, :])
            s_st.inc(inst, 16)
        b.on("sync", sy_dbg3)

    slot_seq = []
    for tc in range(NTC):
        for e in range(4):
            for m in range(NK):
                slot_seq.append(('w1', (tc, e, m)))
        for o in range(NK):
            for e in range(4):
                slot_seq.append(('w2', (tc, o, e)))
    grp_done_seq = []
    w1ld = {}
    w2ld = {}
    NSLOT = 4
    state = {'ld': 0}

    def emit_wload(upto):
        lo, hi = state['ld'], min(upto, len(slot_seq))
        if hi <= lo:
            return
        state['ld'] = hi

        def sc(s):
            for i in range(lo, hi):
                kind, key = slot_seq[i]
                if i >= NSLOT:
                    s.wait_ge(s_pe.h, grp_done_seq[i - NSLOT])
                if kind == 'w1':
                    tc, e, m = key
                    src = dr['w1c'][e, m]
                    wc.wait(s, dr.get('_pfx', '') + 'w1', e * 16 + m)
                else:
                    tc, o, e = key
                    src = dr['w2c'][o, e]
                    wc.wait(s, dr.get('_pfx', '') + 'w2', o * 4 + e)
                dst = b.wring[:, (i % NSLOT) * 2048:(i % NSLOT + 1) * 2048]
                inst = s.dma_start(dst.rearrange("p (k c) -> p k c", k=16), src)
                if kind == 'w1':
                    w1ld[key] = b.rw[i % NSLOT].inc(inst, 16)
                else:
                    w2ld[key] = b.rw[i % NSLOT].inc(inst, 16)
        b.on("scalar", sc)

    def h1s_ap(e, m):
        c = (e * 16 + m) * TL
        return b.aux[:, c:c + TL]

    si = 0
    gelu_done = {}
    epi_dv = {}
    epi_st = {}
    for tc in range(NTC):
        wbc_prev = {}
        for e in range(4):
            pstile = b.psC if e % 2 == 0 else b.psD

            def pe_wbc(t, tc=tc, e=e, pstile=pstile):
                if tc == 0 and e == 0:
                    t.wait_ge(s_dv.h, wrow_done)
                    t.wait_ge(s_dv.h, b.ln_psCD_free)
                if e >= 2:
                    t.wait_ge(s_dv.h, wbc_prev[e - 2])
                elif tc > 0:
                    t.wait_ge(s_dv.h, wbc_prev_tc[2 + e])
                inst = t.matmul(pstile, b.onesrow[0:1, :],
                                b.wrow[:, e * 2048 + tc * TL: e * 2048 + tc * TL + TL],
                                start=True, stop=True)
                s_pe.inc(inst)
            b.on("tensor", pe_wbc)
            mmv = s_pe.v

            def dv_wbc(v, tc=tc, e=e, pstile=pstile, mmv=mmv):
                v.wait_ge(s_pe.h, mmv)
                inst = v.tensor_copy(b.wbc[:, e * TL:(e + 1) * TL], pstile[:])
                s_dv.inc(inst)
            b.on("vector", dv_wbc)
            wbc_prev[e] = s_dv.v
        wbc_prev_tc = wbc_prev
        wbc_done = s_dv.v

        for e in range(4):
            for m in range(NK):
                emit_wload(si + NSLOT)
                pst = b.psA[:, (si % 2) * TL:(si % 2) * TL + TL]

                def pe_h1(t, tc=tc, e=e, m=m, si=si, pst=pst, wbc_done=wbc_done):
                    t.wait_ge(b.rw[si % NSLOT].h, w1ld[(tc, e, m)])
                    if e == 0 and m == 0:
                        t.wait_ge(s_dv.h, wbc_done)
                    if (si - 2) in gelu_done:
                        t.wait_ge(s_ac.h, gelu_done[si - 2])
                    for k in range(NK):
                        inst = t.matmul(
                            pst,
                            b.wring[:, (si % NSLOT) * 2048 + k * 128:(si % NSLOT) * 2048 + (k + 1) * 128],
                            b.xn[:, k * 2048 + tc * TL: k * 2048 + (tc + 1) * TL],
                            start=(k == 0), stop=(k == NK - 1))
                    s_pe.inc(inst)
                b.on("tensor", pe_h1)
                grp_done_seq.append(s_pe.v)
                mmv = s_pe.v

                def ac_gelu(s, tc=tc, e=e, m=m, mmv=mmv, pst=pst):
                    s.wait_ge(s_pe.h, mmv)
                    inst = s.activation(h1s_ap(e, m), pst, AF.Gelu,
                                        bias=b.b1v[:, e * 16 + m: e * 16 + m + 1])
                    s_ac.inc(inst)
                b.on("scalar", ac_gelu)
                gelu_done[si] = s_ac.v

                def dv_scale(v, e=e, m=m, gv=s_ac.v, tc=tc):
                    v.wait_ge(s_ac.h, gv)
                    inst = v.tensor_tensor(h1s_ap(e, m), h1s_ap(e, m),
                                           b.wbc[:, e * TL:(e + 1) * TL], ALU.mult)
                    s_dv.inc(inst)
                b.on("vector", dv_scale)
                si += 1
        h1_all = s_dv.v
        if dbg is not None and tc == 0:
            def sy_dbg4(sy, h1_all=h1_all):
                sy.wait_ge(s_dv.h, h1_all)
                inst = sy.dma_start(dbg['dbg_h1'], b.aux[:, :])
                s_st.inc(inst, 16)
            b.on("sync", sy_dbg4)

        for o in range(NK):
            pso = b.psB[:, (o % 2) * TL:(o % 2) * TL + TL]

            def sy_xo(sy, tc=tc, o=o):
                po = (tc * NK + o) - 4
                if po < 0:
                    sy.wait_ge(s_dv.h, b.ln_xtmp_free)
                else:
                    sy.wait_ge(s_dv.h, epi_dv[po])
                    sy.wait_ge(b.rs[o % 4].h, epi_st[po])
                inst = sy.dma_start(
                    b.xtmp[:, (o % 4) * TL:(o % 4 + 1) * TL],
                    dr['xin'][o * 128:(o + 1) * 128, tc * TL:(tc + 1) * TL])
                b.rx[o % 4].inc(inst, 16)
            b.on("sync", sy_xo)
            xo_ld = b.rx[o % 4].v

            for e in range(4):
                emit_wload(si + NSLOT)

                def pe_out(t, tc=tc, o=o, e=e, si=si, pso=pso, h1_all=h1_all):
                    t.wait_ge(b.rw[si % NSLOT].h, w2ld[(tc, o, e)])
                    if e == 0:
                        t.wait_ge(s_dv.h, h1_all)
                        if o >= 2:
                            t.wait_ge(s_dv.h, epi_dv[tc * NK + o - 2])
                    for m in range(NK):
                        inst = t.matmul(
                            pso,
                            b.wring[:, (si % NSLOT) * 2048 + m * 128:(si % NSLOT) * 2048 + (m + 1) * 128],
                            h1s_ap(e, m),
                            start=(e == 0 and m == 0), stop=False)
                    if e == 3:
                        for e2 in range(4):
                            b2src = b.pp if e2 < 2 else b.io
                            inst = t.matmul(
                                pso,
                                b2src[0:1, (e2 % 2) * 2048 + o * 128: (e2 % 2) * 2048 + (o + 1) * 128],
                                b.wrow[:, e2 * 2048 + tc * TL: e2 * 2048 + tc * TL + TL],
                                start=False, stop=(e2 == 3))
                    s_pe.inc(inst)
                b.on("tensor", pe_out)
                grp_done_seq.append(s_pe.v)
                si += 1
            mmo = s_pe.v

            def dv_epi(v, tc=tc, o=o, mmo=mmo, pso=pso, xo_ld=xo_ld):
                v.wait_ge(s_pe.h, mmo)
                v.wait_ge(b.rx[o % 4].h, xo_ld)
                inst = v.scalar_tensor_tensor(
                    b.xtmp[:, (o % 4) * TL:(o % 4 + 1) * TL], pso, 0.5,
                    b.xtmp[:, (o % 4) * TL:(o % 4 + 1) * TL], ALU.mult, ALU.add)
                s_dv.inc(inst)
            b.on("vector", dv_epi)
            epi_dv[tc * NK + o] = s_dv.v

            def sy_st(sy, tc=tc, o=o, ev=s_dv.v):
                sy.wait_ge(s_dv.h, ev)
                inst = sy.dma_start(
                    dr['xout'][o * 128:(o + 1) * 128, tc * TL:(tc + 1) * TL],
                    b.xtmp[:, (o % 4) * TL:(o % 4 + 1) * TL])
                b.rs[o % 4].inc(inst, 16)
            b.on("sync", sy_st)
            epi_st[tc * NK + o] = b.rs[o % 4].v

    b.final_store = s_st.v
    b.final_rs = [r.v for r in b.rs]



def decl_l0_ffn(nc, pfx=""):
    dr = {}
    dr['gwt'] = nc.dram_tensor(pfx + "gwt", [16, 128, 4], F32, kind="ExternalInput").ap()
    dr['w1f'] = nc.dram_tensor(pfx + "w1f", [4, 16, 128, 16, 128], F32, kind="ExternalInput").ap()
    dr['w2f'] = nc.dram_tensor(pfx + "w2f", [16, 4, 128, 16, 128], F32, kind="ExternalInput").ap()
    dr['b1r'] = nc.dram_tensor(pfx + "b1r", [128, 64], F32, kind="ExternalInput").ap()
    dr['b2row'] = nc.dram_tensor(pfx + "b2row", [1, 8192], F32, kind="ExternalInput").ap()
    dr['idf'] = nc.dram_tensor(pfx + "idf", [128, 128], F32, kind="ExternalInput").ap()
    dr['w1c'] = nc.dram_tensor(pfx + "w1c", [4, 16, 128, 16, 128], BF16).ap()
    dr['w2c'] = nc.dram_tensor(pfx + "w2c", [16, 4, 128, 16, 128], BF16).ap()
    dr['_pfx'] = pfx
    return dr


def fill_l0_ffn(dr, pfx=""):
    items = []
    for e in range(4):
        for m in range(NK):
            items.append((pfx + 'w1', dr['w1f'][e, m], dr['w1c'][e, m]))
    for o in range(NK):
        for e in range(4):
            items.append((pfx + 'w2', dr['w2f'][o, e], dr['w2c'][o, e]))
    return items


def emit_sub_l0_ffn(b, dr, wc, xin, xout):
    dr = dict(dr)
    dr['xin'] = xin
    dr['xout'] = xout
    emit_ln(b, xin, xbase_dram=None)
    emit_l0_ffn(b, dr, wc)
    return


def build_l0_ffn_nc(debug=False):
    b = Bld("l0ffn")
    nc = b.nc
    dr = {}
    if debug:
        dr['dbg_xn'] = nc.dram_tensor("dbg_xn", [128, 32768], BF16, kind="ExternalOutput").ap()
        dr['dbg_wf'] = nc.dram_tensor("dbg_wf", [128, 512], F32, kind="ExternalOutput").ap()
        dr['dbg_wr'] = nc.dram_tensor("dbg_wr", [1, 8192], BF16, kind="ExternalOutput").ap()
        dr['dbg_h1'] = nc.dram_tensor("dbg_h1", [128, 32768], BF16, kind="ExternalOutput").ap()
    dr['xin'] = nc.dram_tensor("xin", [D, 2048], F32, kind="ExternalInput").ap()
    dr['xout'] = nc.dram_tensor("xout", [D, 2048], F32, kind="ExternalOutput").ap()
    dr.update(decl_l0_ffn(nc))

    b.start()
    alloc_common(b)

    wc = WCache(b)
    wc.fill(fill_l0_ffn(dr))

    emit_ln(b, dr['xin'], xbase_dram=None)
    if debug:
        s_dv = b.sem("s_dv"); s_st = b.sem("s_st"); s_x = b.sem("s_x")
        nd = b.ln_norm_done
        def sy_dbg1(sy):
            sy.wait_ge(s_dv.h, nd)
            inst = sy.dma_start(dr['dbg_xn'], b.xn[:, :])
            s_st.inc(inst, 16)
        b.on("sync", sy_dbg1)
    emit_l0_ffn(b, dr, wc, dbg=dr if debug else None)

    s_st = b.sem("s_st")
    fin = b.final_store
    frs = b.final_rs

    def sy_fin(sy):
        sy.wait_ge(s_st.h, fin)
        for p in range(4):
            sy.wait_ge(b.rs[p].h, frs[p])
    b.on("sync", sy_fin)

    return b.finish()


def prep_l0_ffn_inputs(inputs, xs):
    w1 = np.asarray(inputs['L0_ffn_w1'])
    w2 = np.asarray(inputs['L0_ffn_w2'])
    gw = np.asarray(inputs['L0_ffn_gate_w'])
    b1 = np.asarray(inputs['L0_ffn_b1'])
    b2 = np.asarray(inputs['L0_ffn_b2'])
    w1f = np.stack([_tile_w(w1[e]) for e in range(4)])
    w2t = np.stack([_tile_w(w2[e]) for e in range(4)])
    w2f = np.ascontiguousarray(w2t.transpose(1, 0, 2, 3, 4))
    gwt = np.ascontiguousarray(gw.T.reshape(16, 128, 4)).astype(np.float32)
    b1r = np.ascontiguousarray(
        b1.reshape(4, 16, 128).transpose(2, 0, 1).reshape(128, 64)).astype(np.float32)
    b2row = np.ascontiguousarray(b2.reshape(1, 8192)).astype(np.float32)
    idf = np.eye(128, dtype=np.float32)
    return [{
        "xin": xs[c], "gwt": gwt, "w1f": w1f, "w2f": w2f,
        "b1r": b1r, "b2row": b2row, "idf": idf,
    } for c in range(8)]



def _shard_x(x0):
    xs = []
    x0 = np.asarray(x0)
    for c in range(8):
        bi, h = divmod(c, 2)
        idx = _core_token_idx(h)
        xs.append(np.ascontiguousarray(x0[bi][idx, :].T.astype(np.float32)))
    return xs


def _unshard_x(xs):
    out = np.zeros((4, 4096, 2048), dtype=np.float32)
    for c in range(8):
        bi, h = divmod(c, 2)
        idx = _core_token_idx(h)
        out[bi][idx, :] = xs[c].T
    return out


def kernel_unfused(**inputs):
    xs = _shard_x(inputs['L0_attn_query'])
    steps = [
        (build_l0_ffn_nc, prep_l0_ffn_inputs),
        (build_l0_attn_nc, prep_l0_attn_inputs),
        (build_l0_ffn_nc, prep_l0_ffn_inputs),
        (build_l1_ffn_nc, prep_l1_ffn_inputs),
        (build_l1_attn_nc, prep_l1_attn_inputs),
        (build_l1_ffn_nc, prep_l1_ffn_inputs),
    ]
    for bf, pf in steps:
        nc = bf()
        maps = pf(inputs, xs)
        res = run_bass_kernel_spmd(nc, maps, core_ids=list(range(8)))
        xs = [np.asarray(res.results[c]["xout"]) for c in range(8)]
    return _unshard_x(xs)



class WRing:
    def __init__(self, b: Bld, wc: WCache):
        self.b = b
        self.wc = wc
        self.q = []
        self.ld = {}
        self.grp_done = []
        self.emitted = 0
        self.pe_base = b.sem("s_pe").v

    def push(self, group, src_ap, ford=None):
        self.q.append((group, src_ap, ford))
        return len(self.q) - 1

    def ensure(self, upto):
        lo, hi = self.emitted, min(upto, len(self.q))
        if hi <= lo:
            return
        self.emitted = hi
        b = self.b

        def mk(queue_par):
            def sec(s):
                for i in range(lo, hi):
                    if i % 2 != queue_par:
                        continue
                    grp, src, ford = self.q[i]
                    if i >= 4:
                        s.wait_ge(b.sem("s_pe").h, self.grp_done[i - 4])
                    else:
                        s.wait_ge(b.sem("s_pe").h, self.pe_base)
                    self.wc.wait(s, grp, ford)
                    dst = b.wring[:, (i % 4) * 2048:(i % 4 + 1) * 2048]
                    inst = s.dma_start(dst.rearrange("p (k c) -> p k c", k=16), src)
                    self.ld[i] = b.rw[i % 4].inc(inst, 16)
            return sec
        b.on("scalar", mk(0))
        b.on("gpsimd", mk(1))

    def wait(self, t, i):
        t.wait_ge(self.b.rw[i % 4].h, self.ld[i])

    def slot(self, i, k):
        return self.b.wring[:, (i % 4) * 2048 + k * 128:(i % 4) * 2048 + (k + 1) * 128]

    def consumed(self, v):
        self.grp_done.append(v)



def emit_proj_fm(b: Bld, wr: WRing, wgroup, wsrcs, rhs_fn, dst_fn,
                 bias_col=None, norm_col=None, pe_extra_wait=None):
    s_pe, s_dv, s_ac = b.sem("s_pe"), b.sem("s_dv"), b.sem("s_ac")
    ri_base = [r.v for r in b.ri]
    idxs = [wr.push(wgroup, wsrcs[m], ford=m) for m in range(16)]
    evac_done = {}
    sq_used = {}
    norm_red = {}
    io_store = {}
    sq_norm_mm = {}

    for m in range(16):
        wr.ensure(idxs[m] + 4)
        for tc in range(NTC):
            g = m * 4 + tc
            half = b.psA[:, (g % 2) * TL:(g % 2) * TL + TL]

            def pe_mm(t, m=m, tc=tc, g=g, half=half, idx=idxs[m]):
                if g == 0 and pe_extra_wait is not None:
                    for sem, val in pe_extra_wait:
                        t.wait_ge(sem.h, val)
                wr.wait(t, idx)
                if g >= 2:
                    t.wait_ge(s_dv.h, evac_done[g - 2])
                if norm_col is not None and tc == 0 and m >= 1:
                    t.wait_ge(s_dv.h, norm_red[m - 1])
                for k in range(NK):
                    inst = t.matmul(half, wr.slot(idxs[m], k), rhs_fn(k, tc),
                                    start=(k == 0), stop=(k == NK - 1))
                s_pe.inc(inst)
            b.on("tensor", pe_mm)
            mmv = s_pe.v
            if tc == 3:
                wr.consumed(mmv)

            kind, dap = dst_fn(m, tc)
            if kind == 's':
                def dv_ev(v, m=m, tc=tc, half=half, mmv=mmv, dap=dap):
                    v.wait_ge(s_pe.h, mmv)
                    if bias_col is not None:
                        inst = v.tensor_scalar(dap, half, bias_col(m), None, ALU.add)
                    else:
                        inst = v.tensor_copy(dap, half)
                    s_dv.inc(inst)
                b.on("vector", dv_ev)
                evac_done[g] = s_dv.v
                ev_ap = dap
            else:
                slot = b.io[:, (g % 8) * TL:(g % 8 + 1) * TL]

                def dv_ev(v, m=m, tc=tc, half=half, mmv=mmv, slot=slot, g=g):
                    v.wait_ge(s_pe.h, mmv)
                    if g >= 8:
                        v.wait_ge(b.ri[g % 8].h, io_store[g - 8])
                        if norm_col is not None:
                            v.wait_ge(s_ac.h, sq_used[g - 8])
                    else:
                        v.wait_ge(b.ri[g % 8].h, ri_base[g % 8])
                    if bias_col is not None:
                        inst = v.tensor_scalar(slot, half, bias_col(m), None, ALU.add)
                    else:
                        inst = v.tensor_copy(slot, half)
                    s_dv.inc(inst)
                b.on("vector", dv_ev)
                evac_done[g] = s_dv.v

                def sy_st(sy, slot=slot, dap=dap, g=g, ev=s_dv.v):
                    sy.wait_ge(s_dv.h, ev)
                    inst = sy.dma_start(dap, slot)
                    io_store[g] = b.ri[g % 8].inc(inst, 16)
                b.on("sync", sy_st)
                ev_ap = slot

            if norm_col is not None:
                sqslot = b.pp[:, 2048 + (g % 2) * TL: 2048 + (g % 2 + 1) * TL]

                def ac_sq(s, g=g, ev=evac_done[g], ev_ap=ev_ap, sqslot=sqslot):
                    s.wait_ge(s_dv.h, ev)
                    if g >= 2:
                        s.wait_ge(s_pe.h, sq_norm_mm[g - 2])
                    inst = s.activation(sqslot, ev_ap, AF.Square)
                    s_ac.inc(inst)
                b.on("scalar", ac_sq)
                sq_used[g] = s_ac.v
                sqv = s_ac.v

                def pe_nm(t, tc=tc, sqv=sqv, sqslot=sqslot):
                    t.wait_ge(s_ac.h, sqv)
                    inst = t.matmul(b.psBig[:, tc * TL:(tc + 1) * TL], b.ones[:],
                                    sqslot, start=True, stop=True)
                    s_pe.inc(inst)
                b.on("tensor", pe_nm)
                sq_norm_mm[g] = s_pe.v

                if tc == 3:
                    def dv_red(v, m=m, nv=s_pe.v):
                        v.wait_ge(s_pe.h, nv)
                        inst = v.reduce_max(norm_col(m), b.psBig[:, 0:2048], axis=AX.X)
                        b.fence(v, inst)
                        inst = v.tensor_scalar_mul(norm_col(m), norm_col(m), 1.0)
                        s_dv.inc(inst)
                    b.on("vector", dv_red)
                    norm_red[m] = s_dv.v
    b.proj_evac_done = evac_done[63]
    b.proj_io_store = dict(io_store)
    return evac_done



def emit_attn(b: Bld, qscr, kall, vall, dst_buf, masks_ready, gather_done, psum_free):
    s_pe, s_dv, s_ac, s_cc = (b.sem(n) for n in ("s_pe", "s_dv", "s_ac", "s_cc"))

    def dv_nc1(v):
        inst = v.tensor_tensor(b.vec[:, 32:48], b.vec[:, 0:16], b.vec[:, 16:32], ALU.mult)
        s_dv.inc(inst)
    b.on("vector", dv_nc1)
    nc1 = s_dv.v

    def ac_nc(s):
        s.wait_ge(s_dv.h, nc1)
        inst = s.activation(b.vec[:, 32:48], b.vec[:, 32:48], AF.Ln)
        b.fence(s, inst)
        inst = s.activation(b.vec[:, 32:48], b.vec[:, 32:48], AF.Exp, scale=0.5)
        s_ac.inc(inst)
    b.on("scalar", ac_nc)
    nc2 = s_ac.v

    def dv_nc2(v):
        v.wait_ge(s_ac.h, nc2)
        inst = v.tensor_scalar(b.vec[:, 32:48], b.vec[:, 32:48], -RS128, -10.0,
                               ALU.mult, ALU.add)
        s_dv.inc(inst)
    b.on("vector", dv_nc2)
    negc_done = s_dv.v

    kofs, vofs, qofs = 0, 8192, 16384
    head_done = {}
    kld = {}
    vld = {}
    qld = {}
    evac_rec = {}

    for h in range(NH):
        ks = h % 2

        def sy_k(sy, h=h, ks=ks):
            if h >= 2:
                sy.wait_ge(s_pe.h, head_done[h - 2])
            if h == 0:
                sy.wait_ge(s_cc.h, gather_done)
                sy.wait_ge(s_pe.h, b.ln_aux_free)
            for kt in range(32):
                r, j = kt % 2, kt // 2
                inst = sy.dma_start(
                    b.aux[:, kofs + ks * 4096 + kt * 128: kofs + ks * 4096 + (kt + 1) * 128],
                    kall[r * 2048 + h * 128:(r * 2048 + (h + 1) * 128), j * 128:(j + 1) * 128])
                b.rk[ks].inc(inst, 16)
            kld[h] = b.rk[ks].v
        b.on("sync", sy_k)

        def sc_v(s, h=h, ks=ks):
            if h >= 2:
                s.wait_ge(s_pe.h, head_done[h - 2])
            if h == 0:
                s.wait_ge(s_cc.h, gather_done)
                s.wait_ge(s_pe.h, b.ln_aux_free)
            for kt in range(32):
                r, j = kt % 2, kt // 2
                inst = s.dma_start(
                    b.aux[:, vofs + ks * 4096 + kt * 128: vofs + ks * 4096 + (kt + 1) * 128],
                    vall[r * 2048 + h * 128:(r * 2048 + (h + 1) * 128), j * 128:(j + 1) * 128],
                    transpose=True)
                b.rv[ks].inc(inst, 16)
            vld[h] = b.rv[ks].v
        b.on("scalar", sc_v)

        for jb in range(NTC):
            gq = (h * 4 + jb) % 2

            def sy_q(sy, h=h, jb=jb, gq=gq):
                gi = h * 4 + jb
                if gi >= 2:
                    sy.wait_ge(s_dv.h, evac_rec[gi - 2][0])
                inst = sy.dma_start(
                    b.aux[:, qofs + gq * TL: qofs + (gq + 1) * TL],
                    qscr[h * 128:(h + 1) * 128, jb * TL:(jb + 1) * TL])
                qld[(h, jb)] = b.rq[gq].inc(inst, 16)
            b.on("sync", sy_q)

            P = 4 * jb + 4
            exp_done = {}
            mask_done = {}
            spair_done = {}
            sgpv_done = {}

            for i in range(P + 1):
                def pe_kloop(t, h=h, jb=jb, i=i, P=P, gq=gq, ks=ks):
                    if i >= 1:
                        pi = i - 1
                        t.wait_ge(s_ac.h, exp_done[pi])
                        if pi in mask_done:
                            t.wait_ge(s_dv.h, mask_done[pi])
                        if i == 1:
                            gi = h * 4 + jb
                            if gi >= 1:
                                t.wait_ge(s_dv.h, evac_rec[gi - 1][1])
                        for z in range(2):
                            kt = 2 * pi + z
                            pslot = b.pp[:, (pi % 2) * 1024 + z * TL:(pi % 2) * 1024 + (z + 1) * TL]
                            vt = b.aux[:, vofs + ks * 4096 + kt * 128: vofs + ks * 4096 + (kt + 1) * 128]
                            t.matmul(b.psC, vt, pslot,
                                     start=(pi == 0 and z == 0), stop=(pi == P - 1 and z == 1))
                            inst = t.matmul(b.psD, b.ones[:], pslot,
                                            start=(pi == 0 and z == 0), stop=(pi == P - 1 and z == 1))
                        s_pe.inc(inst)
                        sgpv_done[pi] = s_pe.v
                    if i < P:
                        if i == 0:
                            t.wait_ge(b.rq[gq].h, qld[(h, jb)])
                            if jb == 0:
                                if h == 0:
                                    t.wait_ge(s_dv.h, psum_free)
                                t.wait_ge(b.rk[ks].h, kld[h])
                                t.wait_ge(b.rv[ks].h, vld[h])
                        if i >= 2:
                            t.wait_ge(s_ac.h, exp_done[i - 2])
                        ps = b.psA if i % 2 == 0 else b.psB
                        for z in range(2):
                            kt = 2 * i + z
                            ktile = b.aux[:, kofs + ks * 4096 + kt * 128: kofs + ks * 4096 + (kt + 1) * 128]
                            inst = t.matmul(ps[:, z * TL:(z + 1) * TL], ktile,
                                            b.aux[:, qofs + gq * TL: qofs + (gq + 1) * TL],
                                            start=True, stop=True)
                        s_pe.inc(inst)
                        spair_done[i] = s_pe.v
                b.on("tensor", pe_kloop)

                if i < P:
                    def ac_exp(s, h=h, jb=jb, i=i, P=P):
                        s.wait_ge(s_pe.h, spair_done[i])
                        if i == 0 and h == 0 and jb == 0:
                            s.wait_ge(s_dv.h, negc_done)
                        if i >= 2:
                            t2 = sgpv_done.get(i - 2)
                            if t2 is not None:
                                s.wait_ge(s_pe.h, t2)
                        ps = b.psA if i % 2 == 0 else b.psB
                        inst = s.activation(b.pp[:, (i % 2) * 1024:(i % 2 + 1) * 1024],
                                            ps[:], AF.Exp, scale=RS128,
                                            bias=b.vec[:, 32 + h:33 + h])
                        s_ac.inc(inst)
                    b.on("scalar", ac_exp)
                    exp_done[i] = s_ac.v

                    smin = 8 * jb
                    kts = [2 * i, 2 * i + 1]
                    if kts[1] >= smin:
                        def dv_mask(v, h=h, jb=jb, i=i, kts=kts, smin=smin):
                            v.wait_ge(s_ac.h, exp_done[i])
                            if h == 0 and jb == 0:
                                v.wait_ge(b.sem("s_g").h, masks_ready)
                            inst = None
                            for z, kt in enumerate(kts):
                                if kt < smin:
                                    continue
                                sidx = kt - smin
                                pslot = b.pp[:, (i % 2) * 1024 + z * TL:(i % 2) * 1024 + (z + 1) * TL]
                                inst = v.tensor_tensor(pslot, pslot,
                                                       b.ovl[:, sidx * TL:(sidx + 1) * TL], ALU.mult)
                            s_dv.inc(inst)
                        b.on("vector", dv_mask)
                        mask_done[i] = s_dv.v

            def dv_evac(v, h=h, jb=jb, P=P):
                v.wait_ge(s_pe.h, sgpv_done[P - 1])
                inst = v.reciprocal(b.stat[:, 0:TL], b.psD)
                b.fence(v, inst)
                inst = v.tensor_tensor(
                    dst_buf[:, h * 2048 + jb * TL: h * 2048 + (jb + 1) * TL],
                    b.psC, b.stat[:, 0:TL], ALU.mult)
                s_dv.inc(inst)
            b.on("vector", dv_evac)
            evac_rec[h * 4 + jb] = (s_dv.v, s_dv.v)
        head_done[h] = s_pe.v

    b.attn_done = s_dv.v



def emit_outproj(b: Bld, wr: WRing, wgroup, wsrcs, rhs_buf, dr_xin, dr_xout,
                 bias_col=None, dscale=1.0, attn_ready=0, xbase_dram=None,
                 xtmp_free=None):
    s_pe, s_dv = b.sem("s_pe"), b.sem("s_dv")
    idxs = [wr.push(wgroup, wsrcs[o], ford=o) for o in range(16)]
    epi_dv = {}
    epi_st = {}
    xsrc = xbase_dram if xbase_dram is not None else dr_xin

    for o in range(16):
        wr.ensure(idxs[o] + 4)
        for tc in range(NTC):
            g = o * 4 + tc

            def sy_xo(sy, o=o, tc=tc, g=g):
                if g >= 4:
                    sy.wait_ge(s_dv.h, epi_dv[g - 4])
                    sy.wait_ge(b.rs[g % 4].h, epi_st[g - 4])
                else:
                    sy.wait_ge(s_dv.h, xtmp_free if xtmp_free is not None
                               else b.ln_xtmp_free)
                    if getattr(b, 'ln_store_ring', None) is not None:
                        ctr, val = b.ln_store_ring[g % 4]
                        sy.wait_ge(ctr.h, val)
                inst = sy.dma_start(
                    b.xtmp[:, (g % 4) * TL:(g % 4 + 1) * TL],
                    xsrc[o * 128:(o + 1) * 128, tc * TL:(tc + 1) * TL])
                b.rx[g % 4].inc(inst, 16)
            b.on("sync", sy_xo)
            xo_ld = b.rx[g % 4].v

            half = b.psB[:, (g % 2) * TL:(g % 2) * TL + TL]

            def pe_o(t, o=o, tc=tc, g=g, half=half, idx=idxs[o]):
                wr.wait(t, idx)
                if g == 0:
                    t.wait_ge(s_dv.h, attn_ready)
                if g >= 2:
                    t.wait_ge(s_dv.h, epi_dv[g - 2])
                for m in range(NK):
                    inst = t.matmul(half, wr.slot(idxs[o], m),
                                    rhs_buf[:, m * 2048 + tc * TL: m * 2048 + (tc + 1) * TL],
                                    start=(m == 0), stop=(m == NK - 1))
                s_pe.inc(inst)
            b.on("tensor", pe_o)
            mmv = s_pe.v
            if tc == 3:
                wr.consumed(mmv)

            def dv_epi(v, o=o, tc=tc, g=g, half=half, mmv=mmv, xo_ld=xo_ld):
                v.wait_ge(s_pe.h, mmv)
                v.wait_ge(b.rx[g % 4].h, xo_ld)
                xs = b.xtmp[:, (g % 4) * TL:(g % 4 + 1) * TL]
                if bias_col is not None:
                    inst = v.scalar_tensor_tensor(xs, half, bias_col(o), xs,
                                                  ALU.add, ALU.add)
                else:
                    inst = v.scalar_tensor_tensor(xs, half, dscale, xs,
                                                  ALU.mult, ALU.add)
                s_dv.inc(inst)
            b.on("vector", dv_epi)
            epi_dv[g] = s_dv.v

            def sy_st(sy, o=o, tc=tc, g=g, ev=s_dv.v):
                sy.wait_ge(s_dv.h, ev)
                inst = sy.dma_start(
                    dr_xout[o * 128:(o + 1) * 128, tc * TL:(tc + 1) * TL],
                    b.xtmp[:, (g % 4) * TL:(g % 4 + 1) * TL])
                b.rs[g % 4].inc(inst, 16)
            b.on("sync", sy_st)
            epi_st[g] = b.rs[g % 4].v

    b.final_rs = [r.v for r in b.rs]


def emit_boundary(b: Bld):
    s_pe, s_dv, s_ac = b.sem("s_pe"), b.sem("s_dv"), b.sem("s_ac")
    pe0, dv0, ac0 = s_pe.v, s_dv.v, s_ac.v

    def tn(t):
        t.wait_ge(s_dv.h, dv0)
        t.wait_ge(s_ac.h, ac0)
    b.on("tensor", tn)

    def vc(v):
        v.wait_ge(s_pe.h, pe0)
        v.wait_ge(s_ac.h, ac0)
    b.on("vector", vc)

    def sc(s):
        s.wait_ge(s_pe.h, pe0)
        s.wait_ge(s_dv.h, dv0)
    b.on("scalar", sc)

    def gp(g):
        g.wait_ge(s_pe.h, pe0)
        g.wait_ge(s_dv.h, dv0)
    b.on("gpsimd", gp)

    b._ln_prev_norm = dv0
    b._ln_prev_store = {p: b.rs[p].v for p in range(4)}


def emit_finwait(b: Bld):
    frs = b.final_rs

    def sy_fin(sy):
        for p in range(4):
            sy.wait_ge(b.rs[p].h, frs[p])
    b.on("sync", sy_fin)



def decl_l0_attn(nc, pfx=""):
    dr = {}
    for w in ("wq", "wk", "wo"):
        dr[w + 'f'] = nc.dram_tensor(pfx + w + "f", [16, 128, 16, 128], F32, kind="ExternalInput").ap()
        dr[w + 'c'] = nc.dram_tensor(pfx + w + "c", [16, 128, 16, 128], BF16).ap()
    dr['wvtf'] = nc.dram_tensor(pfx + "wvtf", [4, 128, 16, 512], F32, kind="ExternalInput").ap()
    dr['wvtc'] = nc.dram_tensor(pfx + "wvtc", [4, 128, 16, 512], BF16).ap()
    dr['bvrow'] = nc.dram_tensor(pfx + "bvrow", [1, 2048], F32, kind="ExternalInput").ap()
    dr['bqkvo'] = nc.dram_tensor(pfx + "bqkvo", [128, 64], F32, kind="ExternalInput").ap()
    dr['masks'] = nc.dram_tensor(pfx + "masks", [8, 128, 512], F32, kind="ExternalInput").ap()
    dr['qscr'] = nc.dram_tensor(pfx + "qscr", [2048, 2048], BF16).ap()
    dr['kscr'] = nc.dram_tensor(pfx + "kscr", [2048, 2048], BF16).ap()
    dr['vscr'] = nc.dram_tensor(pfx + "vscr", [2048, 2048], BF16).ap()
    dr['kall'] = nc.dram_tensor(pfx + "kall", [4096, 2048], BF16).ap()
    dr['vall'] = nc.dram_tensor(pfx + "vall", [4096, 2048], BF16).ap()
    dr['_pfx'] = pfx
    return dr


def fill_l0_attn(dr, pfx=""):
    items = []
    for w in ("wq", "wk", "wo"):
        for m in range(16):
            items.append((pfx + w, dr[w + 'f'][m], dr[w + 'c'][m]))
    for n in range(4):
        items.append((pfx + 'wvt', dr['wvtf'][n], dr['wvtc'][n]))
    return items


def emit_sub_l0_attn(b, dr, wc, xin, xout):
    pfx = dr.get('_pfx', '')
    nc = b.nc
    s_g = b.sem("s_g")
    s_cc = b.sem("s_cc")
    s_dv = b.sem("s_dv")
    qscr, kscr, vscr, kall, vall = (dr[k] for k in ("qscr", "kscr", "vscr", "kall", "vall"))

    def g_small(g):
        inst = g.dma_start(b.b1v[:, 0:64], dr['bqkvo'])
        s_g.inc(inst, 16)
        for s8 in range(8):
            inst = g.dma_start(b.ovl[:, s8 * 512:(s8 + 1) * 512], dr['masks'][s8])
            s_g.inc(inst, 16)
    b.on("gpsimd", g_small)
    masks_ready = s_g.v
    bias_ready = s_g.v

    emit_ln(b, xin, xbase_dram=None)

    wr = WRing(b, wc)

    def rhs_xn(k, tc):
        return b.xn[:, k * 2048 + tc * TL: k * 2048 + (tc + 1) * TL]

    emit_proj_fm(b, wr, pfx + 'wq', [dr['wqc'][m] for m in range(16)], rhs_xn,
                 lambda m, tc: ('d', qscr[m * 128:(m + 1) * 128, tc * TL:(tc + 1) * TL]),
                 bias_col=lambda m: b.b1v[:, m:m + 1],
                 norm_col=lambda m: b.vec[:, m:m + 1],
                 pe_extra_wait=[(b.sem("s_dv"), b.ln_norm_done), (s_g, bias_ready)])
    emit_proj_fm(b, wr, pfx + 'wk', [dr['wkc'][m] for m in range(16)], rhs_xn,
                 lambda m, tc: ('d', kscr[m * 128:(m + 1) * 128, tc * TL:(tc + 1) * TL]),
                 bias_col=lambda m: b.b1v[:, 16 + m:17 + m],
                 norm_col=lambda m: b.vec[:, 16 + m:17 + m])
    def g_bv(g):
        inst = g.dma_start(b.wbc[0:1, 0:2048], dr['bvrow'])
        b.sem("s_g").inc(inst, 16)
    b.on("gpsimd", g_bv)
    bv_done = b.sem("s_g").v

    def pe_bvw(t):
        t.wait_ge(b.sem("s_g").h, bv_done)
    b.on("tensor", pe_bvw)
    emit_proj_tm(b, wc, pfx + 'wvt', dr['wvtc'], vscr, bvrow_sb=b.wbc)
    proj_psum_free = b.proj_evac_done

    ri_totals = [r.v for r in b.ri]

    def g_gather(g):
        for p in range(8):
            g.wait_ge(b.ri[p].h, ri_totals[p])
        g.collective_compute("AllGather", ALU.bypass,
                             replica_groups=[[0, 1], [2, 3], [4, 5], [6, 7]],
                             ins=[kscr.opt()], outs=[kall.opt()]).then_inc(s_cc.h)
        s_cc.v += 1
        g.collective_compute("AllGather", ALU.bypass,
                             replica_groups=[[0, 1], [2, 3], [4, 5], [6, 7]],
                             ins=[vscr.opt()], outs=[vall.opt()]).then_inc(s_cc.h)
        s_cc.v += 1
    b.on("gpsimd", g_gather)
    gather_done = s_cc.v

    emit_attn_v2(b, qscr, kall, vall, b.xn, masks_ready, gather_done, proj_psum_free)

    emit_outproj(b, wr, pfx + 'wo', [dr['woc'][o] for o in range(16)], b.xn,
                 xin, xout,
                 bias_col=lambda o: b.b1v[:, 48 + o:49 + o],
                 attn_ready=b.attn_done)


def build_l0_attn_nc():
    b = Bld("l0attn")
    nc = b.nc
    dr = {}
    dr['xin'] = nc.dram_tensor("xin", [D, 2048], F32, kind="ExternalInput").ap()
    dr['xout'] = nc.dram_tensor("xout", [D, 2048], F32, kind="ExternalOutput").ap()
    dr.update(decl_l0_attn(nc))

    b.start()
    alloc_common(b)
    wc = WCache(b)
    wc.fill(fill_l0_attn(dr))
    emit_sub_l0_attn(b, dr, wc, dr['xin'], dr['xout'])
    emit_finwait(b)
    return b.finish()


def _attn_masks(hpar):
    row = np.arange(128)[:, None]
    col = np.arange(512)[None, :]
    qq = col // 128
    rq = col % 128
    out = np.zeros((8, 128, 512), dtype=np.float32)
    for s in range(8):
        out[s] = (((2 * qq + hpar - s) * 128 + rq - row) >= 0).astype(np.float32)
    return out


def prep_l0_attn_inputs(inputs, xs):
    wq = _tile_w(np.asarray(inputs['L0_attn_w_q']))
    wk = _tile_w(np.asarray(inputs['L0_attn_w_k']))
    wvt = _tile_rhs_v(np.asarray(inputs['L0_attn_w_v']))
    wo = _tile_w(np.asarray(inputs['L0_attn_w_o']))
    bq = _perpart(inputs['L0_attn_b_q'])
    bk = _perpart(inputs['L0_attn_b_k'])
    bv = _perpart(inputs['L0_attn_b_v'])
    bo = _perpart(inputs['L0_attn_b_o'])
    bqkvo = np.concatenate([bq, bk, bv, bo], axis=1).astype(np.float32)
    bvrow = np.ascontiguousarray(np.asarray(inputs['L0_attn_b_v']).reshape(1, 2048)).astype(np.float32)
    maps = []
    for c in range(8):
        hpar = c % 2
        maps.append({
            "xin": xs[c], "wqf": wq, "wkf": wk, "wvtf": wvt, "wof": wo,
            "bqkvo": bqkvo, "bvrow": bvrow, "masks": _attn_masks(hpar),
        })
    return maps



def emit_rms(b: Bld, wcol_fn, eps_col, first_wait=None):
    s_pe, s_dv, s_ac = b.sem("s_pe"), b.sem("s_dv"), b.sem("s_ac")
    ms = b.stat[:, TL:2 * TL]
    sq_off = 0
    prev_norm = None
    prev_stat = None
    for tc in range(NTC):
        def ac_sq(s, tc=tc, prev_norm=prev_norm):
            if tc == 0 and first_wait is not None:
                for sem, val in first_wait:
                    s.wait_ge(sem.h, val)
            if prev_norm is not None:
                s.wait_ge(s_dv.h, prev_norm)
            for k in range(NK):
                inst = s.activation(b.aux[:, sq_off + k * TL: sq_off + (k + 1) * TL],
                                    b.xn[:, k * 2048 + tc * TL: k * 2048 + (tc + 1) * TL],
                                    AF.Square)
            s_ac.inc(inst)
        b.on("scalar", ac_sq)
        sqd = s_ac.v

        def pe_ms(t, tc=tc, sqd=sqd, prev_stat=prev_stat):
            t.wait_ge(s_ac.h, sqd)
            if prev_stat is not None:
                t.wait_ge(s_ac.h, prev_stat)
            for k in range(NK):
                inst = t.matmul(b.psC, b.ones[:],
                                b.aux[:, sq_off + k * TL: sq_off + (k + 1) * TL],
                                start=(k == 0), stop=(k == NK - 1))
            s_pe.inc(inst)
        b.on("tensor", pe_ms)
        msd = s_pe.v

        def ac_rstd(s, tc=tc, msd=msd):
            s.wait_ge(s_pe.h, msd)
            inst = s.activation(ms, b.psC, AF.Ln, scale=1.0 / D, bias=eps_col)
            b.fence(s, inst)
            inst = s.activation(ms, ms, AF.Exp, scale=-0.5)
            s_ac.inc(inst)
        b.on("scalar", ac_rstd)
        prev_stat = s_ac.v
        rstdd = s_ac.v

        def dv_nrm(v, tc=tc, rstdd=rstdd):
            v.wait_ge(s_ac.h, rstdd)
            for k in range(NK):
                xnk = b.xn[:, k * 2048 + tc * TL: k * 2048 + (tc + 1) * TL]
                inst = v.scalar_tensor_tensor(xnk, xnk, wcol_fn(k), ms,
                                              ALU.mult, ALU.mult)
            s_dv.inc(inst)
        b.on("vector", dv_nrm)
        prev_norm = s_dv.v
    b.rms_done = prev_norm


def emit_xn_reload(b: Bld, xn_dram, first_wait=None):
    s_xr = b.sem("s_xr")

    def sy(sy_):
        if first_wait is not None:
            for sem, val in first_wait:
                sy_.wait_ge(sem.h, val)
        for k in range(NK):
            inst = sy_.dma_start(b.xn[:, k * 2048:(k + 1) * 2048],
                                 xn_dram[k * 128:(k + 1) * 128, :])
            s_xr.inc(inst, 16)
    b.on("sync", sy)
    b.xn_reload_done = s_xr.v
    return s_xr


def emit_Amm(b: Bld, src_buf, evac_cb, first_wait=None):
    s_pe = b.sem("s_pe")
    done = []
    prev_evac = None
    for tc in range(NTC):
        def pe_a(t, tc=tc, prev_evac=prev_evac):
            if tc == 0 and first_wait is not None:
                for sem, val in first_wait:
                    t.wait_ge(sem.h, val)
            if prev_evac is not None:
                t.wait_ge(prev_evac[0].h, prev_evac[1])
            for k in range(NK):
                inst = t.matmul(b.psE[0:16, :], b.wbc[:, k * 16:(k + 1) * 16],
                                src_buf[:, k * 2048 + tc * TL: k * 2048 + (tc + 1) * TL],
                                start=(k == 0), stop=(k == NK - 1))
            s_pe.inc(inst)
        b.on("tensor", pe_a)
        prev_evac = evac_cb(tc, s_pe.v)
        done.append(s_pe.v)
    return done



def emit_l1_mlp(b: Bld, wr: WRing, dr, xbase_dram, xout_dram, modhA, h_ready):
    pfx = dr.get('_pfx', '')
    s_pe, s_dv, s_ac = b.sem("s_pe"), b.sem("s_dv"), b.sem("s_ac")
    epi_dv = {}
    epi_st = {}
    for tc in range(NTC):
        silu_done = {}
        gu_done = {}
        wi_idx = {}
        wu_idx = {}
        for fb in range(64):
            wi_idx[fb] = wr.push(pfx + 'wi', dr['wic'][fb], ford=fb)
            wu_idx[fb] = wr.push(pfx + 'wu', dr['wuc'][fb], ford=fb)
        for fb in range(64):
            wr.ensure(wi_idx[fb] + 4)
            pg = (b.psA if fb % 2 == 0 else b.psB)[:, 0:TL]
            pu = (b.psA if fb % 2 == 0 else b.psB)[:, TL:2 * TL]

            def pe_gu(t, tc=tc, fb=fb, pg=pg, pu=pu):
                wr.wait(t, wi_idx[fb])
                if fb == 0:
                    t.wait_ge(s_dv.h, h_ready)
                if fb >= 2:
                    t.wait_ge(s_ac.h, silu_done[fb - 2])
                    t.wait_ge(s_dv.h, gu_done[fb - 2])
                for k in range(NK):
                    t.matmul(pg, wr.slot(wi_idx[fb], k),
                             b.xn[:, k * 2048 + tc * TL: k * 2048 + (tc + 1) * TL],
                             start=(k == 0), stop=False)
                t.matmul(pg, b.ovl[0:16, fb * 128:(fb + 1) * 128],
                         modhA[0:16, tc * TL:(tc + 1) * TL],
                         start=False, stop=True)
                wr.wait(t, wu_idx[fb])
                for k in range(NK):
                    inst = t.matmul(pu, wr.slot(wu_idx[fb], k),
                                    b.xn[:, k * 2048 + tc * TL: k * 2048 + (tc + 1) * TL],
                                    start=(k == 0), stop=(k == NK - 1))
                s_pe.inc(inst)
            b.on("tensor", pe_gu)
            wr.consumed(s_pe.v)
            wr.consumed(s_pe.v)
            mmv = s_pe.v

            gslot = b.pp[:, (fb % 2) * TL:(fb % 2 + 1) * TL]
            uslot = b.pp[:, 1024 + (fb % 2) * TL: 1024 + (fb % 2 + 1) * TL]

            def ac_silu(s, fb=fb, mmv=mmv, pg=pg, gslot=gslot):
                s.wait_ge(s_pe.h, mmv)
                if fb >= 2:
                    s.wait_ge(s_dv.h, gu_done[fb - 2])
                inst = s.activation(gslot, pg, AF.Silu)
                s_ac.inc(inst)
            b.on("scalar", ac_silu)
            silu_done[fb] = s_ac.v

            def dv_gu(v, tc=tc, fb=fb, mmv=mmv, pu=pu, gslot=gslot, uslot=uslot,
                      sl=s_ac.v):
                v.wait_ge(s_pe.h, mmv)
                v.tensor_copy(uslot, pu)
                v.wait_ge(s_ac.h, sl)
                inst = v.tensor_tensor(b.aux[:, fb * TL:(fb + 1) * TL],
                                       gslot, uslot, ALU.mult)
                s_dv.inc(inst)
            b.on("vector", dv_gu)
            gu_done[fb] = s_dv.v
        gu_all = s_dv.v

        for o in range(16):
            qidx = [wr.push(pfx + 'wd', dr['wdc'][o, q4], ford=o * 4 + q4) for q4 in range(4)]
            wr.ensure(qidx[-1] + 1)
            g = tc * 16 + o

            def sy_xb(sy, tc=tc, o=o, g=g):
                if g >= 4:
                    sy.wait_ge(s_dv.h, epi_dv[g - 4])
                    sy.wait_ge(b.rs[g % 4].h, epi_st[g - 4])
                else:
                    sy.wait_ge(s_dv.h, b.ln_xtmp_free)
                    if b.ln_store_ring is not None:
                        ctr, val = b.ln_store_ring[g % 4]
                        sy.wait_ge(ctr.h, val)
                inst = sy.dma_start(
                    b.xtmp[:, (g % 4) * TL:(g % 4 + 1) * TL],
                    xbase_dram[o * 128:(o + 1) * 128, tc * TL:(tc + 1) * TL])
                b.rx[g % 4].inc(inst, 16)
            b.on("sync", sy_xb)
            xb_ld = b.rx[g % 4].v

            pd = (b.psA if o % 2 == 0 else b.psB)[:, 0:TL]

            def pe_d(t, tc=tc, o=o, g=g, pd=pd, qidx=qidx, gu_all=gu_all):
                if o == 0:
                    t.wait_ge(s_dv.h, gu_all)
                if o >= 2:
                    t.wait_ge(s_dv.h, epi_dv[g - 2])
                for q4 in range(4):
                    wr.wait(t, qidx[q4])
                    for kk in range(NK):
                        fb = q4 * 16 + kk
                        inst = t.matmul(pd, wr.slot(qidx[q4], kk),
                                        b.aux[:, fb * TL:(fb + 1) * TL],
                                        start=(q4 == 0 and kk == 0),
                                        stop=(q4 == 3 and kk == NK - 1))
                s_pe.inc(inst)
            b.on("tensor", pe_d)
            for _ in range(4):
                wr.consumed(s_pe.v)
            mmd = s_pe.v

            def dv_epi(v, g=g, mmd=mmd, pd=pd, xb_ld=xb_ld):
                v.wait_ge(s_pe.h, mmd)
                v.wait_ge(b.rx[g % 4].h, xb_ld)
                xs = b.xtmp[:, (g % 4) * TL:(g % 4 + 1) * TL]
                inst = v.scalar_tensor_tensor(xs, pd, 0.5, xs, ALU.mult, ALU.add)
                s_dv.inc(inst)
            b.on("vector", dv_epi)
            epi_dv[g] = s_dv.v

            def sy_st(sy, tc=tc, o=o, g=g, ev=s_dv.v):
                sy.wait_ge(s_dv.h, ev)
                inst = sy.dma_start(
                    xout_dram[o * 128:(o + 1) * 128, tc * TL:(tc + 1) * TL],
                    b.xtmp[:, (g % 4) * TL:(g % 4 + 1) * TL])
                epi_st[g] = b.rs[g % 4].inc(inst, 16)
            b.on("sync", sy_st)

    b.final_rs = [r.v for r in b.rs]



def decl_l1_ffn(nc, pfx=""):
    dr = {}
    for w in ("wq", "wk", "wo"):
        dr[w + 'f'] = nc.dram_tensor(pfx + w + "f", [16, 128, 16, 128], F32, kind="ExternalInput").ap()
        dr[w + 'c'] = nc.dram_tensor(pfx + w + "c", [16, 128, 16, 128], BF16).ap()
    dr['wvtf'] = nc.dram_tensor(pfx + "wvtf", [4, 128, 16, 512], F32, kind="ExternalInput").ap()
    dr['wvtc'] = nc.dram_tensor(pfx + "wvtc", [4, 128, 16, 512], BF16).ap()
    dr['wif'] = nc.dram_tensor(pfx + "wif", [64, 128, 16, 128], F32, kind="ExternalInput").ap()
    dr['wic'] = nc.dram_tensor(pfx + "wic", [64, 128, 16, 128], BF16).ap()
    dr['wuf'] = nc.dram_tensor(pfx + "wuf", [64, 128, 16, 128], F32, kind="ExternalInput").ap()
    dr['wuc'] = nc.dram_tensor(pfx + "wuc", [64, 128, 16, 128], BF16).ap()
    dr['wdf'] = nc.dram_tensor(pfx + "wdf", [16, 4, 128, 16, 128], F32, kind="ExternalInput").ap()
    dr['wdc'] = nc.dram_tensor(pfx + "wdc", [16, 4, 128, 16, 128], BF16).ap()
    dr['apack'] = nc.dram_tensor(pfx + "apack", [16, 128, 16], F32, kind="ExternalInput").ap()
    dr['bmat'] = nc.dram_tensor(pfx + "bmat", [16, 8192], F32, kind="ExternalInput").ap()
    dr['lnw'] = nc.dram_tensor(pfx + "lnw", [128, 32], F32, kind="ExternalInput").ap()
    dr['masks'] = nc.dram_tensor(pfx + "masks", [8, 128, 512], F32, kind="ExternalInput").ap()
    dr['_pfx'] = pfx
    return dr


def decl_l1_ffn_scratch(nc, pfx):
    dr = {}
    dr['qscr'] = nc.dram_tensor(pfx + "qscr", [2048, 2048], BF16).ap()
    dr['kscr'] = nc.dram_tensor(pfx + "kscr", [2048, 2048], BF16).ap()
    dr['vscr'] = nc.dram_tensor(pfx + "vscr", [2048, 2048], BF16).ap()
    dr['kall'] = nc.dram_tensor(pfx + "kall", [4096, 2048], BF16).ap()
    dr['vall'] = nc.dram_tensor(pfx + "vall", [4096, 2048], BF16).ap()
    dr['xnscr'] = nc.dram_tensor(pfx + "xnscr", [2048, 2048], BF16).ap()
    dr['xbscr'] = nc.dram_tensor(pfx + "xbscr", [2048, 2048], F32).ap()
    return dr


def fill_l1_ffn(dr, pfx=""):
    items = []
    for w in ("wq", "wk", "wo"):
        for m in range(16):
            items.append((pfx + w, dr[w + 'f'][m], dr[w + 'c'][m]))
    for n in range(4):
        items.append((pfx + 'wvt', dr['wvtf'][n], dr['wvtc'][n]))
    for fb in range(64):
        items.append((pfx + 'wi', dr['wif'][fb], dr['wic'][fb]))
        items.append((pfx + 'wu', dr['wuf'][fb], dr['wuc'][fb]))
    for o in range(16):
        for q4 in range(4):
            items.append((pfx + 'wd', dr['wdf'][o, q4], dr['wdc'][o, q4]))
    return items


def emit_sub_l1_ffn(b, dr, wc, xin, xout, scr):
    pfx = dr.get('_pfx', '')
    s_g = b.sem("s_g")
    s_cc = b.sem("s_cc")
    s_dv = b.sem("s_dv")
    s_pe = b.sem("s_pe")
    qscr, kscr, vscr, kall, vall = (scr[k] for k in ("qscr", "kscr", "vscr", "kall", "vall"))
    xnscr, xbscr = scr['xnscr'], scr['xbscr']

    def g_small(g):
        inst = g.dma_start(b.b1v[:, 0:32], dr['lnw'])
        s_g.inc(inst, 16)
        for s8 in range(8):
            inst = g.dma_start(b.ovl[:, s8 * 512:(s8 + 1) * 512], dr['masks'][s8])
            s_g.inc(inst, 16)
        inst = g.dma_start(b.wbc[:, 0:256].rearrange("p (k c) -> p k c", k=16),
                           dr['apack'].rearrange("k p c -> p k c"))
        s_g.inc(inst, 16)
    b.on("gpsimd", g_small)
    small_done = s_g.v
    masks_ready = small_done

    s_xn0 = b.sem("s_xn").v
    emit_ln(b, xin, xbase_dram=xbscr, xscale=0.5, xn_dram=xnscr)
    xn_stored = b.sem("s_xn").v

    emit_rms(b, lambda k: b.b1v[:, k:k + 1], b.vec[:, 63:64],
             first_wait=[(b.sem("s_dv"), b.ln_norm_done), (s_g, small_done),
                         (b.sem("s_pe"), b.ln_aux_free), (b.sem("s_xn"), xn_stored)])

    wr = WRing(b, wc)

    def rhs_xn(k, tc):
        return b.xn[:, k * 2048 + tc * TL: k * 2048 + (tc + 1) * TL]

    emit_proj_fm(b, wr, pfx + 'wq', [dr['wqc'][m] for m in range(16)], rhs_xn,
                 lambda m, tc: ('d', qscr[m * 128:(m + 1) * 128, tc * TL:(tc + 1) * TL]),
                 norm_col=lambda m: b.vec[:, m:m + 1],
                 pe_extra_wait=[(b.sem("s_dv"), b.rms_done)])
    emit_proj_fm(b, wr, pfx + 'wk', [dr['wkc'][m] for m in range(16)], rhs_xn,
                 lambda m, tc: ('d', kscr[m * 128:(m + 1) * 128, tc * TL:(tc + 1) * TL]),
                 norm_col=lambda m: b.vec[:, 16 + m:17 + m])
    emit_proj_tm(b, wc, pfx + 'wvt', dr['wvtc'], vscr)
    proj_psum_free = b.proj_evac_done

    ri_totals = [r.v for r in b.ri]

    def g_gather(g):
        for p in range(8):
            g.wait_ge(b.ri[p].h, ri_totals[p])
        g.collective_compute("AllGather", ALU.bypass,
                             replica_groups=[[0, 1], [2, 3], [4, 5], [6, 7]],
                             ins=[kscr.opt()], outs=[kall.opt()]).then_inc(s_cc.h)
        s_cc.v += 1
        g.collective_compute("AllGather", ALU.bypass,
                             replica_groups=[[0, 1], [2, 3], [4, 5], [6, 7]],
                             ins=[vscr.opt()], outs=[vall.opt()]).then_inc(s_cc.h)
        s_cc.v += 1
    b.on("gpsimd", g_gather)
    gather_done = s_cc.v

    emit_attn_v2(b, qscr, kall, vall, b.xn, masks_ready, gather_done, proj_psum_free)

    emit_proj_fm(b, wr, pfx + 'wo', [dr['woc'][o] for o in range(16)], rhs_xn,
                 lambda m, tc: ('s', b.aux[:, m * 2048 + tc * TL: m * 2048 + (tc + 1) * TL]),
                 pe_extra_wait=[(s_dv, b.attn_done)])
    wo_done_dv = b.proj_evac_done

    s_ac = b.sem("s_ac")
    mod_done = {}

    def mk_mod_evac(tc, pev):
        def ac_mod(s, tc=tc, pev=pev):
            s.wait_ge(s_pe.h, pev)
            inst = s.activation(b.io[0:16, 2048 + tc * TL: 2048 + (tc + 1) * TL],
                                b.psE[0:16, :], AF.Tanh)
            s_ac.inc(inst)
        b.on("scalar", ac_mod)
        mod_done[tc] = s_ac.v
        return (s_ac, s_ac.v)

    hA2_done = emit_Amm(b, b.aux, mk_mod_evac,
                        first_wait=[(s_dv, wo_done_dv), (s_g, small_done)])
    hA2_pe = hA2_done[-1]

    sxr = emit_xn_reload(b, xnscr, first_wait=[(s_dv, wo_done_dv)])
    emit_rms(b, lambda k: b.b1v[:, 16 + k:17 + k], b.vec[:, 63:64],
             first_wait=[(b.sem("s_xr"), b.xn_reload_done), (s_pe, hA2_pe)])
    h_done = b.rms_done

    modhA_done = {}

    def mk_modha_evac(tc, pev):
        def dv_mha(v, tc=tc, pev=pev):
            v.wait_ge(s_pe.h, pev)
            v.wait_ge(s_ac.h, mod_done[NTC - 1])
            inst = v.scalar_tensor_tensor(
                b.io[0:16, tc * TL:(tc + 1) * TL], b.psE[0:16, :], 0.1,
                b.io[0:16, 2048 + tc * TL: 2048 + (tc + 1) * TL], ALU.mult, ALU.mult)
            s_dv.inc(inst)
        b.on("vector", dv_mha)
        modhA_done[tc] = s_dv.v
        return (s_dv, s_dv.v)

    emit_Amm(b, b.xn, mk_modha_evac, first_wait=[(s_dv, h_done)])

    s_gb = b.sem("s_gb")
    gb0 = s_gb.v

    def g_bmat(g):
        g.wait_ge(s_dv.h, b.attn_done)
        inst = g.dma_start(b.ovl[0:16, :], dr['bmat'])
        s_gb.inc(inst, 16)
    b.on("gpsimd", g_bmat)
    bmat_done = s_gb.v

    def pe_guard(t):
        t.wait_ge(s_gb.h, bmat_done)
        t.wait_ge(s_dv.h, modhA_done[NTC - 1])
    b.on("tensor", pe_guard)

    emit_l1_mlp(b, wr, dr, xbscr, xout, b.io, h_done)


def build_l1_ffn_nc():
    b = Bld("l1ffn")
    nc = b.nc
    dr = {}
    dr['xin'] = nc.dram_tensor("xin", [D, 2048], F32, kind="ExternalInput").ap()
    dr['xout'] = nc.dram_tensor("xout", [D, 2048], F32, kind="ExternalOutput").ap()
    dr.update(decl_l1_ffn(nc))
    scr = decl_l1_ffn_scratch(nc, "s_")

    b.start()
    alloc_common(b)
    wc = WCache(b)
    wc.fill(fill_l1_ffn(dr))
    emit_sub_l1_ffn(b, dr, wc, dr['xin'], dr['xout'], scr)
    emit_finwait(b)
    return b.finish()


def prep_l1_ffn_inputs(inputs, xs):
    wq = _tile_w(np.ascontiguousarray(np.asarray(inputs['L1_ffn_Wq']).T))
    wk = _tile_w(np.ascontiguousarray(np.asarray(inputs['L1_ffn_Wk']).T))
    wo = _tile_w(np.ascontiguousarray(np.asarray(inputs['L1_ffn_Wo']).T))
    wvt = _tile_rhs_v(np.ascontiguousarray(np.asarray(inputs['L1_ffn_Wv']).T))
    wi = _tile_w(np.ascontiguousarray(np.asarray(inputs['L1_ffn_W_in_base']).T))
    wu = _tile_w(np.ascontiguousarray(np.asarray(inputs['L1_ffn_W_up_base']).T))
    wd = _tile_w(np.ascontiguousarray(np.asarray(inputs['L1_ffn_W_down']).T))
    wd = np.ascontiguousarray(wd.reshape(16, 128, 4, 16, 128).transpose(0, 2, 1, 3, 4))
    apack = np.ascontiguousarray(np.asarray(inputs['L1_ffn_A']).reshape(16, 128, 16)).astype(np.float32)
    bmat = np.ascontiguousarray(np.asarray(inputs['L1_ffn_B'])).astype(np.float32)
    lnw = np.concatenate([_perpart(inputs['L1_ffn_ln1_w']),
                          _perpart(inputs['L1_ffn_ln2_w'])], axis=1).astype(np.float32)
    maps = []
    for c in range(8):
        maps.append({
            "xin": xs[c], "wqf": wq, "wkf": wk, "wvtf": wvt, "wof": wo,
            "wif": wi, "wuf": wu, "wdf": wd, "apack": apack, "bmat": bmat,
            "lnw": lnw, "masks": _attn_masks(c % 2),
        })
    return maps



def emit_proj_xpos(b: Bld, wr: WRing, wgroup, wsrcs, rhs_fn, cos_d, sin_d, dscr,
                   pe_extra_wait=None):
    s_pe, s_dv = b.sem("s_pe"), b.sem("s_dv")
    ri_base = [r.v for r in b.ri]
    xtmp_prev = getattr(b, '_xpos_prev_t12', None)
    idxs = [wr.push(wgroup, wsrcs[m], ford=m) for m in range(16)]
    evac_done = {}
    t12_done = {}
    rot_done = {}
    io_store = {}
    cs_ld = {}

    for m in range(16):
        wr.ensure(idxs[m] + 4)
        for tc in range(NTC):
            g = m * 4 + tc

            def sy_cs(sy, m=m, tc=tc, g=g):
                if g >= 2:
                    sy.wait_ge(s_dv.h, t12_done[g - 2])
                else:
                    if pe_extra_wait is not None:
                        for sem, val in pe_extra_wait:
                            sy.wait_ge(sem.h, val)
                    if xtmp_prev is not None:
                        sy.wait_ge(s_dv.h, xtmp_prev)
                inst = sy.dma_start(
                    b.xtmp[:, (g % 2) * TL:(g % 2 + 1) * TL],
                    cos_d[m * 128:(m + 1) * 128, tc * TL:(tc + 1) * TL])
                b.rx[g % 2].inc(inst, 16)
                inst = sy.dma_start(
                    b.xtmp[:, (2 + g % 2) * TL:(2 + g % 2 + 1) * TL],
                    sin_d[m * 128:(m + 1) * 128, tc * TL:(tc + 1) * TL])
                b.rx[2 + g % 2].inc(inst, 16)
                cs_ld[g] = (b.rx[g % 2].v, b.rx[2 + g % 2].v)
            b.on("sync", sy_cs)

            half = b.psA[:, (g % 2) * TL:(g % 2) * TL + TL]
            rhalf = b.psB[:, (g % 2) * TL:(g % 2) * TL + TL]
            qslot = b.pp[:, (g % 2) * TL:(g % 2 + 1) * TL]

            def pe_mm(t, m=m, tc=tc, g=g, half=half, idx=idxs[m]):
                if g == 0 and pe_extra_wait is not None:
                    for sem, val in pe_extra_wait:
                        t.wait_ge(sem.h, val)
                wr.wait(t, idx)
                if g >= 2:
                    t.wait_ge(s_dv.h, t12_done[g - 2])
                for k in range(NK):
                    inst = t.matmul(half, wr.slot(idxs[m], k), rhs_fn(k, tc),
                                    start=(k == 0), stop=(k == NK - 1))
                s_pe.inc(inst)
            b.on("tensor", pe_mm)
            mmv = s_pe.v
            if tc == 3:
                wr.consumed(mmv)

            def dv_ev(v, g=g, half=half, mmv=mmv, qslot=qslot):
                v.wait_ge(s_pe.h, mmv)
                inst = v.tensor_copy(qslot, half)
                s_dv.inc(inst)
            b.on("vector", dv_ev)
            evac_done[g] = s_dv.v

            def pe_rot(t, g=g, rhalf=rhalf, qslot=qslot, ev=s_dv.v):
                t.wait_ge(s_dv.h, ev)
                inst = t.matmul(rhalf, b.rt[:, :], qslot, start=True, stop=True)
                s_pe.inc(inst)
            b.on("tensor", pe_rot)
            rot_done[g] = s_pe.v

            tA = b.stat[:, (g % 2) * TL:(g % 2 + 1) * TL]
            tB = b.stat[:, (2 + g % 2) * TL:(2 + g % 2 + 1) * TL]
            ioslot = b.io[:, (g % 8) * TL:(g % 8 + 1) * TL]

            def dv_x(v, g=g, rhalf=rhalf, qslot=qslot, tA=tA, tB=tB,
                     ioslot=ioslot, rv=s_pe.v):
                v.wait_ge(b.rx[g % 2].h, cs_ld[g][0])
                v.wait_ge(b.rx[2 + g % 2].h, cs_ld[g][1])
                v.wait_ge(s_pe.h, rv)
                if g >= 8:
                    v.wait_ge(b.ri[g % 8].h, io_store[g - 8])
                else:
                    v.wait_ge(b.ri[g % 8].h, ri_base[g % 8])
                inst = v.tensor_tensor(tA, qslot, b.xtmp[:, (g % 2) * TL:(g % 2 + 1) * TL],
                                       ALU.mult)
                inst = v.tensor_tensor(tB, rhalf, b.xtmp[:, (2 + g % 2) * TL:(2 + g % 2 + 1) * TL],
                                       ALU.mult)
                b.fence(v, inst)
                inst = v.tensor_tensor(ioslot, tA, tB, ALU.add)
                s_dv.inc(inst)
            b.on("vector", dv_x)
            t12_done[g] = s_dv.v

            def sy_st(sy, m=m, tc=tc, g=g, ioslot=ioslot, ev=s_dv.v):
                sy.wait_ge(s_dv.h, ev)
                inst = sy.dma_start(
                    dscr[m * 128:(m + 1) * 128, tc * TL:(tc + 1) * TL], ioslot)
                io_store[g] = b.ri[g % 8].inc(inst, 16)
            b.on("sync", sy_st)

    b.proj_evac_done = t12_done[63]
    b._xpos_prev_t12 = t12_done[63]



def emit_attn_decay(b: Bld, qscr, kall, vall, dst_buf, masks_ready, gather_done,
                    psum_free, gammas):
    s_pe, s_dv, s_ac, s_cc = (b.sem(n) for n in ("s_pe", "s_dv", "s_ac", "s_cc"))
    kofs, vofs, qofs = 0, 8192, 16384
    head_done = {}
    kld = {}
    vld = {}
    qld = {}
    evac_rec = {}

    for h in range(NH):
        ks = h % 2

        def sy_k(sy, h=h, ks=ks):
            if h >= 2:
                sy.wait_ge(s_pe.h, head_done[h - 2])
            if h == 0:
                sy.wait_ge(s_cc.h, gather_done)
                sy.wait_ge(s_pe.h, b.ln_aux_free)
            for kt in range(32):
                r, j = kt % 2, kt // 2
                inst = sy.dma_start(
                    b.aux[:, kofs + ks * 4096 + kt * 128: kofs + ks * 4096 + (kt + 1) * 128],
                    kall[r * 2048 + h * 128:(r * 2048 + (h + 1) * 128), j * 128:(j + 1) * 128])
                b.rk[ks].inc(inst, 16)
            kld[h] = b.rk[ks].v
        b.on("sync", sy_k)

        def sc_v(s, h=h, ks=ks):
            if h >= 2:
                s.wait_ge(s_pe.h, head_done[h - 2])
            if h == 0:
                s.wait_ge(s_cc.h, gather_done)
                s.wait_ge(s_pe.h, b.ln_aux_free)
            for kt in range(32):
                r, j = kt % 2, kt // 2
                inst = s.dma_start(
                    b.aux[:, vofs + ks * 4096 + kt * 128: vofs + ks * 4096 + (kt + 1) * 128],
                    vall[r * 2048 + h * 128:(r * 2048 + (h + 1) * 128), j * 128:(j + 1) * 128],
                    transpose=True)
                b.rv[ks].inc(inst, 16)
            vld[h] = b.rv[ks].v
        b.on("scalar", sc_v)

        for jb in range(NTC):
            gq = (h * 4 + jb) % 2

            def sy_q(sy, h=h, jb=jb, gq=gq):
                gi = h * 4 + jb
                if gi >= 2:
                    sy.wait_ge(s_dv.h, evac_rec[gi - 2])
                inst = sy.dma_start(
                    b.aux[:, qofs + gq * TL: qofs + (gq + 1) * TL],
                    qscr[h * 128:(h + 1) * 128, jb * TL:(jb + 1) * TL])
                qld[(h, jb)] = b.rq[gq].inc(inst, 16)
            b.on("sync", sy_q)

            P = 4 * jb + 4
            ev_done = {}
            mask_done = {}
            spair_done = {}
            pv_done = {}

            for i in range(P + 1):
                def pe_kloop(t, h=h, jb=jb, i=i, P=P, gq=gq, ks=ks):
                    if i >= 1:
                        pi = i - 1
                        t.wait_ge(s_dv.h, mask_done.get(pi, ev_done[pi]))
                        if i == 1:
                            gi = h * 4 + jb
                            if gi >= 1:
                                t.wait_ge(s_dv.h, evac_rec[gi - 1])
                            elif psum_free is not None:
                                t.wait_ge(s_dv.h, psum_free)
                        for z in range(2):
                            kt = 2 * pi + z
                            pslot = b.pp[:, (pi % 2) * 1024 + z * TL:(pi % 2) * 1024 + (z + 1) * TL]
                            vt = b.aux[:, vofs + ks * 4096 + kt * 128: vofs + ks * 4096 + (kt + 1) * 128]
                            inst = t.matmul(b.psC, vt, pslot,
                                            start=(pi == 0 and z == 0), stop=(pi == P - 1 and z == 1))
                        s_pe.inc(inst)
                        pv_done[pi] = s_pe.v
                    if i < P:
                        if i == 0:
                            t.wait_ge(b.rq[gq].h, qld[(h, jb)])
                            if jb == 0:
                                t.wait_ge(b.rk[ks].h, kld[h])
                                t.wait_ge(b.rv[ks].h, vld[h])
                        if i >= 2:
                            t.wait_ge(s_dv.h, ev_done[i - 2])
                        ps = b.psA if i % 2 == 0 else b.psB
                        for z in range(2):
                            kt = 2 * i + z
                            ktile = b.aux[:, kofs + ks * 4096 + kt * 128: kofs + ks * 4096 + (kt + 1) * 128]
                            inst = t.matmul(ps[:, z * TL:(z + 1) * TL], ktile,
                                            b.aux[:, qofs + gq * TL: qofs + (gq + 1) * TL],
                                            start=True, stop=True)
                        s_pe.inc(inst)
                        spair_done[i] = s_pe.v
                b.on("tensor", pe_kloop)

                if i < P:
                    smin = 8 * jb

                    def dv_ev(v, h=h, jb=jb, i=i, smin=smin):
                        v.wait_ge(s_pe.h, spair_done[i])
                        if i >= 2:
                            v.wait_ge(s_pe.h, pv_done[i - 2])
                        if h == 0 and jb == 0 and i == 0:
                            v.wait_ge(b.sem("s_g").h, masks_ready)
                        ps = b.psA if i % 2 == 0 else b.psB
                        inst = None
                        for z in range(2):
                            kt = 2 * i + z
                            imm = float(gammas[h] ** (128 * (8 * jb - kt))) * RS128
                            pslot = b.pp[:, (i % 2) * 1024 + z * TL:(i % 2) * 1024 + (z + 1) * TL]
                            inst = v.tensor_scalar(pslot, ps[:, z * TL:(z + 1) * TL],
                                                   b.b1v[:, 32 + h:33 + h], imm,
                                                   ALU.mult, ALU.mult)
                        s_dv.inc(inst)
                    b.on("vector", dv_ev)
                    ev_done[i] = s_dv.v

                    kts = [2 * i, 2 * i + 1]
                    if kts[1] >= smin:
                        def dv_mask(v, i=i, kts=kts, smin=smin, ev=s_dv.v):
                            v.wait_ge(s_dv.h, ev)
                            inst = None
                            for z, kt in enumerate(kts):
                                if kt < smin:
                                    continue
                                sidx = kt - smin
                                pslot = b.pp[:, (i % 2) * 1024 + z * TL:(i % 2) * 1024 + (z + 1) * TL]
                                inst = v.tensor_tensor(pslot, pslot,
                                                       b.ovl[:, sidx * TL:(sidx + 1) * TL], ALU.mult)
                            s_dv.inc(inst)
                        b.on("vector", dv_mask)
                        mask_done[i] = s_dv.v

            ybf = b.pp[:, 2048:2048 + TL]
            ysq = b.pp[:, 2048 + TL:2048 + 2 * TL]
            mstat = b.stat[:, 0:TL]
            msq = b.stat[:, TL:2 * TL]
            tmp2 = b.stat[:, 2 * TL:3 * TL]

            def dv_y(v, h=h, jb=jb, P=P):
                v.wait_ge(s_pe.h, pv_done[P - 1])
                inst = v.tensor_copy(ybf, b.psC)
                s_dv.inc(inst)
            b.on("vector", dv_y)
            ycp = s_dv.v

            def ac_sq(s, ycp=ycp):
                s.wait_ge(s_dv.h, ycp)
                inst = s.activation(ysq, ybf, AF.Square)
                s_ac.inc(inst)
            b.on("scalar", ac_sq)
            sqd = s_ac.v

            def pe_gn(t, sqd=sqd):
                t.wait_ge(s_ac.h, sqd)
                t.matmul(b.psE, b.ones[:], ybf, start=True, stop=True)
                inst = t.matmul(b.psF, b.ones[:], ysq, start=True, stop=True)
                s_pe.inc(inst)
            b.on("tensor", pe_gn)
            gnm = s_pe.v

            def dv_st(v, gnm=gnm):
                v.wait_ge(s_pe.h, gnm)
                v.tensor_scalar_mul(mstat, b.psE, 1.0 / 128.0)
                inst = v.tensor_scalar_mul(msq, b.psF, 1.0 / 128.0)
                b.fence(v, inst)
                inst = v.tensor_tensor(tmp2, mstat, mstat, ALU.mult)
                b.fence(v, inst)
                inst = v.tensor_tensor(msq, msq, tmp2, ALU.subtract)
                s_dv.inc(inst)
            b.on("vector", dv_st)
            vard = s_dv.v

            def ac_rstd(s, vard=vard):
                s.wait_ge(s_dv.h, vard)
                inst = s.activation(msq, msq, AF.Ln, bias=b.vec[:, 62:63])
                b.fence(s, inst)
                inst = s.activation(msq, msq, AF.Exp, scale=-0.5)
                s_ac.inc(inst)
            b.on("scalar", ac_rstd)
            rsd = s_ac.v

            def dv_gn(v, h=h, jb=jb, rsd=rsd):
                v.wait_ge(s_ac.h, rsd)
                inst = v.tensor_tensor(tmp2, ybf, mstat, ALU.subtract)
                b.fence(v, inst)
                inst = v.tensor_tensor(tmp2, tmp2, msq, ALU.mult)
                b.fence(v, inst)
                inst = v.tensor_scalar(
                    dst_buf[:, h * 2048 + jb * TL: h * 2048 + (jb + 1) * TL],
                    tmp2, b.b1v[:, 48 + h:49 + h], b.b1v[:, 64 + h:65 + h],
                    ALU.mult, ALU.add)
                s_dv.inc(inst)
            b.on("vector", dv_gn)
            evac_rec[h * 4 + jb] = s_dv.v
        head_done[h] = s_pe.v

    b.attn_done = s_dv.v



def emit_gate_silu(b: Bld, wr: WRing, wgroup, wsrcs, rhs_fn, pe_extra_wait=None):
    s_pe, s_dv, s_ac = b.sem("s_pe"), b.sem("s_dv"), b.sem("s_ac")
    idxs = [wr.push(wgroup, wsrcs[m], ford=m) for m in range(16)]
    silu_done = {}
    tt_done = {}
    for m in range(16):
        wr.ensure(idxs[m] + 4)
        for tc in range(NTC):
            g = m * 4 + tc
            half = b.psA[:, (g % 2) * TL:(g % 2) * TL + TL]
            gslot = b.pp[:, (g % 2) * TL:(g % 2 + 1) * TL]

            def pe_mm(t, g=g, half=half, idx=idxs[m]):
                if g == 0 and pe_extra_wait is not None:
                    for sem, val in pe_extra_wait:
                        t.wait_ge(sem.h, val)
                wr.wait(t, idx)
                if g >= 2:
                    t.wait_ge(s_ac.h, silu_done[g - 2])
                for k in range(NK):
                    inst = t.matmul(half, wr.slot(idx, k), rhs_fn(k, tc),
                                    start=(k == 0), stop=(k == NK - 1))
                s_pe.inc(inst)
            b.on("tensor", pe_mm)
            mmv = s_pe.v
            if tc == 3:
                wr.consumed(mmv)

            def ac_si(s, g=g, half=half, mmv=mmv, gslot=gslot):
                s.wait_ge(s_pe.h, mmv)
                if g >= 2:
                    s.wait_ge(s_dv.h, tt_done[g - 2])
                inst = s.activation(gslot, half, AF.Silu)
                s_ac.inc(inst)
            b.on("scalar", ac_si)
            silu_done[g] = s_ac.v

            def dv_tt(v, m=m, tc=tc, g=g, gslot=gslot, sl=s_ac.v):
                v.wait_ge(s_ac.h, sl)
                xnk = b.xn[:, m * 2048 + tc * TL: m * 2048 + (tc + 1) * TL]
                inst = v.tensor_tensor(xnk, xnk, gslot, ALU.mult)
                s_dv.inc(inst)
            b.on("vector", dv_tt)
            tt_done[g] = s_dv.v
    b.gate_done = s_dv.v



def decl_l1_attn(nc, pfx=""):
    dr = {}
    for w in ("wq", "wk", "wg", "wp"):
        dr[w + 'f'] = nc.dram_tensor(pfx + w + "f", [16, 128, 16, 128], F32, kind="ExternalInput").ap()
        dr[w + 'c'] = nc.dram_tensor(pfx + w + "c", [16, 128, 16, 128], BF16).ap()
    dr['wvtf'] = nc.dram_tensor(pfx + "wvtf", [4, 128, 16, 512], F32, kind="ExternalInput").ap()
    dr['wvtc'] = nc.dram_tensor(pfx + "wvtc", [4, 128, 16, 512], BF16).ap()
    dr['vecs'] = nc.dram_tensor(pfx + "vecs", [128, 48], F32, kind="ExternalInput").ap()
    dr['rt'] = nc.dram_tensor(pfx + "rt", [128, 128], F32, kind="ExternalInput").ap()
    dr['masks'] = nc.dram_tensor(pfx + "masks", [8, 128, 512], F32, kind="ExternalInput").ap()
    for t in ("cosq", "sinq", "cosk", "sink"):
        dr[t] = nc.dram_tensor(pfx + t, [2048, 2048], F32, kind="ExternalInput").ap()
    dr['qscr'] = nc.dram_tensor(pfx + "qscr", [2048, 2048], BF16).ap()
    dr['kscr'] = nc.dram_tensor(pfx + "kscr", [2048, 2048], BF16).ap()
    dr['vscr'] = nc.dram_tensor(pfx + "vscr", [2048, 2048], BF16).ap()
    dr['kall'] = nc.dram_tensor(pfx + "kall", [4096, 2048], BF16).ap()
    dr['vall'] = nc.dram_tensor(pfx + "vall", [4096, 2048], BF16).ap()
    dr['xnscr'] = nc.dram_tensor(pfx + "xnscr", [2048, 2048], BF16).ap()
    dr['_pfx'] = pfx
    return dr


def fill_l1_attn(dr, pfx=""):
    items = []
    for w in ("wq", "wk", "wg", "wp"):
        for m in range(16):
            items.append((pfx + w, dr[w + 'f'][m], dr[w + 'c'][m]))
    for n in range(4):
        items.append((pfx + 'wvt', dr['wvtf'][n], dr['wvtc'][n]))
    return items


def emit_sub_l1_attn(b, dr, wc, xin, xout):
    pfx = dr.get('_pfx', '')
    s_g = b.sem("s_g")
    s_cc = b.sem("s_cc")
    s_dv = b.sem("s_dv")
    qscr, kscr, vscr, kall, vall = (dr[k] for k in ("qscr", "kscr", "vscr", "kall", "vall"))
    xnscr = dr['xnscr']

    def g_small(g):
        inst = g.dma_start(b.b1v[:, 32:80], dr['vecs'])
        s_g.inc(inst, 16)
        for s8 in range(8):
            inst = g.dma_start(b.ovl[:, s8 * 512:(s8 + 1) * 512], dr['masks'][s8])
            s_g.inc(inst, 16)
        inst = g.dma_start(b.rt[:, :], dr['rt'])
        s_g.inc(inst, 16)
    b.on("gpsimd", g_small)
    small_done = s_g.v
    masks_ready = small_done

    emit_ln(b, xin, xbase_dram=None, xn_dram=xnscr)
    xn_stored = b.sem("s_xn").v

    wr = WRing(b, wc)

    def rhs_xn(k, tc):
        return b.xn[:, k * 2048 + tc * TL: k * 2048 + (tc + 1) * TL]

    base_wait = [(b.sem("s_dv"), b.ln_norm_done), (s_g, small_done),
                 (b.sem("s_pe"), b.ln_aux_free)]
    emit_proj_xpos(b, wr, pfx + 'wq', [dr['wqc'][m] for m in range(16)], rhs_xn,
                   dr['cosq'], dr['sinq'], qscr, pe_extra_wait=base_wait)
    emit_proj_xpos(b, wr, pfx + 'wk', [dr['wkc'][m] for m in range(16)], rhs_xn,
                   dr['cosk'], dr['sink'], kscr)
    emit_proj_tm(b, wc, pfx + 'wvt', dr['wvtc'], vscr)
    proj_psum_free = b.proj_evac_done

    ri_totals = [r.v for r in b.ri]

    def g_gather(g):
        for p in range(8):
            g.wait_ge(b.ri[p].h, ri_totals[p])
        g.collective_compute("AllGather", ALU.bypass,
                             replica_groups=[[0, 1], [2, 3], [4, 5], [6, 7]],
                             ins=[kscr.opt()], outs=[kall.opt()]).then_inc(s_cc.h)
        s_cc.v += 1
        g.collective_compute("AllGather", ALU.bypass,
                             replica_groups=[[0, 1], [2, 3], [4, 5], [6, 7]],
                             ins=[vscr.opt()], outs=[vall.opt()]).then_inc(s_cc.h)
        s_cc.v += 1
    b.on("gpsimd", g_gather)
    gather_done = s_cc.v

    gammas = [1.0 - 2.0 ** (-5.0 - h) for h in range(NH)]
    emit_attn_decay_v2(b, qscr, kall, vall, b.xn, masks_ready, gather_done,
                       proj_psum_free, gammas)

    s_xr = b.sem("s_xr")
    xr0 = s_xr.v

    def sy_xr(sy):
        sy.wait_ge(s_dv.h, b.attn_done)
        sy.wait_ge(b.sem("s_xn").h, xn_stored)
        for k in range(NK):
            inst = sy.dma_start(b.aux[:, k * 2048:(k + 1) * 2048],
                                xnscr[k * 128:(k + 1) * 128, :])
            s_xr.inc(inst, 16)
    b.on("sync", sy_xr)
    xr_done = s_xr.v

    def rhs_aux(k, tc):
        return b.aux[:, k * 2048 + tc * TL: k * 2048 + (tc + 1) * TL]

    emit_gate_silu(b, wr, pfx + 'wg', [dr['wgc'][m] for m in range(16)], rhs_aux,
                   pe_extra_wait=[(s_xr, xr_done), (s_dv, b.attn_done)])

    emit_outproj(b, wr, pfx + 'wp', [dr['wpc'][o] for o in range(16)], b.xn,
                 xin, xout, attn_ready=b.gate_done,
                 xtmp_free=b._xpos_prev_t12)


def build_l1_attn_nc():
    b = Bld("l1attn")
    nc = b.nc
    dr = {}
    dr['xin'] = nc.dram_tensor("xin", [D, 2048], F32, kind="ExternalInput").ap()
    dr['xout'] = nc.dram_tensor("xout", [D, 2048], F32, kind="ExternalOutput").ap()
    dr.update(decl_l1_attn(nc))

    b.start()
    alloc_common(b)
    wc = WCache(b)
    wc.fill(fill_l1_attn(dr))
    emit_sub_l1_attn(b, dr, wc, dr['xin'], dr['xout'])
    emit_finwait(b)
    return b.finish()


def _xpos_tables(hpar, with_gamma):
    C = 2048
    idx = _core_token_idx(hpar).astype(np.float64)
    i2 = np.arange(0, C, 2, dtype=np.float64)
    dim_scale = (i2 + 0.4 * C) / (1.4 * C)
    inv_freq = 1.0 / 10000.0 ** (i2 / C)
    sinus = idx[None, :] * inv_freq[:, None]
    scale = dim_scale[:, None] ** (idx[None, :] / 512.0)
    sin = np.sin(sinus) * scale
    cos = np.cos(sinus) * scale
    sin2 = np.repeat(sin, 2, axis=0)
    cos2 = np.repeat(cos, 2, axis=0)
    if with_gamma:
        col = np.arange(2048)
        pat = (col % 512) // 128 * 256 + (col % 128)
        gam = np.array([1.0 - 2.0 ** (-5.0 - h) for h in range(16)])
        gfac = gam[(np.arange(2048) // 128)][:, None] ** pat[None, :]
        sin2 = sin2 * gfac
        cos2 = cos2 * gfac
    return (np.ascontiguousarray(cos2).astype(np.float32),
            np.ascontiguousarray(sin2).astype(np.float32))


def prep_l1_attn_inputs(inputs, xs):
    wqkv = np.asarray(inputs['L1_attn_w_qkv'])
    wq = _tile_w(np.ascontiguousarray(wqkv[:, 0:2048]))
    wk = _tile_w(np.ascontiguousarray(wqkv[:, 2048:4096]))
    wvt = _tile_rhs_v(np.ascontiguousarray(wqkv[:, 4096:6144]))
    wg = _tile_w(np.asarray(inputs['L1_attn_w_g']))
    wp = _tile_w(np.asarray(inputs['L1_attn_w_proj']))
    gnw = _perpart(inputs['L1_attn_gn_weight'])
    gnb = _perpart(inputs['L1_attn_gn_bias'])
    RT = np.zeros((128, 128), dtype=np.float32)
    for i in range(64):
        RT[2 * i + 1, 2 * i] = -1.0
        RT[2 * i, 2 * i + 1] = 1.0
    gam = np.array([1.0 - 2.0 ** (-5.0 - h) for h in range(16)], dtype=np.float64)
    maps = []
    for c in range(8):
        hpar = c % 2
        row = np.arange(128, dtype=np.float64)
        dvec = (gam[None, :] ** (128.0 * hpar - row[:, None])).astype(np.float32)
        vecs = np.concatenate([dvec, gnw, gnb], axis=1).astype(np.float32)
        cosq, sinq = _xpos_tables(hpar, True)
        cosk, sink = _xpos_tables(hpar, False)
        maps.append({
            "xin": xs[c], "wqf": wq, "wkf": wk, "wvtf": wvt, "wgf": wg, "wpf": wp,
            "vecs": vecs, "rt": RT, "masks": _attn_masks(hpar),
            "cosq": cosq, "sinq": sinq, "cosk": cosk, "sink": sink,
        })
    return maps



def build_fused_nc():
    b = Bld("mkfused")
    nc = b.nc
    xin = nc.dram_tensor("xin", [D, 2048], F32, kind="ExternalInput").ap()
    xout = nc.dram_tensor("xout", [D, 2048], F32, kind="ExternalOutput").ap()
    xA = nc.dram_tensor("xbufA", [D, 2048], F32).ap()
    xB = nc.dram_tensor("xbufB", [D, 2048], F32).ap()

    d_f0 = decl_l0_ffn(nc, "f0_")
    d_a0 = decl_l0_attn(nc, "a0_")
    d_f1 = decl_l1_ffn(nc, "f1_")
    d_a1 = decl_l1_attn(nc, "a1_")
    scr4 = decl_l1_ffn_scratch(nc, "s4_")
    scr6 = decl_l1_ffn_scratch(nc, "s6_")

    b.start()
    alloc_common(b)
    wc = WCache(b)
    s_pe = b.sem("s_pe")
    wc.fill(fill_l0_ffn(d_f0, "f0_"))

    emit_sub_l0_ffn(b, d_f0, wc, xin, xA)
    wc.fill(fill_l0_attn(d_a0, "a0_"), gate=(s_pe, max(0, s_pe.v - 4000)))
    emit_boundary(b)
    emit_sub_l0_attn(b, d_a0, wc, xA, xB)
    wc.fill(fill_l1_ffn(d_f1, "f1_"), gate=(s_pe, max(0, s_pe.v - 3000)))
    emit_boundary(b)
    emit_sub_l0_ffn(b, d_f0, wc, xB, xA)
    emit_boundary(b)
    emit_sub_l1_ffn(b, d_f1, wc, xA, xB, scr4)
    wc.fill(fill_l1_attn(d_a1, "a1_"), gate=(s_pe, max(0, s_pe.v - 3000)))
    emit_boundary(b)
    emit_sub_l1_attn(b, d_a1, wc, xB, xA)
    emit_boundary(b)
    emit_sub_l1_ffn(b, d_f1, wc, xA, xout, scr6)
    emit_finwait(b)
    return b.finish()


def prep_fused_inputs(inputs, xs):
    m_f0 = prep_l0_ffn_inputs(inputs, xs)
    m_a0 = prep_l0_attn_inputs(inputs, xs)
    m_f1 = prep_l1_ffn_inputs(inputs, xs)
    m_a1 = prep_l1_attn_inputs(inputs, xs)
    maps = []
    for c in range(8):
        mp = {"xin": xs[c]}
        for pfx, mm in (("f0_", m_f0[c]), ("a0_", m_a0[c]),
                        ("f1_", m_f1[c]), ("a1_", m_a1[c])):
            for k, v in mm.items():
                if k == "xin":
                    continue
                mp[pfx + k] = v
        maps.append(mp)
    return maps


_NC_CACHE = {}


def kernel_fused(**inputs):
    xs = _shard_x(inputs['L0_attn_query'])
    if 'fused' not in _NC_CACHE:
        _NC_CACHE['fused'] = build_fused_nc()
    nc = _NC_CACHE['fused']
    maps = prep_fused_inputs(inputs, xs)
    res = run_bass_kernel_spmd(nc, maps, core_ids=list(range(8)))
    xs = [np.asarray(res.results[c]["xout"]) for c in range(8)]
    return _unshard_x(xs)


def kernel(**inputs):
    return kernel_fused(**inputs)



def _emit_attn_v2(b: Bld, qscr, kall, vall, dst_buf, masks_ready, gather_done,
                  psum_free, mode, gammas=None):
    s_pe, s_dv, s_ac, s_cc = (b.sem(n) for n in ("s_pe", "s_dv", "s_ac", "s_cc"))
    s_g = b.sem("s_g")
    VOFS, KOFS, QOFS = 0, 16384, 3072

    if mode == 'softmax':
        def dv_nc1(v):
            inst = v.tensor_tensor(b.vec[:, 32:48], b.vec[:, 0:16], b.vec[:, 16:32], ALU.mult)
            s_dv.inc(inst)
        b.on("vector", dv_nc1)
        nc1 = s_dv.v

        def ac_nc(s):
            s.wait_ge(s_dv.h, nc1)
            inst = s.activation(b.vec[:, 32:48], b.vec[:, 32:48], AF.Ln)
            b.fence(s, inst)
            inst = s.activation(b.vec[:, 32:48], b.vec[:, 32:48], AF.Exp, scale=0.5)
            s_ac.inc(inst)
        b.on("scalar", ac_nc)
        nc2 = s_ac.v

        def dv_nc2(v):
            v.wait_ge(s_ac.h, nc2)
            inst = v.tensor_scalar(b.vec[:, 32:48], b.vec[:, 32:48], -RS128, -10.0,
                                   ALU.mult, ALU.add)
            s_dv.inc(inst)
        b.on("vector", dv_nc2)
        negc_done = s_dv.v
    else:
        negc_done = None

    qfree = {}
    prev_evac = None
    prev_gn = psum_free
    group_done = {}

    for hg in range(4):
        kbase = [b.rk4[hh].v for hh in range(4)]
        vbase = [b.rv4[q].v for q in range(4)]

        def sy_k(sy, hg=hg):
            if hg >= 1:
                sy.wait_ge(s_pe.h, group_done[hg - 1])
            if hg == 0:
                sy.wait_ge(s_cc.h, gather_done)
                sy.wait_ge(s_pe.h, b.ln_aux_free)
            for hh in range(4):
                h = hg * 4 + hh
                for r in range(2):
                    srcap = kall[r * 2048 + h * 128: r * 2048 + (h + 1) * 128, :]
                    dst = b.aux[:, KOFS + hh * 4096 + r * 2048:
                                KOFS + hh * 4096 + (r + 1) * 2048]
                    inst = sy.dma_start(dst, srcap)
                    b.rk4[hh].inc(inst, 16)
        b.on("sync", sy_k)

        def sc_v(s, hg=hg):
            if hg >= 1:
                s.wait_ge(s_pe.h, group_done[hg - 1])
            if hg == 0:
                s.wait_ge(s_cc.h, gather_done)
                s.wait_ge(s_pe.h, b.ln_aux_free)
            for kt in range(32):
                r, j = kt % 2, kt // 2
                srcap = vall[r * 2048 + j * 128: r * 2048 + (j + 1) * 128,
                             hg * 512:(hg + 1) * 512]
                inst = s.dma_start(b.aux[:, VOFS + kt * 512: VOFS + (kt + 1) * 512],
                                   srcap)
                b.rv4[kt // 8].inc(inst, 16)
        b.on("scalar", sc_v)

        for hh in range(4):
            h = hg * 4 + hh
            for jb in range(NTC):
                gi = h * 4 + jb
                gq = gi % 2
                P = 4 * jb + 4

                def sy_q(sy, h=h, jb=jb, gq=gq, gi=gi):
                    if gi >= 2:
                        sy.wait_ge(s_pe.h, qfree[gi - 2])
                    inst = sy.dma_start(
                        b.pp[:, QOFS + gq * TL: QOFS + (gq + 1) * TL],
                        qscr[h * 128:(h + 1) * 128, jb * TL:(jb + 1) * TL])
                    b.rq[gq].inc(inst, 16)
                b.on("sync", sy_q)
                qld = b.rq[gq].v

                pe0, ac0, dv0 = s_pe.v, s_ac.v, s_dv.v
                mask_val = {}
                ev_val = {}
                dvc = dv0
                if mode == 'softmax':
                    for pi in range(4 * jb, P):
                        dvc += 1
                        mask_val[pi] = dvc
                    evac_val = dvc + 1
                else:
                    for i in range(P):
                        dvc += 1
                        ev_val[i] = dvc
                        if i >= 4 * jb:
                            dvc += 1
                            mask_val[i] = dvc
                    ycp_val = dvc + 1
                    vard_val = dvc + 2
                    evac_val = dvc + 3

                def pe_sec(t, h=h, hh=hh, jb=jb, gi=gi, gq=gq, P=P,
                           pe0=pe0, ac0=ac0, dv0=dv0, qld=qld,
                           kbase=list(kbase), vbase=list(vbase), prev_evac=prev_evac,
                           mask_val=dict(mask_val), ev_val=dict(ev_val)):
                    for i in range(P + 1):
                        if i >= 1:
                            pi = i - 1
                            if mode == 'softmax':
                                t.wait_ge(s_ac.h, ac0 + pi + 1)
                                if pi in mask_val:
                                    t.wait_ge(s_dv.h, mask_val[pi])
                            else:
                                t.wait_ge(s_dv.h, mask_val.get(pi, ev_val[pi]))
                            if i == 1 and prev_evac is not None:
                                t.wait_ge(s_dv.h, prev_evac)
                            if i == 1:
                                for q in range(jb + 1):
                                    t.wait_ge(b.rv4[q].h, vbase[q] + 16 * 8)
                            for z in range(2):
                                kt = 2 * pi + z
                                pslot = b.pp[:, (pi % 2) * 1024 + z * TL:(pi % 2) * 1024 + (z + 1) * TL]
                                vt = b.aux[:, VOFS + kt * 512 + hh * 128: VOFS + kt * 512 + (hh + 1) * 128]
                                inst = t.matmul(b.psC, vt, pslot,
                                                start=(pi == 0 and z == 0),
                                                stop=(pi == P - 1 and z == 1))
                                if mode == 'softmax':
                                    inst = t.matmul(b.psD, b.ones[:], pslot,
                                                    start=(pi == 0 and z == 0),
                                                    stop=(pi == P - 1 and z == 1))
                        if i < P:
                            if i == 0:
                                t.wait_ge(b.rq[gq].h, qld)
                                if jb == 0 and hh == 0:
                                    if gi == 0 and psum_free is not None:
                                        t.wait_ge(s_dv.h, psum_free)
                                if jb == 0:
                                    t.wait_ge(b.rk4[hh].h, kbase[hh] + 32)
                            if i >= 2:
                                t.wait_ge(s_ac.h, ac0 + i - 1) if mode == 'softmax' \
                                    else t.wait_ge(s_dv.h, ev_val[i - 2])
                            ps = b.psA if i % 2 == 0 else b.psB
                            for z in range(2):
                                kt = 2 * i + z
                                kcol = (kt % 2) * 2048 + (kt // 2) * 128
                                ktile = b.aux[:, KOFS + hh * 4096 + kcol: KOFS + hh * 4096 + kcol + 128]
                                inst = t.matmul(ps[:, z * TL:(z + 1) * TL], ktile,
                                                b.pp[:, QOFS + gq * TL: QOFS + (gq + 1) * TL],
                                                start=True, stop=True)
                            s_pe.inc(inst)
                        else:
                            s_pe.inc(inst)
                b.on("tensor", pe_sec)
                assert s_pe.v == pe0 + P + 1
                pe_end = s_pe.v
                qfree[gi] = pe_end

                if mode == 'softmax':
                    def ac_sec(s, h=h, jb=jb, P=P, pe0=pe0, dv0=dv0, gi=gi):
                        if gi == 0 and negc_done is not None:
                            s.wait_ge(s_dv.h, negc_done)
                        for i in range(P):
                            s.wait_ge(s_pe.h, pe0 + i + 1)
                            ps = b.psA if i % 2 == 0 else b.psB
                            inst = s.activation(b.pp[:, (i % 2) * 1024:(i % 2 + 1) * 1024],
                                                ps[:], AF.Exp, scale=RS128,
                                                bias=b.vec[:, 32 + h:33 + h])
                            s_ac.inc(inst)
                    b.on("scalar", ac_sec)

                    def dv_sec(v, h=h, jb=jb, P=P, pe0=pe0, ac0=ac0, gi=gi,
                               mask_val=dict(mask_val)):
                        if gi == 0:
                            v.wait_ge(s_g.h, masks_ready)
                        for pi in range(4 * jb, P):
                            v.wait_ge(s_ac.h, ac0 + pi + 1)
                            inst = None
                            for z in range(2):
                                kt = 2 * pi + z
                                if kt < 8 * jb:
                                    continue
                                sidx = kt - 8 * jb
                                pslot = b.pp[:, (pi % 2) * 1024 + z * TL:(pi % 2) * 1024 + (z + 1) * TL]
                                inst = v.tensor_tensor(pslot, pslot,
                                                       b.ovl[:, sidx * TL:(sidx + 1) * TL], ALU.mult)
                            s_dv.inc(inst)
                        v.wait_ge(s_pe.h, pe0 + P + 1)
                        inst = v.reciprocal(b.stat[:, 0:TL], b.psD)
                        b.fence(v, inst)
                        inst = v.tensor_tensor(
                            dst_buf[:, h * 2048 + jb * TL: h * 2048 + (jb + 1) * TL],
                            b.psC, b.stat[:, 0:TL], ALU.mult)
                        s_dv.inc(inst)
                    b.on("vector", dv_sec)
                else:
                    def dv_sec(v, h=h, hh=hh, jb=jb, P=P, pe0=pe0, gi=gi,
                               mask_val=dict(mask_val), ev_val=dict(ev_val)):
                        if gi == 0:
                            v.wait_ge(s_g.h, masks_ready)
                        for i in range(P):
                            v.wait_ge(s_pe.h, pe0 + i + 1)
                            ps = b.psA if i % 2 == 0 else b.psB
                            inst = None
                            for z in range(2):
                                kt = 2 * i + z
                                imm = float(gammas[h] ** (128 * (8 * jb - kt))) * RS128
                                pslot = b.pp[:, (i % 2) * 1024 + z * TL:(i % 2) * 1024 + (z + 1) * TL]
                                inst = v.tensor_scalar(pslot, ps[:, z * TL:(z + 1) * TL],
                                                       b.b1v[:, 32 + h:33 + h], imm,
                                                       ALU.mult, ALU.mult)
                            s_dv.inc(inst)
                            if i >= 4 * jb:
                                inst2 = None
                                for z in range(2):
                                    kt = 2 * i + z
                                    if kt < 8 * jb:
                                        continue
                                    sidx = kt - 8 * jb
                                    pslot = b.pp[:, (i % 2) * 1024 + z * TL:(i % 2) * 1024 + (z + 1) * TL]
                                    inst2 = v.tensor_tensor(pslot, pslot,
                                                            b.ovl[:, sidx * TL:(sidx + 1) * TL], ALU.mult)
                                s_dv.inc(inst2)
                        ybf = b.pp[:, 2048:2048 + TL]
                        v.wait_ge(s_pe.h, pe0 + P + 1)
                        inst = v.tensor_copy(ybf, b.psC)
                        s_dv.inc(inst)
                    b.on("vector", dv_sec)

                    ycp = s_dv.v

                    def ac_gn1(s, ycp=ycp):
                        s.wait_ge(s_dv.h, ycp)
                        inst = s.activation(b.pp[:, 2048 + TL:2048 + 2 * TL],
                                            b.pp[:, 2048:2048 + TL], AF.Square)
                        s_ac.inc(inst)
                    b.on("scalar", ac_gn1)
                    sqd = s_ac.v

                    def pe_gn(t, sqd=sqd, prev_gn=prev_gn):
                        t.wait_ge(s_ac.h, sqd)
                        if prev_gn is not None:
                            t.wait_ge(s_dv.h, prev_gn)
                        t.matmul(b.psE, b.ones[:], b.pp[:, 2048:2048 + TL],
                                 start=True, stop=True)
                        inst = t.matmul(b.psF, b.ones[:], b.pp[:, 2048 + TL:2048 + 2 * TL],
                                        start=True, stop=True)
                        s_pe.inc(inst)
                    b.on("tensor", pe_gn)
                    gnm = s_pe.v
                    qfree[gi] = gnm

                    def dv_gn2(v, gnm=gnm):
                        v.wait_ge(s_pe.h, gnm)
                        v.tensor_scalar_mul(b.stat[:, 0:TL], b.psE, 1.0 / 128.0)
                        inst = v.tensor_scalar_mul(b.stat[:, TL:2 * TL], b.psF, 1.0 / 128.0)
                        b.fence(v, inst)
                        inst = v.tensor_tensor(b.stat[:, 2 * TL:3 * TL],
                                               b.stat[:, 0:TL], b.stat[:, 0:TL], ALU.mult)
                        b.fence(v, inst)
                        inst = v.tensor_tensor(b.stat[:, TL:2 * TL], b.stat[:, TL:2 * TL],
                                               b.stat[:, 2 * TL:3 * TL], ALU.subtract)
                        s_dv.inc(inst)
                    b.on("vector", dv_gn2)
                    vard = s_dv.v
                    prev_gn = vard

                    def ac_gn2(s, vard=vard):
                        s.wait_ge(s_dv.h, vard)
                        inst = s.activation(b.stat[:, TL:2 * TL], b.stat[:, TL:2 * TL],
                                            AF.Ln, bias=b.vec[:, 62:63])
                        b.fence(s, inst)
                        inst = s.activation(b.stat[:, TL:2 * TL], b.stat[:, TL:2 * TL],
                                            AF.Exp, scale=-0.5)
                        s_ac.inc(inst)
                    b.on("scalar", ac_gn2)
                    rsd = s_ac.v

                    def dv_gn3(v, h=h, jb=jb, rsd=rsd):
                        v.wait_ge(s_ac.h, rsd)
                        inst = v.tensor_tensor(b.stat[:, 2 * TL:3 * TL],
                                               b.pp[:, 2048:2048 + TL],
                                               b.stat[:, 0:TL], ALU.subtract)
                        b.fence(v, inst)
                        inst = v.tensor_tensor(b.stat[:, 2 * TL:3 * TL],
                                               b.stat[:, 2 * TL:3 * TL],
                                               b.stat[:, TL:2 * TL], ALU.mult)
                        b.fence(v, inst)
                        inst = v.tensor_scalar(
                            dst_buf[:, h * 2048 + jb * TL: h * 2048 + (jb + 1) * TL],
                            b.stat[:, 2 * TL:3 * TL],
                            b.b1v[:, 48 + h:49 + h], b.b1v[:, 64 + h:65 + h],
                            ALU.mult, ALU.add)
                        s_dv.inc(inst)
                    b.on("vector", dv_gn3)

                prev_evac = s_dv.v
        group_done[hg] = s_pe.v

    b.attn_done = s_dv.v


def emit_attn_v2(b, qscr, kall, vall, dst_buf, masks_ready, gather_done, psum_free):
    _emit_attn_v2(b, qscr, kall, vall, dst_buf, masks_ready, gather_done,
                  psum_free, 'softmax')


def emit_attn_decay_v2(b, qscr, kall, vall, dst_buf, masks_ready, gather_done,
                       psum_free, gammas):
    _emit_attn_v2(b, qscr, kall, vall, dst_buf, masks_ready, gather_done,
                  psum_free, 'decay', gammas)



def emit_proj_tm(b: Bld, wc: WCache, wgroup, wsrc, dscr, bvrow_sb=None,
                 pe_extra_wait=None):
    s_pe, s_dv = b.sem("s_pe"), b.sem("s_dv")
    s_k = b.sem("s_k2")
    ri_base = [r.v for r in b.ri]
    pe_base = s_pe.v
    dv_base = s_dv.v
    io_store = {}
    evac_done = {}
    prev_n_pe = None
    for n in range(4):
        rw_t = [b.rw[0].v + 16]

        def sc_w(s, n=n, prev_n_pe=prev_n_pe):
            if prev_n_pe is not None:
                s.wait_ge(s_pe.h, prev_n_pe)
            else:
                s.wait_ge(s_pe.h, pe_base)
            wc.wait(s, wgroup, n)
            inst = s.dma_start(b.wring[:, :].rearrange("p (k c) -> p k c", k=16),
                               wsrc[n])
            b.rw[0].inc(inst, 16)
        b.on("scalar", sc_w)

        for tt in range(NTT):
            g = n * 16 + tt
            half = (b.psA if (g % 4) < 2 else b.psB)[:, (g % 2) * TL:(g % 2) * TL + TL]

            def pe_mm(t, n=n, tt=tt, g=g, half=half, rw_t=list(rw_t)):
                if tt == 0:
                    if g == 0 and pe_extra_wait is not None:
                        for sem, val in pe_extra_wait:
                            t.wait_ge(sem.h, val)
                    t.wait_ge(b.rw[0].h, rw_t[0])
                if g >= 4:
                    t.wait_ge(s_dv.h, evac_done[g - 4])
                else:
                    t.wait_ge(s_dv.h, dv_base)
                for k in range(NK):
                    sl, kk = k // 4, k % 4
                    inst = t.matmul(half,
                                    b.xn[:, k * 2048 + tt * 128: k * 2048 + (tt + 1) * 128],
                                    b.wring[:, sl * 2048 + kk * 512: sl * 2048 + (kk + 1) * 512],
                                    start=(k == 0), stop=(bvrow_sb is None and k == NK - 1))
                if bvrow_sb is not None:
                    inst = t.matmul(half, b.onesrow[0:1, :],
                                    bvrow_sb[0:1, n * TL:(n + 1) * TL],
                                    start=False, stop=True)
                s_pe.inc(inst)
            b.on("tensor", pe_mm)
            mmv = s_pe.v

            slot = b.io[:, (g % 8) * TL:(g % 8 + 1) * TL]

            def dv_ev(v, g=g, half=half, mmv=mmv, slot=slot):
                v.wait_ge(s_pe.h, mmv)
                if g >= 8:
                    v.wait_ge(b.ri[g % 8].h, io_store[g - 8])
                else:
                    v.wait_ge(b.ri[g % 8].h, ri_base[g % 8])
                inst = v.tensor_copy(slot, half)
                s_dv.inc(inst)
            b.on("vector", dv_ev)
            evac_done[g] = s_dv.v

            def sy_st(sy, n=n, tt=tt, g=g, slot=slot, ev=s_dv.v):
                sy.wait_ge(s_dv.h, ev)
                inst = sy.dma_start(
                    dscr[tt * 128:(tt + 1) * 128, n * TL:(n + 1) * TL], slot)
                io_store[g] = b.ri[g % 8].inc(inst, 16)
            b.on("sync", sy_st)
        prev_n_pe = s_pe.v

    b.proj_evac_done = evac_done[63]


# version 10 (final)
# speedup vs baseline: 1.2116x; 1.0059x over previous
import math
import numpy as np
from contextlib import ExitStack

import concourse.bass as bass
import concourse.mybir as mybir
from concourse.bass_utils import run_bass_kernel_spmd

F32 = mybir.dt.float32
BF16 = mybir.dt.bfloat16
AF = mybir.ActivationFunctionType
ALU = mybir.AluOpType
AX = mybir.AxisListType

D = 2048
NK = 16
TL = 512
NTC = 4
NTT = 16
NH = 16
HD = 128
EPS_LN = 1e-5
RS128 = 1.0 / math.sqrt(128.0)



def _tile_w(w):
    din, dout = w.shape
    return np.ascontiguousarray(
        w.reshape(din // 128, 128, dout // 128, 128).transpose(2, 1, 0, 3)
    ).astype(np.float32, copy=False)


def _tile_rhs_v(w):
    return np.ascontiguousarray(
        np.asarray(w).reshape(16, 128, 4, 512).transpose(2, 1, 0, 3)
    ).astype(np.float32, copy=False)


def _perpart(v):
    return np.ascontiguousarray(np.asarray(v).reshape(NK, 128).T).astype(np.float32, copy=False)


def _core_token_idx(h):
    j = np.arange(NTT)
    return ((2 * j + h)[:, None] * 128 + np.arange(128)[None, :]).reshape(-1)



class Ctr:
    def __init__(self, h):
        self.h = h
        self.v = 0

    def inc(self, inst, n=1):
        inst.then_inc(self.h, n)
        self.v += n
        return self.v


class Bld:

    def __init__(self, name="mk"):
        self.nc = bass.Bass("TRN2", target_bir_lowering=False, name=name, num_swdge_queues=2)
        self.es = ExitStack()
        self._sem = {}
        self._block = None

    def sem(self, name) -> Ctr:
        if name not in self._sem:
            self._sem[name] = Ctr(self.es.enter_context(self.nc.semaphore(name)))
        return self._sem[name]

    def sb(self, name, shape, dt):
        return self.es.enter_context(self.nc.sbuf_tensor(name, shape, dt))

    def ps(self, name, shape, dt=F32):
        return self.es.enter_context(self.nc.psum_tensor(name, shape, dt))

    def ring(self, name, n):
        return [self.sem(f"{name}{i}") for i in range(n)]

    def fence(self, eng, inst):
        s = self.sem("s_f")
        s.inc(inst)
        eng.wait_ge(s.h, s.v)

    def start(self):
        self._block = self.nc.Block()
        self.block = self._block.__enter__()
        return self

    def on(self, eng, fn):
        getattr(self.block, eng)(fn)

    def finish(self):
        self._block.__exit__(None, None, None)
        self.es.close()
        return self.nc


def alloc_common(b: Bld):
    b.xn = b.sb("xn_buf", [128, 32768], BF16)
    b.aux = b.sb("aux_buf", [128, 32768], BF16)
    b.xtmp = b.sb("xtmp", [128, 4096], F32)
    b.wring = b.sb("wring", [128, 8192], BF16)
    b.stat = b.sb("stat", [128, 2048], F32)
    b.wbc = b.sb("wbc", [128, 2048], BF16)
    b.vec = b.sb("vecbuf", [128, 64], F32)
    b.b1v = b.sb("b1v", [128, 128], F32)
    b.rt = b.sb("rtbuf", [128, 128], BF16)
    b.gwb = b.sb("gwb", [128, 64], BF16)
    b.ovl = b.sb("ovl", [128, 8192], BF16)
    b.wrow = b.ovl[0:1, :]
    b.pp = b.sb("pp", [128, 4096], BF16)
    b.io = b.sb("io", [128, 4096], BF16)
    b.ones = b.sb("ones128", [128, 128], BF16)
    b.onesrow = b.sb("onesrow", [1, 128], BF16)
    b.idf = b.sb("idf32", [128, 128], F32)
    b.psA = b.ps("psA", [128, 1024])
    b.psB = b.ps("psB", [128, 1024])
    b.psBig = b.ps("psBig", [128, 2048])
    b.psC = b.psBig[:, 0:512]
    b.psD = b.psBig[:, 512:1024]
    b.psE = b.psBig[:, 1024:1536]
    b.psF = b.psBig[:, 1536:2048]

    b.rx = b.ring("rx", 4)
    b.rs = b.ring("rs", 4)
    b.rw = b.ring("rw", 4)
    b.ri = b.ring("ri", 8)
    b.rk4 = b.ring("rk4", 4)
    b.rv4 = b.ring("rv4", 4)
    b.rk = b.ring("rk", 2)
    b.rv = b.ring("rv", 2)
    b.rq = b.ring("rq", 2)
    s_ps = b.sem("s_ps")

    def g_init(g):
        g.memset(b.ones[:], 1.0)
        g.memset(b.onesrow[:], 1.0)
        g.memset(b.vec[:, 62:63], EPS_LN)
        inst = g.memset(b.vec[:, 63:64], 1e-6)
        s_ps.inc(inst)

    b.on("gpsimd", g_init)
    b.init_done = s_ps.v


class WCache:

    SUB = 16

    def __init__(self, b: Bld):
        self.b = b
        self.gsem = {}
        self.group_done = {}
        self.counts = {}

    def fill(self, items, gate=None):
        ords = dict(self.counts)
        plan = []
        for g, srcap, dstap in items:
            o = ords.get(g, 0)
            ords[g] = o + 1
            sub = o // self.SUB
            key = (g, sub)
            if key not in self.gsem:
                self.gsem[key] = self.b.sem(f"wc_{g}_{sub}")
            plan.append((key, srcap, dstap))
        self.counts.update(ords)

        def gfn(gp):
            if gate is not None:
                gp.wait_ge(gate[0].h, gate[1])
            for key, srcap, dstap in plan:
                inst = gp.dma_start(dstap, srcap)
                self.gsem[key].inc(inst, 16)
                self.group_done[key] = self.gsem[key].v
                self.group_done[key[0]] = None
        self.b.on("gpsimd", gfn)

    def wait(self, eng, grp, ordinal=None):
        if ordinal is not None:
            key = (grp, ordinal // self.SUB)
            eng.wait_ge(self.gsem[key].h, self.group_done[key])
        else:
            n = self.counts[grp]
            for sub in range((n + self.SUB - 1) // self.SUB):
                key = (grp, sub)
                eng.wait_ge(self.gsem[key].h, self.group_done[key])



def emit_ln(b: Bld, x_dram, xbase_dram=None, xscale=0.5, xn_dram=None):
    s_pe, s_dv, s_ac, s_ps = (b.sem(n) for n in ("s_pe", "s_dv", "s_ac", "s_ps"))
    s_xn = b.sem("s_xn") if xn_dram is not None else None
    mean = b.stat[:, 0:2048]
    rstd = b.pp[:, :].bitcast(F32)[:, 0:2048]
    m2buf = b.aux[:, 8192:16384].bitcast(F32)[:, 0:2048]

    def slot(k):
        return b.xtmp[:, (k % 2) * 2048:(k % 2 + 1) * 2048]

    def xb_ap(k):
        return b.aux[:, (k % 2) * 2048:(k % 2 + 1) * 2048]

    def x2_ap(k):
        return b.aux[:, 4096 + (k % 2) * 2048: 4096 + (k % 2 + 1) * 2048]

    prev_norm = getattr(b, '_ln_prev_norm', None)
    prev_store = getattr(b, '_ln_prev_store', None)
    ld1 = {}
    cast1 = {}
    sq1 = {}
    mm1 = {}
    for k in range(NK):
        def sy1(sy, k=k):
            if k >= 2:
                sy.wait_ge(s_dv.h, cast1[k - 2])
                sy.wait_ge(s_ac.h, sq1[k - 2])
            else:
                if prev_norm is not None:
                    sy.wait_ge(s_dv.h, prev_norm)
                if prev_store is not None:
                    for p in range(4):
                        sy.wait_ge(b.rs[p].h, prev_store[p])
            inst = sy.dma_start(slot(k), x_dram[k * 128:(k + 1) * 128, :])
            ld1[k] = b.rx[k % 2].inc(inst, 16)
        b.on("sync", sy1)

        def dv1(v, k=k):
            v.wait_ge(b.rx[k % 2].h, ld1[k])
            if k == 0 and prev_norm is not None:
                v.wait_ge(s_pe.h, b._ln_prev_pe) if hasattr(b, '_ln_prev_pe') else None
            inst = v.tensor_copy(xb_ap(k), slot(k))
            cast1[k] = s_dv.inc(inst)
        b.on("vector", dv1)

        def ac1(s, k=k):
            s.wait_ge(b.rx[k % 2].h, ld1[k])
            inst = s.activation(x2_ap(k), slot(k), AF.Square)
            sq1[k] = s_ac.inc(inst)
        b.on("scalar", ac1)

        def pe1(t, k=k):
            if k == 0:
                t.wait_ge(s_ps.h, b.init_done)
            t.wait_ge(s_dv.h, cast1[k])
            t.wait_ge(s_ac.h, sq1[k])
            for tc in range(NTC):
                t.matmul(b.psBig[:, tc * TL:(tc + 1) * TL], b.ones[:],
                         xb_ap(k)[:, tc * TL:(tc + 1) * TL],
                         start=(k == 0), stop=(k == NK - 1))
                psq = (b.psA if tc < 2 else b.psB)[:, (tc % 2) * TL:(tc % 2) * TL + TL]
                inst = t.matmul(psq, b.ones[:],
                                x2_ap(k)[:, tc * TL:(tc + 1) * TL],
                                start=(k == 0), stop=(k == NK - 1))
            s_pe.inc(inst)
            mm1[k] = s_pe.v
        b.on("tensor", pe1)
    b.ln_aux_free = mm1[NK - 1]

    def dv_stat(v):
        v.wait_ge(s_pe.h, mm1[NK - 1])
        v.tensor_scalar_mul(mean, b.psBig[:, 0:2048], 1.0 / D)
        v.tensor_scalar_mul(rstd[:, 0:1024], b.psA[:, 0:1024], 1.0 / D)
        inst = v.tensor_scalar_mul(rstd[:, 1024:2048], b.psB[:, 0:1024], 1.0 / D)
        b.fence(v, inst)
        inst = v.tensor_tensor(m2buf, mean, mean, ALU.mult)
        b.fence(v, inst)
        inst = v.tensor_tensor(rstd, rstd, m2buf, ALU.subtract)
        s_dv.inc(inst)
    b.on("vector", dv_stat)
    vard = s_dv.v
    b.ln_psCD_free = vard

    def ac_rstd(s):
        s.wait_ge(s_dv.h, vard)
        inst = s.activation(rstd, rstd, AF.Ln, bias=b.vec[:, 62:63])
        b.fence(s, inst)
        inst = s.activation(rstd, rstd, AF.Exp, scale=-0.5)
        s_ac.inc(inst)
    b.on("scalar", ac_rstd)
    rstdd = s_ac.v

    ld2 = {}
    norm2 = {}
    st2 = {}
    stn = {}
    for k in range(NK):
        def sy2(sy, k=k):
            if k >= 2:
                sy.wait_ge(s_dv.h, norm2[k - 2])
                if xbase_dram is not None:
                    sy.wait_ge(b.rs[k % 2].h, st2[k - 2])
            else:
                sy.wait_ge(s_dv.h, cast1[NK - 1])
                sy.wait_ge(s_ac.h, sq1[NK - 1])
            inst = sy.dma_start(slot(k), x_dram[k * 128:(k + 1) * 128, :])
            ld2[k] = b.rx[k % 2].inc(inst, 16)
        b.on("sync", sy2)

        def dv2(v, k=k):
            if k == 0:
                v.wait_ge(s_ac.h, rstdd)
            v.wait_ge(b.rx[k % 2].h, ld2[k])
            xnk = b.xn[:, k * 2048:(k + 1) * 2048]
            inst = v.tensor_tensor(m2buf, slot(k), mean, ALU.subtract)
            b.fence(v, inst)
            inst = v.tensor_tensor(xnk, m2buf, rstd, ALU.mult)
            if xbase_dram is not None:
                b.fence(v, inst)
                inst = v.scalar_tensor_tensor(slot(k), xnk, xscale, slot(k),
                                              ALU.mult, ALU.add)
            norm2[k] = s_dv.inc(inst)
        b.on("vector", dv2)

        def sy2b(sy, k=k):
            sy.wait_ge(s_dv.h, norm2[k])
            if xbase_dram is not None:
                inst = sy.dma_start(xbase_dram[k * 128:(k + 1) * 128, :], slot(k))
                st2[k] = b.rs[k % 2].inc(inst, 16)
            if xn_dram is not None:
                inst = sy.dma_start(xn_dram[k * 128:(k + 1) * 128, :],
                                    b.xn[:, k * 2048:(k + 1) * 2048])
                s_xn.inc(inst, 16)
        if xbase_dram is not None or xn_dram is not None:
            b.on("sync", sy2b)

    b.ln_norm_done = norm2[NK - 1]
    b.ln_xtmp_free = norm2[NK - 1]
    b.ln_store_ring = ({p: (b.rs[p % 2], st2[14 + p % 2]) for p in range(4)}
                       if xbase_dram is not None else None)
    b._ln_prev_norm = None
    b._ln_prev_store = None



def emit_l0_ffn(b: Bld, dr, wc: WCache, dbg=None):
    s_x, s_pe, s_dv, s_ac, s_st = (b.sem(n) for n in
                                   ("s_x", "s_pe", "s_dv", "s_ac", "s_st"))
    s_k = b.sem("s_k")
    s_g = b.sem("s_g")

    u_all = b.stat[:, 0:64]
    cnt = b.stat[:, 64:128]
    wu = b.stat[:, 128:192]
    s2 = b.stat[:, 192:208]
    r2 = b.stat[:, 208:224]
    wfin = b.stat[:, 256:320]

    def g_small(g):
        g.wait_ge(b.sem("s_dv").h, b.ln_norm_done)
        inst = g.dma_start(b.gwb[:, :].rearrange("p (k c) -> p k c", k=16),
                           dr['gwt'].rearrange("k p c -> p k c"))
        s_g.inc(inst, 16)
        inst = g.dma_start(b.b1v[:, 0:64], dr['b1r'])
        s_g.inc(inst, 16)
        inst = g.dma_start(b.pp[0:1, 0:4096], dr['b2row'][0:1, 0:4096])
        s_g.inc(inst, 16)
        inst = g.dma_start(b.io[0:1, 0:4096], dr['b2row'][0:1, 4096:8192])
        s_g.inc(inst, 16)
        inst = g.dma_start(b.idf[:, :], dr['idf'])
        s_g.inc(inst, 16)
    b.on("gpsimd", g_small)
    g_done = s_g.v

    def pe_router(t):
        t.wait_ge(s_g.h, g_done)
        t.wait_ge(s_dv.h, b.ln_norm_done)
        for tt in range(NTT):
            for k in range(NK):
                inst = t.matmul(b.psE[:, tt * 4:(tt + 1) * 4],
                                b.xn[:, k * 2048 + tt * 128: k * 2048 + (tt + 1) * 128],
                                b.gwb[:, k * 4:(k + 1) * 4],
                                start=(k == 0), stop=(k == NK - 1))
        s_pe.inc(inst)
    b.on("tensor", pe_router)
    rlogit = s_pe.v

    def dv_rmax(v):
        v.wait_ge(s_pe.h, rlogit)
        for tt in range(NTT):
            inst = v.reduce_max(b.vec[:, tt:tt + 1], b.psE[:, tt * 4:(tt + 1) * 4], axis=AX.X)
        b.fence(v, inst)
        inst = v.tensor_scalar_mul(b.vec[:, 0:16], b.vec[:, 0:16], -1.0)
        s_dv.inc(inst)
    b.on("vector", dv_rmax)
    rmax = s_dv.v

    def ac_uexp(s):
        s.wait_ge(s_dv.h, rmax)
        for tt in range(NTT):
            inst = s.activation(u_all[:, tt * 4:(tt + 1) * 4],
                                b.psE[:, tt * 4:(tt + 1) * 4], AF.Exp,
                                bias=b.vec[:, tt:tt + 1])
        s_ac.inc(inst)
    b.on("scalar", ac_uexp)
    uexp = s_ac.v

    gts = b.stat[:, 320:512]
    te = b.stat[:, 512:576]

    def dv_top2(v):
        v.wait_ge(s_ac.h, uexp)
        gi = 0
        for e in range(4):
            for j in range(4):
                if j == e:
                    continue
                inst = v.tensor_tensor(gts[:, gi * 16:(gi + 1) * 16],
                                       u_all[:, j::4], u_all[:, e::4], ALU.is_gt)
                gi += 1
        b.fence(v, inst)
        for e in range(4):
            inst = v.tensor_tensor(te[:, e * 16:(e + 1) * 16],
                                   gts[:, (3 * e) * 16:(3 * e + 1) * 16],
                                   gts[:, (3 * e + 1) * 16:(3 * e + 2) * 16], ALU.add)
        b.fence(v, inst)
        for e in range(4):
            inst = v.tensor_tensor(cnt[:, e::4], te[:, e * 16:(e + 1) * 16],
                                   gts[:, (3 * e + 2) * 16:(3 * e + 3) * 16], ALU.add)
        b.fence(v, inst)
        inst = v.tensor_scalar(cnt[:, :], cnt[:, :], 2.0, None, ALU.is_lt)
        b.fence(v, inst)
        inst = v.tensor_tensor(wu[:, :], cnt[:, :], u_all[:, :], ALU.mult)
        b.fence(v, inst)
        v.tensor_tensor(s2, wu[:, 0::4], wu[:, 1::4], ALU.add)
        inst = v.tensor_tensor(r2, wu[:, 2::4], wu[:, 3::4], ALU.add)
        b.fence(v, inst)
        inst = v.tensor_tensor(s2, s2, r2, ALU.add)
        b.fence(v, inst)
        inst = v.reciprocal(r2, s2)
        b.fence(v, inst)
        for e in range(4):
            inst = v.tensor_tensor(wfin[:, e::4], wu[:, e::4], r2, ALU.mult)
        s_dv.inc(inst)
    b.on("vector", dv_top2)
    wfin_done = s_dv.v
    if dbg is not None:
        def sy_dbg2(sy):
            sy.wait_ge(s_dv.h, wfin_done)
            inst = sy.dma_start(dbg['dbg_wf'], b.stat[:, 0:512])
            s_st.inc(inst, 16)
        b.on("sync", sy_dbg2)

    rowcp = {}
    prev = None
    for e in range(4):
        for grp in range(4):
            def pe_tr(t, e=e, grp=grp, prev=prev):
                if prev is None:
                    t.wait_ge(s_dv.h, wfin_done)
                else:
                    t.wait_ge(s_dv.h, prev)
                for q in range(4):
                    tt = grp * 4 + q
                    inst = t.transpose(b.psF[0:1, q * 128:(q + 1) * 128],
                                       wfin[:, tt * 4 + e: tt * 4 + e + 1],
                                       b.idf[:, :])
                s_pe.inc(inst)
            b.on("tensor", pe_tr)
            trd = s_pe.v

            def dv_row(v, e=e, grp=grp, trd=trd):
                v.wait_ge(s_pe.h, trd)
                inst = v.tensor_copy(
                    b.wrow[:, e * 2048 + grp * TL: e * 2048 + (grp + 1) * TL],
                    b.psF[0:1, 0:TL])
                s_dv.inc(inst)
            b.on("vector", dv_row)
            prev = s_dv.v
    wrow_done = prev
    if dbg is not None:
        def sy_dbg3(sy):
            sy.wait_ge(s_dv.h, wrow_done)
            inst = sy.dma_start(dbg['dbg_wr'], b.wrow[:, :])
            s_st.inc(inst, 16)
        b.on("sync", sy_dbg3)

    slot_seq = []
    for tc in range(NTC):
        for e in range(4):
            for m in range(NK):
                slot_seq.append(('w1', (tc, e, m)))
        for o in range(NK):
            for e in range(4):
                slot_seq.append(('w2', (tc, o, e)))
    grp_done_seq = []
    w1ld = {}
    w2ld = {}
    NSLOT = 4
    state = {'ld': 0}

    def emit_wload(upto):
        lo, hi = state['ld'], min(upto, len(slot_seq))
        if hi <= lo:
            return
        state['ld'] = hi

        def sc(s):
            for i in range(lo, hi):
                kind, key = slot_seq[i]
                if i >= NSLOT:
                    s.wait_ge(s_pe.h, grp_done_seq[i - NSLOT])
                if kind == 'w1':
                    tc, e, m = key
                    src = dr['w1c'][e, m]
                    wc.wait(s, dr.get('_pfx', '') + 'w1', e * 16 + m)
                else:
                    tc, o, e = key
                    src = dr['w2c'][o, e]
                    wc.wait(s, dr.get('_pfx', '') + 'w2', o * 4 + e)
                dst = b.wring[:, (i % NSLOT) * 2048:(i % NSLOT + 1) * 2048]
                inst = s.dma_start(dst.rearrange("p (k c) -> p k c", k=16), src)
                if kind == 'w1':
                    w1ld[key] = b.rw[i % NSLOT].inc(inst, 16)
                else:
                    w2ld[key] = b.rw[i % NSLOT].inc(inst, 16)
        b.on("scalar", sc)

    def h1s_ap(e, m):
        c = (e * 16 + m) * TL
        return b.aux[:, c:c + TL]

    si = 0
    gelu_done = {}
    epi_dv = {}
    epi_st = {}
    for tc in range(NTC):
        wbc_prev = {}
        for e in range(4):
            pstile = b.psC if e % 2 == 0 else b.psD

            def pe_wbc(t, tc=tc, e=e, pstile=pstile):
                if tc == 0 and e == 0:
                    t.wait_ge(s_dv.h, wrow_done)
                    t.wait_ge(s_dv.h, b.ln_psCD_free)
                if e >= 2:
                    t.wait_ge(s_dv.h, wbc_prev[e - 2])
                elif tc > 0:
                    t.wait_ge(s_dv.h, wbc_prev_tc[2 + e])
                inst = t.matmul(pstile, b.onesrow[0:1, :],
                                b.wrow[:, e * 2048 + tc * TL: e * 2048 + tc * TL + TL],
                                start=True, stop=True)
                s_pe.inc(inst)
            b.on("tensor", pe_wbc)
            mmv = s_pe.v

            def dv_wbc(v, tc=tc, e=e, pstile=pstile, mmv=mmv):
                v.wait_ge(s_pe.h, mmv)
                inst = v.tensor_copy(b.wbc[:, e * TL:(e + 1) * TL], pstile[:])
                s_dv.inc(inst)
            b.on("vector", dv_wbc)
            wbc_prev[e] = s_dv.v
        wbc_prev_tc = wbc_prev
        wbc_done = s_dv.v

        for e in range(4):
            for m in range(NK):
                emit_wload(si + NSLOT)
                pst = b.psA[:, (si % 2) * TL:(si % 2) * TL + TL]

                def pe_h1(t, tc=tc, e=e, m=m, si=si, pst=pst, wbc_done=wbc_done):
                    t.wait_ge(b.rw[si % NSLOT].h, w1ld[(tc, e, m)])
                    if e == 0 and m == 0:
                        t.wait_ge(s_dv.h, wbc_done)
                    if (si - 2) in gelu_done:
                        t.wait_ge(s_ac.h, gelu_done[si - 2])
                    for k in range(NK):
                        inst = t.matmul(
                            pst,
                            b.wring[:, (si % NSLOT) * 2048 + k * 128:(si % NSLOT) * 2048 + (k + 1) * 128],
                            b.xn[:, k * 2048 + tc * TL: k * 2048 + (tc + 1) * TL],
                            start=(k == 0), stop=(k == NK - 1))
                    s_pe.inc(inst)
                b.on("tensor", pe_h1)
                grp_done_seq.append(s_pe.v)
                mmv = s_pe.v

                def ac_gelu(s, tc=tc, e=e, m=m, mmv=mmv, pst=pst):
                    s.wait_ge(s_pe.h, mmv)
                    inst = s.activation(h1s_ap(e, m), pst, AF.Gelu,
                                        bias=b.b1v[:, e * 16 + m: e * 16 + m + 1])
                    s_ac.inc(inst)
                b.on("scalar", ac_gelu)
                gelu_done[si] = s_ac.v

                def dv_scale(v, e=e, m=m, gv=s_ac.v, tc=tc):
                    v.wait_ge(s_ac.h, gv)
                    inst = v.tensor_tensor(h1s_ap(e, m), h1s_ap(e, m),
                                           b.wbc[:, e * TL:(e + 1) * TL], ALU.mult)
                    s_dv.inc(inst)
                b.on("vector", dv_scale)
                si += 1
        h1_all = s_dv.v
        if dbg is not None and tc == 0:
            def sy_dbg4(sy, h1_all=h1_all):
                sy.wait_ge(s_dv.h, h1_all)
                inst = sy.dma_start(dbg['dbg_h1'], b.aux[:, :])
                s_st.inc(inst, 16)
            b.on("sync", sy_dbg4)

        for o in range(NK):
            pso = b.psB[:, (o % 2) * TL:(o % 2) * TL + TL]

            def sy_xo(sy, tc=tc, o=o):
                po = (tc * NK + o) - 4
                if po < 0:
                    sy.wait_ge(s_dv.h, b.ln_xtmp_free)
                else:
                    sy.wait_ge(s_dv.h, epi_dv[po])
                    sy.wait_ge(b.rs[o % 4].h, epi_st[po])
                inst = sy.dma_start(
                    b.xtmp[:, (o % 4) * TL:(o % 4 + 1) * TL],
                    dr['xin'][o * 128:(o + 1) * 128, tc * TL:(tc + 1) * TL])
                b.rx[o % 4].inc(inst, 16)
            b.on("sync", sy_xo)
            xo_ld = b.rx[o % 4].v

            for e in range(4):
                emit_wload(si + NSLOT)

                def pe_out(t, tc=tc, o=o, e=e, si=si, pso=pso, h1_all=h1_all):
                    t.wait_ge(b.rw[si % NSLOT].h, w2ld[(tc, o, e)])
                    if e == 0:
                        t.wait_ge(s_dv.h, h1_all)
                        if o >= 2:
                            t.wait_ge(s_dv.h, epi_dv[tc * NK + o - 2])
                    for m in range(NK):
                        inst = t.matmul(
                            pso,
                            b.wring[:, (si % NSLOT) * 2048 + m * 128:(si % NSLOT) * 2048 + (m + 1) * 128],
                            h1s_ap(e, m),
                            start=(e == 0 and m == 0), stop=False)
                    if e == 3:
                        for e2 in range(4):
                            b2src = b.pp if e2 < 2 else b.io
                            inst = t.matmul(
                                pso,
                                b2src[0:1, (e2 % 2) * 2048 + o * 128: (e2 % 2) * 2048 + (o + 1) * 128],
                                b.wrow[:, e2 * 2048 + tc * TL: e2 * 2048 + tc * TL + TL],
                                start=False, stop=(e2 == 3))
                    s_pe.inc(inst)
                b.on("tensor", pe_out)
                grp_done_seq.append(s_pe.v)
                si += 1
            mmo = s_pe.v

            def dv_epi(v, tc=tc, o=o, mmo=mmo, pso=pso, xo_ld=xo_ld):
                v.wait_ge(s_pe.h, mmo)
                v.wait_ge(b.rx[o % 4].h, xo_ld)
                inst = v.scalar_tensor_tensor(
                    b.xtmp[:, (o % 4) * TL:(o % 4 + 1) * TL], pso, 0.5,
                    b.xtmp[:, (o % 4) * TL:(o % 4 + 1) * TL], ALU.mult, ALU.add)
                s_dv.inc(inst)
            b.on("vector", dv_epi)
            epi_dv[tc * NK + o] = s_dv.v

            def sy_st(sy, tc=tc, o=o, ev=s_dv.v):
                sy.wait_ge(s_dv.h, ev)
                inst = sy.dma_start(
                    dr['xout'][o * 128:(o + 1) * 128, tc * TL:(tc + 1) * TL],
                    b.xtmp[:, (o % 4) * TL:(o % 4 + 1) * TL])
                b.rs[o % 4].inc(inst, 16)
            b.on("sync", sy_st)
            epi_st[tc * NK + o] = b.rs[o % 4].v

    b.final_store = s_st.v
    b.final_rs = [r.v for r in b.rs]



def decl_l0_ffn(nc, pfx=""):
    dr = {}
    dr['gwt'] = nc.dram_tensor(pfx + "gwt", [16, 128, 4], F32, kind="ExternalInput").ap()
    dr['w1f'] = nc.dram_tensor(pfx + "w1f", [4, 16, 128, 16, 128], F32, kind="ExternalInput").ap()
    dr['w2f'] = nc.dram_tensor(pfx + "w2f", [16, 4, 128, 16, 128], F32, kind="ExternalInput").ap()
    dr['b1r'] = nc.dram_tensor(pfx + "b1r", [128, 64], F32, kind="ExternalInput").ap()
    dr['b2row'] = nc.dram_tensor(pfx + "b2row", [1, 8192], F32, kind="ExternalInput").ap()
    dr['idf'] = nc.dram_tensor(pfx + "idf", [128, 128], F32, kind="ExternalInput").ap()
    dr['w1c'] = nc.dram_tensor(pfx + "w1c", [4, 16, 128, 16, 128], BF16).ap()
    dr['w2c'] = nc.dram_tensor(pfx + "w2c", [16, 4, 128, 16, 128], BF16).ap()
    dr['_pfx'] = pfx
    return dr


def fill_l0_ffn(dr, pfx=""):
    items = []
    for e in range(4):
        for m in range(NK):
            items.append((pfx + 'w1', dr['w1f'][e, m], dr['w1c'][e, m]))
    for o in range(NK):
        for e in range(4):
            items.append((pfx + 'w2', dr['w2f'][o, e], dr['w2c'][o, e]))
    return items


def emit_sub_l0_ffn(b, dr, wc, xin, xout):
    dr = dict(dr)
    dr['xin'] = xin
    dr['xout'] = xout
    emit_ln(b, xin, xbase_dram=None)
    emit_l0_ffn(b, dr, wc)
    return


def build_l0_ffn_nc(debug=False):
    b = Bld("l0ffn")
    nc = b.nc
    dr = {}
    if debug:
        dr['dbg_xn'] = nc.dram_tensor("dbg_xn", [128, 32768], BF16, kind="ExternalOutput").ap()
        dr['dbg_wf'] = nc.dram_tensor("dbg_wf", [128, 512], F32, kind="ExternalOutput").ap()
        dr['dbg_wr'] = nc.dram_tensor("dbg_wr", [1, 8192], BF16, kind="ExternalOutput").ap()
        dr['dbg_h1'] = nc.dram_tensor("dbg_h1", [128, 32768], BF16, kind="ExternalOutput").ap()
    dr['xin'] = nc.dram_tensor("xin", [D, 2048], F32, kind="ExternalInput").ap()
    dr['xout'] = nc.dram_tensor("xout", [D, 2048], F32, kind="ExternalOutput").ap()
    dr.update(decl_l0_ffn(nc))

    b.start()
    alloc_common(b)

    wc = WCache(b)
    wc.fill(fill_l0_ffn(dr))

    emit_ln(b, dr['xin'], xbase_dram=None)
    if debug:
        s_dv = b.sem("s_dv"); s_st = b.sem("s_st"); s_x = b.sem("s_x")
        nd = b.ln_norm_done
        def sy_dbg1(sy):
            sy.wait_ge(s_dv.h, nd)
            inst = sy.dma_start(dr['dbg_xn'], b.xn[:, :])
            s_st.inc(inst, 16)
        b.on("sync", sy_dbg1)
    emit_l0_ffn(b, dr, wc, dbg=dr if debug else None)

    s_st = b.sem("s_st")
    fin = b.final_store
    frs = b.final_rs

    def sy_fin(sy):
        sy.wait_ge(s_st.h, fin)
        for p in range(4):
            sy.wait_ge(b.rs[p].h, frs[p])
    b.on("sync", sy_fin)

    return b.finish()


def prep_l0_ffn_inputs(inputs, xs):
    w1 = np.asarray(inputs['L0_ffn_w1'])
    w2 = np.asarray(inputs['L0_ffn_w2'])
    gw = np.asarray(inputs['L0_ffn_gate_w'])
    b1 = np.asarray(inputs['L0_ffn_b1'])
    b2 = np.asarray(inputs['L0_ffn_b2'])
    w1f = np.stack([_tile_w(w1[e]) for e in range(4)])
    w2t = np.stack([_tile_w(w2[e]) for e in range(4)])
    w2f = np.ascontiguousarray(w2t.transpose(1, 0, 2, 3, 4))
    gwt = np.ascontiguousarray(gw.T.reshape(16, 128, 4)).astype(np.float32)
    b1r = np.ascontiguousarray(
        b1.reshape(4, 16, 128).transpose(2, 0, 1).reshape(128, 64)).astype(np.float32)
    b2row = np.ascontiguousarray(b2.reshape(1, 8192)).astype(np.float32)
    idf = np.eye(128, dtype=np.float32)
    return [{
        "xin": xs[c], "gwt": gwt, "w1f": w1f, "w2f": w2f,
        "b1r": b1r, "b2row": b2row, "idf": idf,
    } for c in range(8)]



def _shard_x(x0):
    xs = []
    x0 = np.asarray(x0)
    for c in range(8):
        bi, h = divmod(c, 2)
        idx = _core_token_idx(h)
        xs.append(np.ascontiguousarray(x0[bi][idx, :].T.astype(np.float32)))
    return xs


def _unshard_x(xs):
    out = np.zeros((4, 4096, 2048), dtype=np.float32)
    for c in range(8):
        bi, h = divmod(c, 2)
        idx = _core_token_idx(h)
        out[bi][idx, :] = xs[c].T
    return out


def kernel_unfused(**inputs):
    xs = _shard_x(inputs['L0_attn_query'])
    steps = [
        (build_l0_ffn_nc, prep_l0_ffn_inputs),
        (build_l0_attn_nc, prep_l0_attn_inputs),
        (build_l0_ffn_nc, prep_l0_ffn_inputs),
        (build_l1_ffn_nc, prep_l1_ffn_inputs),
        (build_l1_attn_nc, prep_l1_attn_inputs),
        (build_l1_ffn_nc, prep_l1_ffn_inputs),
    ]
    for bf, pf in steps:
        nc = bf()
        maps = pf(inputs, xs)
        res = run_bass_kernel_spmd(nc, maps, core_ids=list(range(8)))
        xs = [np.asarray(res.results[c]["xout"]) for c in range(8)]
    return _unshard_x(xs)



class WRing:
    def __init__(self, b: Bld, wc: WCache):
        self.b = b
        self.wc = wc
        self.q = []
        self.ld = {}
        self.grp_done = []
        self.emitted = 0
        self.pe_base = b.sem("s_pe").v

    def push(self, group, src_ap, ford=None):
        self.q.append((group, src_ap, ford))
        return len(self.q) - 1

    def ensure(self, upto):
        lo, hi = self.emitted, min(upto, len(self.q))
        if hi <= lo:
            return
        self.emitted = hi
        b = self.b

        def mk(queue_par):
            def sec(s):
                for i in range(lo, hi):
                    if i % 2 != queue_par:
                        continue
                    grp, src, ford = self.q[i]
                    if i >= 4:
                        s.wait_ge(b.sem("s_pe").h, self.grp_done[i - 4])
                    else:
                        s.wait_ge(b.sem("s_pe").h, self.pe_base)
                    self.wc.wait(s, grp, ford)
                    dst = b.wring[:, (i % 4) * 2048:(i % 4 + 1) * 2048]
                    inst = s.dma_start(dst.rearrange("p (k c) -> p k c", k=16), src)
                    self.ld[i] = b.rw[i % 4].inc(inst, 16)
            return sec
        b.on("scalar", mk(0))
        b.on("gpsimd", mk(1))

    def wait(self, t, i):
        t.wait_ge(self.b.rw[i % 4].h, self.ld[i])

    def slot(self, i, k):
        return self.b.wring[:, (i % 4) * 2048 + k * 128:(i % 4) * 2048 + (k + 1) * 128]

    def consumed(self, v):
        self.grp_done.append(v)



def emit_proj_fm(b: Bld, wr: WRing, wgroup, wsrcs, rhs_fn, dst_fn,
                 bias_col=None, norm_col=None, pe_extra_wait=None):
    s_pe, s_dv, s_ac = b.sem("s_pe"), b.sem("s_dv"), b.sem("s_ac")
    ri_base = [r.v for r in b.ri]
    idxs = [wr.push(wgroup, wsrcs[m], ford=m) for m in range(16)]
    evac_done = {}
    sq_used = {}
    norm_red = {}
    io_store = {}
    sq_norm_mm = {}

    for m in range(16):
        wr.ensure(idxs[m] + 4)
        for tc in range(NTC):
            g = m * 4 + tc
            half = b.psA[:, (g % 2) * TL:(g % 2) * TL + TL]

            def pe_mm(t, m=m, tc=tc, g=g, half=half, idx=idxs[m]):
                if g == 0 and pe_extra_wait is not None:
                    for sem, val in pe_extra_wait:
                        t.wait_ge(sem.h, val)
                wr.wait(t, idx)
                if g >= 2:
                    t.wait_ge(s_dv.h, evac_done[g - 2])
                if norm_col is not None and tc == 0 and m >= 1:
                    t.wait_ge(s_dv.h, norm_red[m - 1])
                for k in range(NK):
                    inst = t.matmul(half, wr.slot(idxs[m], k), rhs_fn(k, tc),
                                    start=(k == 0), stop=(k == NK - 1))
                s_pe.inc(inst)
            b.on("tensor", pe_mm)
            mmv = s_pe.v
            if tc == 3:
                wr.consumed(mmv)

            kind, dap = dst_fn(m, tc)
            if kind == 's':
                def dv_ev(v, m=m, tc=tc, half=half, mmv=mmv, dap=dap):
                    v.wait_ge(s_pe.h, mmv)
                    if bias_col is not None:
                        inst = v.tensor_scalar(dap, half, bias_col(m), None, ALU.add)
                    else:
                        inst = v.tensor_copy(dap, half)
                    s_dv.inc(inst)
                b.on("vector", dv_ev)
                evac_done[g] = s_dv.v
                ev_ap = dap
            else:
                slot = b.io[:, (g % 8) * TL:(g % 8 + 1) * TL]

                def dv_ev(v, m=m, tc=tc, half=half, mmv=mmv, slot=slot, g=g):
                    v.wait_ge(s_pe.h, mmv)
                    if g >= 8:
                        v.wait_ge(b.ri[g % 8].h, io_store[g - 8])
                        if norm_col is not None:
                            v.wait_ge(s_ac.h, sq_used[g - 8])
                    else:
                        v.wait_ge(b.ri[g % 8].h, ri_base[g % 8])
                    if bias_col is not None:
                        inst = v.tensor_scalar(slot, half, bias_col(m), None, ALU.add)
                    else:
                        inst = v.tensor_copy(slot, half)
                    s_dv.inc(inst)
                b.on("vector", dv_ev)
                evac_done[g] = s_dv.v

                def sy_st(sy, slot=slot, dap=dap, g=g, ev=s_dv.v):
                    sy.wait_ge(s_dv.h, ev)
                    inst = sy.dma_start(dap, slot)
                    io_store[g] = b.ri[g % 8].inc(inst, 16)
                b.on("sync", sy_st)
                ev_ap = slot

            if norm_col is not None:
                sqslot = b.pp[:, 2048 + (g % 2) * TL: 2048 + (g % 2 + 1) * TL]

                def ac_sq(s, g=g, ev=evac_done[g], ev_ap=ev_ap, sqslot=sqslot):
                    s.wait_ge(s_dv.h, ev)
                    if g >= 2:
                        s.wait_ge(s_pe.h, sq_norm_mm[g - 2])
                    inst = s.activation(sqslot, ev_ap, AF.Square)
                    s_ac.inc(inst)
                b.on("scalar", ac_sq)
                sq_used[g] = s_ac.v
                sqv = s_ac.v

                def pe_nm(t, tc=tc, sqv=sqv, sqslot=sqslot):
                    t.wait_ge(s_ac.h, sqv)
                    inst = t.matmul(b.psBig[:, tc * TL:(tc + 1) * TL], b.ones[:],
                                    sqslot, start=True, stop=True)
                    s_pe.inc(inst)
                b.on("tensor", pe_nm)
                sq_norm_mm[g] = s_pe.v

                if tc == 3:
                    def dv_red(v, m=m, nv=s_pe.v):
                        v.wait_ge(s_pe.h, nv)
                        inst = v.reduce_max(norm_col(m), b.psBig[:, 0:2048], axis=AX.X)
                        b.fence(v, inst)
                        inst = v.tensor_scalar_mul(norm_col(m), norm_col(m), 1.0)
                        s_dv.inc(inst)
                    b.on("vector", dv_red)
                    norm_red[m] = s_dv.v
    b.proj_evac_done = evac_done[63]
    b.proj_io_store = dict(io_store)
    return evac_done



def emit_attn(b: Bld, qscr, kall, vall, dst_buf, masks_ready, gather_done, psum_free):
    s_pe, s_dv, s_ac, s_cc = (b.sem(n) for n in ("s_pe", "s_dv", "s_ac", "s_cc"))

    def dv_nc1(v):
        inst = v.tensor_tensor(b.vec[:, 32:48], b.vec[:, 0:16], b.vec[:, 16:32], ALU.mult)
        s_dv.inc(inst)
    b.on("vector", dv_nc1)
    nc1 = s_dv.v

    def ac_nc(s):
        s.wait_ge(s_dv.h, nc1)
        inst = s.activation(b.vec[:, 32:48], b.vec[:, 32:48], AF.Ln)
        b.fence(s, inst)
        inst = s.activation(b.vec[:, 32:48], b.vec[:, 32:48], AF.Exp, scale=0.5)
        s_ac.inc(inst)
    b.on("scalar", ac_nc)
    nc2 = s_ac.v

    def dv_nc2(v):
        v.wait_ge(s_ac.h, nc2)
        inst = v.tensor_scalar(b.vec[:, 32:48], b.vec[:, 32:48], -RS128, -10.0,
                               ALU.mult, ALU.add)
        s_dv.inc(inst)
    b.on("vector", dv_nc2)
    negc_done = s_dv.v

    kofs, vofs, qofs = 0, 8192, 16384
    head_done = {}
    kld = {}
    vld = {}
    qld = {}
    evac_rec = {}

    for h in range(NH):
        ks = h % 2

        def sy_k(sy, h=h, ks=ks):
            if h >= 2:
                sy.wait_ge(s_pe.h, head_done[h - 2])
            if h == 0:
                sy.wait_ge(s_cc.h, gather_done)
                sy.wait_ge(s_pe.h, b.ln_aux_free)
            for kt in range(32):
                r, j = kt % 2, kt // 2
                inst = sy.dma_start(
                    b.aux[:, kofs + ks * 4096 + kt * 128: kofs + ks * 4096 + (kt + 1) * 128],
                    kall[r * 2048 + h * 128:(r * 2048 + (h + 1) * 128), j * 128:(j + 1) * 128])
                b.rk[ks].inc(inst, 16)
            kld[h] = b.rk[ks].v
        b.on("sync", sy_k)

        def sc_v(s, h=h, ks=ks):
            if h >= 2:
                s.wait_ge(s_pe.h, head_done[h - 2])
            if h == 0:
                s.wait_ge(s_cc.h, gather_done)
                s.wait_ge(s_pe.h, b.ln_aux_free)
            for kt in range(32):
                r, j = kt % 2, kt // 2
                inst = s.dma_start(
                    b.aux[:, vofs + ks * 4096 + kt * 128: vofs + ks * 4096 + (kt + 1) * 128],
                    vall[r * 2048 + h * 128:(r * 2048 + (h + 1) * 128), j * 128:(j + 1) * 128],
                    transpose=True)
                b.rv[ks].inc(inst, 16)
            vld[h] = b.rv[ks].v
        b.on("scalar", sc_v)

        for jb in range(NTC):
            gq = (h * 4 + jb) % 2

            def sy_q(sy, h=h, jb=jb, gq=gq):
                gi = h * 4 + jb
                if gi >= 2:
                    sy.wait_ge(s_dv.h, evac_rec[gi - 2][0])
                inst = sy.dma_start(
                    b.aux[:, qofs + gq * TL: qofs + (gq + 1) * TL],
                    qscr[h * 128:(h + 1) * 128, jb * TL:(jb + 1) * TL])
                qld[(h, jb)] = b.rq[gq].inc(inst, 16)
            b.on("sync", sy_q)

            P = 4 * jb + 4
            exp_done = {}
            mask_done = {}
            spair_done = {}
            sgpv_done = {}

            for i in range(P + 1):
                def pe_kloop(t, h=h, jb=jb, i=i, P=P, gq=gq, ks=ks):
                    if i >= 1:
                        pi = i - 1
                        t.wait_ge(s_ac.h, exp_done[pi])
                        if pi in mask_done:
                            t.wait_ge(s_dv.h, mask_done[pi])
                        if i == 1:
                            gi = h * 4 + jb
                            if gi >= 1:
                                t.wait_ge(s_dv.h, evac_rec[gi - 1][1])
                        for z in range(2):
                            kt = 2 * pi + z
                            pslot = b.pp[:, (pi % 2) * 1024 + z * TL:(pi % 2) * 1024 + (z + 1) * TL]
                            vt = b.aux[:, vofs + ks * 4096 + kt * 128: vofs + ks * 4096 + (kt + 1) * 128]
                            t.matmul(b.psC, vt, pslot,
                                     start=(pi == 0 and z == 0), stop=(pi == P - 1 and z == 1))
                            inst = t.matmul(b.psD, b.ones[:], pslot,
                                            start=(pi == 0 and z == 0), stop=(pi == P - 1 and z == 1))
                        s_pe.inc(inst)
                        sgpv_done[pi] = s_pe.v
                    if i < P:
                        if i == 0:
                            t.wait_ge(b.rq[gq].h, qld[(h, jb)])
                            if jb == 0:
                                if h == 0:
                                    t.wait_ge(s_dv.h, psum_free)
                                t.wait_ge(b.rk[ks].h, kld[h])
                                t.wait_ge(b.rv[ks].h, vld[h])
                        if i >= 2:
                            t.wait_ge(s_ac.h, exp_done[i - 2])
                        ps = b.psA if i % 2 == 0 else b.psB
                        for z in range(2):
                            kt = 2 * i + z
                            ktile = b.aux[:, kofs + ks * 4096 + kt * 128: kofs + ks * 4096 + (kt + 1) * 128]
                            inst = t.matmul(ps[:, z * TL:(z + 1) * TL], ktile,
                                            b.aux[:, qofs + gq * TL: qofs + (gq + 1) * TL],
                                            start=True, stop=True)
                        s_pe.inc(inst)
                        spair_done[i] = s_pe.v
                b.on("tensor", pe_kloop)

                if i < P:
                    def ac_exp(s, h=h, jb=jb, i=i, P=P):
                        s.wait_ge(s_pe.h, spair_done[i])
                        if i == 0 and h == 0 and jb == 0:
                            s.wait_ge(s_dv.h, negc_done)
                        if i >= 2:
                            t2 = sgpv_done.get(i - 2)
                            if t2 is not None:
                                s.wait_ge(s_pe.h, t2)
                        ps = b.psA if i % 2 == 0 else b.psB
                        inst = s.activation(b.pp[:, (i % 2) * 1024:(i % 2 + 1) * 1024],
                                            ps[:], AF.Exp, scale=RS128,
                                            bias=b.vec[:, 32 + h:33 + h])
                        s_ac.inc(inst)
                    b.on("scalar", ac_exp)
                    exp_done[i] = s_ac.v

                    smin = 8 * jb
                    kts = [2 * i, 2 * i + 1]
                    if kts[1] >= smin:
                        def dv_mask(v, h=h, jb=jb, i=i, kts=kts, smin=smin):
                            v.wait_ge(s_ac.h, exp_done[i])
                            if h == 0 and jb == 0:
                                v.wait_ge(b.sem("s_g").h, masks_ready)
                            inst = None
                            for z, kt in enumerate(kts):
                                if kt < smin:
                                    continue
                                sidx = kt - smin
                                pslot = b.pp[:, (i % 2) * 1024 + z * TL:(i % 2) * 1024 + (z + 1) * TL]
                                inst = v.tensor_tensor(pslot, pslot,
                                                       b.ovl[:, sidx * TL:(sidx + 1) * TL], ALU.mult)
                            s_dv.inc(inst)
                        b.on("vector", dv_mask)
                        mask_done[i] = s_dv.v

            def dv_evac(v, h=h, jb=jb, P=P):
                v.wait_ge(s_pe.h, sgpv_done[P - 1])
                inst = v.reciprocal(b.stat[:, 0:TL], b.psD)
                b.fence(v, inst)
                inst = v.tensor_tensor(
                    dst_buf[:, h * 2048 + jb * TL: h * 2048 + (jb + 1) * TL],
                    b.psC, b.stat[:, 0:TL], ALU.mult)
                s_dv.inc(inst)
            b.on("vector", dv_evac)
            evac_rec[h * 4 + jb] = (s_dv.v, s_dv.v)
        head_done[h] = s_pe.v

    b.attn_done = s_dv.v



def emit_outproj(b: Bld, wr: WRing, wgroup, wsrcs, rhs_buf, dr_xin, dr_xout,
                 bias_col=None, dscale=1.0, attn_ready=0, xbase_dram=None,
                 xtmp_free=None):
    s_pe, s_dv = b.sem("s_pe"), b.sem("s_dv")
    idxs = [wr.push(wgroup, wsrcs[o], ford=o) for o in range(16)]
    epi_dv = {}
    epi_st = {}
    xsrc = xbase_dram if xbase_dram is not None else dr_xin

    for o in range(16):
        wr.ensure(idxs[o] + 4)
        for tc in range(NTC):
            g = o * 4 + tc

            def sy_xo(sy, o=o, tc=tc, g=g):
                if g >= 4:
                    sy.wait_ge(s_dv.h, epi_dv[g - 4])
                    sy.wait_ge(b.rs[g % 4].h, epi_st[g - 4])
                else:
                    sy.wait_ge(s_dv.h, xtmp_free if xtmp_free is not None
                               else b.ln_xtmp_free)
                    if getattr(b, 'ln_store_ring', None) is not None:
                        ctr, val = b.ln_store_ring[g % 4]
                        sy.wait_ge(ctr.h, val)
                inst = sy.dma_start(
                    b.xtmp[:, (g % 4) * TL:(g % 4 + 1) * TL],
                    xsrc[o * 128:(o + 1) * 128, tc * TL:(tc + 1) * TL])
                b.rx[g % 4].inc(inst, 16)
            b.on("sync", sy_xo)
            xo_ld = b.rx[g % 4].v

            half = b.psB[:, (g % 2) * TL:(g % 2) * TL + TL]

            def pe_o(t, o=o, tc=tc, g=g, half=half, idx=idxs[o]):
                wr.wait(t, idx)
                if g == 0:
                    t.wait_ge(s_dv.h, attn_ready)
                if g >= 2:
                    t.wait_ge(s_dv.h, epi_dv[g - 2])
                for m in range(NK):
                    inst = t.matmul(half, wr.slot(idxs[o], m),
                                    rhs_buf[:, m * 2048 + tc * TL: m * 2048 + (tc + 1) * TL],
                                    start=(m == 0), stop=(m == NK - 1))
                s_pe.inc(inst)
            b.on("tensor", pe_o)
            mmv = s_pe.v
            if tc == 3:
                wr.consumed(mmv)

            def dv_epi(v, o=o, tc=tc, g=g, half=half, mmv=mmv, xo_ld=xo_ld):
                v.wait_ge(s_pe.h, mmv)
                v.wait_ge(b.rx[g % 4].h, xo_ld)
                xs = b.xtmp[:, (g % 4) * TL:(g % 4 + 1) * TL]
                if bias_col is not None:
                    inst = v.scalar_tensor_tensor(xs, half, bias_col(o), xs,
                                                  ALU.add, ALU.add)
                else:
                    inst = v.scalar_tensor_tensor(xs, half, dscale, xs,
                                                  ALU.mult, ALU.add)
                s_dv.inc(inst)
            b.on("vector", dv_epi)
            epi_dv[g] = s_dv.v

            def sy_st(sy, o=o, tc=tc, g=g, ev=s_dv.v):
                sy.wait_ge(s_dv.h, ev)
                inst = sy.dma_start(
                    dr_xout[o * 128:(o + 1) * 128, tc * TL:(tc + 1) * TL],
                    b.xtmp[:, (g % 4) * TL:(g % 4 + 1) * TL])
                b.rs[g % 4].inc(inst, 16)
            b.on("sync", sy_st)
            epi_st[g] = b.rs[g % 4].v

    b.final_rs = [r.v for r in b.rs]


def emit_boundary(b: Bld):
    s_pe, s_dv, s_ac = b.sem("s_pe"), b.sem("s_dv"), b.sem("s_ac")
    pe0, dv0, ac0 = s_pe.v, s_dv.v, s_ac.v

    def tn(t):
        t.wait_ge(s_dv.h, dv0)
        t.wait_ge(s_ac.h, ac0)
    b.on("tensor", tn)

    def vc(v):
        v.wait_ge(s_pe.h, pe0)
        v.wait_ge(s_ac.h, ac0)
    b.on("vector", vc)

    def sc(s):
        s.wait_ge(s_pe.h, pe0)
        s.wait_ge(s_dv.h, dv0)
    b.on("scalar", sc)

    def gp(g):
        g.wait_ge(s_pe.h, pe0)
        g.wait_ge(s_dv.h, dv0)
    b.on("gpsimd", gp)

    b._ln_prev_norm = dv0
    b._ln_prev_store = {p: b.rs[p].v for p in range(4)}


def emit_finwait(b: Bld):
    frs = b.final_rs

    def sy_fin(sy):
        for p in range(4):
            sy.wait_ge(b.rs[p].h, frs[p])
    b.on("sync", sy_fin)



def decl_l0_attn(nc, pfx=""):
    dr = {}
    for w in ("wq", "wk", "wo"):
        dr[w + 'f'] = nc.dram_tensor(pfx + w + "f", [16, 128, 16, 128], F32, kind="ExternalInput").ap()
        dr[w + 'c'] = nc.dram_tensor(pfx + w + "c", [16, 128, 16, 128], BF16).ap()
    dr['wvtf'] = nc.dram_tensor(pfx + "wvtf", [4, 128, 16, 512], F32, kind="ExternalInput").ap()
    dr['wvtc'] = nc.dram_tensor(pfx + "wvtc", [4, 128, 16, 512], BF16).ap()
    dr['bvrow'] = nc.dram_tensor(pfx + "bvrow", [1, 2048], F32, kind="ExternalInput").ap()
    dr['bqkvo'] = nc.dram_tensor(pfx + "bqkvo", [128, 64], F32, kind="ExternalInput").ap()
    dr['masks'] = nc.dram_tensor(pfx + "masks", [8, 128, 512], F32, kind="ExternalInput").ap()
    dr['qscr'] = nc.dram_tensor(pfx + "qscr", [2048, 2048], BF16).ap()
    dr['kscr'] = nc.dram_tensor(pfx + "kscr", [2048, 2048], BF16).ap()
    dr['vscr'] = nc.dram_tensor(pfx + "vscr", [2048, 2048], BF16).ap()
    dr['kall'] = nc.dram_tensor(pfx + "kall", [4096, 2048], BF16).ap()
    dr['vall'] = nc.dram_tensor(pfx + "vall", [4096, 2048], BF16).ap()
    dr['_pfx'] = pfx
    return dr


def fill_l0_attn(dr, pfx=""):
    items = []
    for w in ("wq", "wk", "wo"):
        for m in range(16):
            items.append((pfx + w, dr[w + 'f'][m], dr[w + 'c'][m]))
    for n in range(4):
        items.append((pfx + 'wvt', dr['wvtf'][n], dr['wvtc'][n]))
    return items


def emit_sub_l0_attn(b, dr, wc, xin, xout):
    pfx = dr.get('_pfx', '')
    nc = b.nc
    s_g = b.sem("s_g")
    s_cc = b.sem("s_cc")
    s_dv = b.sem("s_dv")
    qscr, kscr, vscr, kall, vall = (dr[k] for k in ("qscr", "kscr", "vscr", "kall", "vall"))

    def g_small(g):
        inst = g.dma_start(b.b1v[:, 0:64], dr['bqkvo'])
        s_g.inc(inst, 16)
        for s8 in range(8):
            inst = g.dma_start(b.ovl[:, s8 * 512:(s8 + 1) * 512], dr['masks'][s8])
            s_g.inc(inst, 16)
    b.on("gpsimd", g_small)
    masks_ready = s_g.v
    bias_ready = s_g.v

    emit_ln(b, xin, xbase_dram=None)

    wr = WRing(b, wc)

    def rhs_xn(k, tc):
        return b.xn[:, k * 2048 + tc * TL: k * 2048 + (tc + 1) * TL]

    emit_proj_fm(b, wr, pfx + 'wq', [dr['wqc'][m] for m in range(16)], rhs_xn,
                 lambda m, tc: ('d', qscr[m * 128:(m + 1) * 128, tc * TL:(tc + 1) * TL]),
                 bias_col=lambda m: b.b1v[:, m:m + 1],
                 norm_col=lambda m: b.vec[:, m:m + 1],
                 pe_extra_wait=[(b.sem("s_dv"), b.ln_norm_done), (s_g, bias_ready)])
    emit_proj_fm(b, wr, pfx + 'wk', [dr['wkc'][m] for m in range(16)], rhs_xn,
                 lambda m, tc: ('d', kscr[m * 128:(m + 1) * 128, tc * TL:(tc + 1) * TL]),
                 bias_col=lambda m: b.b1v[:, 16 + m:17 + m],
                 norm_col=lambda m: b.vec[:, 16 + m:17 + m])
    ri_k = [r.v for r in b.ri]

    def g_bv(g):
        inst = g.dma_start(b.wbc[0:1, 0:2048], dr['bvrow'])
        b.sem("s_g").inc(inst, 16)
    b.on("gpsimd", g_bv)
    bv_done = b.sem("s_g").v

    def g_gather_k(g):
        for p in range(8):
            g.wait_ge(b.ri[p].h, ri_k[p])
        g.collective_compute("AllGather", ALU.bypass,
                             replica_groups=[[0, 1], [2, 3], [4, 5], [6, 7]],
                             ins=[kscr.opt()], outs=[kall.opt()]).then_inc(s_cc.h)
        s_cc.v += 1
    b.on("gpsimd", g_gather_k)

    def pe_bvw(t):
        t.wait_ge(b.sem("s_g").h, bv_done)
    b.on("tensor", pe_bvw)
    emit_proj_tm(b, wc, pfx + 'wvt', dr['wvtc'], vscr, bvrow_sb=b.wbc)
    proj_psum_free = b.proj_evac_done
    ri_v = [r.v for r in b.ri]

    def g_gather_v(g):
        for p in range(8):
            g.wait_ge(b.ri[p].h, ri_v[p])
        g.collective_compute("AllGather", ALU.bypass,
                             replica_groups=[[0, 1], [2, 3], [4, 5], [6, 7]],
                             ins=[vscr.opt()], outs=[vall.opt()]).then_inc(s_cc.h)
        s_cc.v += 1
    b.on("gpsimd", g_gather_v)
    gather_done = s_cc.v

    emit_attn_v2(b, qscr, kall, vall, b.xn, masks_ready, gather_done, proj_psum_free)

    emit_outproj(b, wr, pfx + 'wo', [dr['woc'][o] for o in range(16)], b.xn,
                 xin, xout,
                 bias_col=lambda o: b.b1v[:, 48 + o:49 + o],
                 attn_ready=b.attn_done)


def build_l0_attn_nc():
    b = Bld("l0attn")
    nc = b.nc
    dr = {}
    dr['xin'] = nc.dram_tensor("xin", [D, 2048], F32, kind="ExternalInput").ap()
    dr['xout'] = nc.dram_tensor("xout", [D, 2048], F32, kind="ExternalOutput").ap()
    dr.update(decl_l0_attn(nc))

    b.start()
    alloc_common(b)
    wc = WCache(b)
    wc.fill(fill_l0_attn(dr))
    emit_sub_l0_attn(b, dr, wc, dr['xin'], dr['xout'])
    emit_finwait(b)
    return b.finish()


def _attn_masks(hpar):
    row = np.arange(128)[:, None]
    col = np.arange(512)[None, :]
    qq = col // 128
    rq = col % 128
    out = np.zeros((8, 128, 512), dtype=np.float32)
    for s in range(8):
        out[s] = (((2 * qq + hpar - s) * 128 + rq - row) >= 0).astype(np.float32)
    return out


def prep_l0_attn_inputs(inputs, xs):
    wq = _tile_w(np.asarray(inputs['L0_attn_w_q']))
    wk = _tile_w(np.asarray(inputs['L0_attn_w_k']))
    wvt = _tile_rhs_v(np.asarray(inputs['L0_attn_w_v']))
    wo = _tile_w(np.asarray(inputs['L0_attn_w_o']))
    bq = _perpart(inputs['L0_attn_b_q'])
    bk = _perpart(inputs['L0_attn_b_k'])
    bv = _perpart(inputs['L0_attn_b_v'])
    bo = _perpart(inputs['L0_attn_b_o'])
    bqkvo = np.concatenate([bq, bk, bv, bo], axis=1).astype(np.float32)
    bvrow = np.ascontiguousarray(np.asarray(inputs['L0_attn_b_v']).reshape(1, 2048)).astype(np.float32)
    maps = []
    for c in range(8):
        hpar = c % 2
        maps.append({
            "xin": xs[c], "wqf": wq, "wkf": wk, "wvtf": wvt, "wof": wo,
            "bqkvo": bqkvo, "bvrow": bvrow, "masks": _attn_masks(hpar),
        })
    return maps



def emit_rms(b: Bld, wcol_fn, eps_col, first_wait=None):
    s_pe, s_dv, s_ac = b.sem("s_pe"), b.sem("s_dv"), b.sem("s_ac")
    ms = b.stat[:, TL:2 * TL]
    sq_off = 0
    prev_norm = None
    prev_stat = None
    for tc in range(NTC):
        def ac_sq(s, tc=tc, prev_norm=prev_norm):
            if tc == 0 and first_wait is not None:
                for sem, val in first_wait:
                    s.wait_ge(sem.h, val)
            if prev_norm is not None:
                s.wait_ge(s_dv.h, prev_norm)
            for k in range(NK):
                inst = s.activation(b.aux[:, sq_off + k * TL: sq_off + (k + 1) * TL],
                                    b.xn[:, k * 2048 + tc * TL: k * 2048 + (tc + 1) * TL],
                                    AF.Square)
            s_ac.inc(inst)
        b.on("scalar", ac_sq)
        sqd = s_ac.v

        def pe_ms(t, tc=tc, sqd=sqd, prev_stat=prev_stat):
            t.wait_ge(s_ac.h, sqd)
            if prev_stat is not None:
                t.wait_ge(s_ac.h, prev_stat)
            for k in range(NK):
                inst = t.matmul(b.psC, b.ones[:],
                                b.aux[:, sq_off + k * TL: sq_off + (k + 1) * TL],
                                start=(k == 0), stop=(k == NK - 1))
            s_pe.inc(inst)
        b.on("tensor", pe_ms)
        msd = s_pe.v

        def ac_rstd(s, tc=tc, msd=msd):
            s.wait_ge(s_pe.h, msd)
            inst = s.activation(ms, b.psC, AF.Ln, scale=1.0 / D, bias=eps_col)
            b.fence(s, inst)
            inst = s.activation(ms, ms, AF.Exp, scale=-0.5)
            s_ac.inc(inst)
        b.on("scalar", ac_rstd)
        prev_stat = s_ac.v
        rstdd = s_ac.v

        def dv_nrm(v, tc=tc, rstdd=rstdd):
            v.wait_ge(s_ac.h, rstdd)
            for k in range(NK):
                xnk = b.xn[:, k * 2048 + tc * TL: k * 2048 + (tc + 1) * TL]
                inst = v.scalar_tensor_tensor(xnk, xnk, wcol_fn(k), ms,
                                              ALU.mult, ALU.mult)
            s_dv.inc(inst)
        b.on("vector", dv_nrm)
        prev_norm = s_dv.v
    b.rms_done = prev_norm


def emit_xn_reload(b: Bld, xn_dram, first_wait=None):
    s_xr = b.sem("s_xr")

    def sy(sy_):
        if first_wait is not None:
            for sem, val in first_wait:
                sy_.wait_ge(sem.h, val)
        for k in range(NK):
            inst = sy_.dma_start(b.xn[:, k * 2048:(k + 1) * 2048],
                                 xn_dram[k * 128:(k + 1) * 128, :])
            s_xr.inc(inst, 16)
    b.on("sync", sy)
    b.xn_reload_done = s_xr.v
    return s_xr


def emit_Amm(b: Bld, src_buf, evac_cb, first_wait=None):
    s_pe = b.sem("s_pe")
    done = []
    prev_evac = None
    for tc in range(NTC):
        def pe_a(t, tc=tc, prev_evac=prev_evac):
            if tc == 0 and first_wait is not None:
                for sem, val in first_wait:
                    t.wait_ge(sem.h, val)
            if prev_evac is not None:
                t.wait_ge(prev_evac[0].h, prev_evac[1])
            for k in range(NK):
                inst = t.matmul(b.psE[0:16, :], b.wbc[:, k * 16:(k + 1) * 16],
                                src_buf[:, k * 2048 + tc * TL: k * 2048 + (tc + 1) * TL],
                                start=(k == 0), stop=(k == NK - 1))
            s_pe.inc(inst)
        b.on("tensor", pe_a)
        prev_evac = evac_cb(tc, s_pe.v)
        done.append(s_pe.v)
    return done



def emit_l1_mlp(b: Bld, wr: WRing, dr, xbase_dram, xout_dram, modhA, h_ready):
    pfx = dr.get('_pfx', '')
    s_pe, s_dv, s_ac = b.sem("s_pe"), b.sem("s_dv"), b.sem("s_ac")
    epi_dv = {}
    epi_st = {}
    for tc in range(NTC):
        silu_done = {}
        gu_done = {}
        wi_idx = {}
        wu_idx = {}
        for fb in range(64):
            wi_idx[fb] = wr.push(pfx + 'wi', dr['wic'][fb], ford=fb)
            wu_idx[fb] = wr.push(pfx + 'wu', dr['wuc'][fb], ford=fb)
        for fb in range(64):
            wr.ensure(wi_idx[fb] + 4)
            pg = (b.psA if fb % 2 == 0 else b.psB)[:, 0:TL]
            pu = (b.psA if fb % 2 == 0 else b.psB)[:, TL:2 * TL]

            def pe_gu(t, tc=tc, fb=fb, pg=pg, pu=pu):
                wr.wait(t, wi_idx[fb])
                if fb == 0:
                    t.wait_ge(s_dv.h, h_ready)
                if fb >= 2:
                    t.wait_ge(s_ac.h, silu_done[fb - 2])
                    t.wait_ge(s_dv.h, gu_done[fb - 2])
                for k in range(NK):
                    t.matmul(pg, wr.slot(wi_idx[fb], k),
                             b.xn[:, k * 2048 + tc * TL: k * 2048 + (tc + 1) * TL],
                             start=(k == 0), stop=False)
                t.matmul(pg, b.ovl[0:16, fb * 128:(fb + 1) * 128],
                         modhA[0:16, tc * TL:(tc + 1) * TL],
                         start=False, stop=True)
                wr.wait(t, wu_idx[fb])
                for k in range(NK):
                    inst = t.matmul(pu, wr.slot(wu_idx[fb], k),
                                    b.xn[:, k * 2048 + tc * TL: k * 2048 + (tc + 1) * TL],
                                    start=(k == 0), stop=(k == NK - 1))
                s_pe.inc(inst)
            b.on("tensor", pe_gu)
            wr.consumed(s_pe.v)
            wr.consumed(s_pe.v)
            mmv = s_pe.v

            gslot = b.pp[:, (fb % 2) * TL:(fb % 2 + 1) * TL]
            uslot = b.pp[:, 1024 + (fb % 2) * TL: 1024 + (fb % 2 + 1) * TL]

            def ac_silu(s, fb=fb, mmv=mmv, pg=pg, gslot=gslot):
                s.wait_ge(s_pe.h, mmv)
                if fb >= 2:
                    s.wait_ge(s_dv.h, gu_done[fb - 2])
                inst = s.activation(gslot, pg, AF.Silu)
                s_ac.inc(inst)
            b.on("scalar", ac_silu)
            silu_done[fb] = s_ac.v

            def dv_gu(v, tc=tc, fb=fb, mmv=mmv, pu=pu, gslot=gslot, uslot=uslot,
                      sl=s_ac.v):
                v.wait_ge(s_pe.h, mmv)
                v.tensor_copy(uslot, pu)
                v.wait_ge(s_ac.h, sl)
                inst = v.tensor_tensor(b.aux[:, fb * TL:(fb + 1) * TL],
                                       gslot, uslot, ALU.mult)
                s_dv.inc(inst)
            b.on("vector", dv_gu)
            gu_done[fb] = s_dv.v
        gu_all = s_dv.v

        for o in range(16):
            qidx = [wr.push(pfx + 'wd', dr['wdc'][o, q4], ford=o * 4 + q4) for q4 in range(4)]
            wr.ensure(qidx[-1] + 1)
            g = tc * 16 + o

            def sy_xb(sy, tc=tc, o=o, g=g):
                if g >= 4:
                    sy.wait_ge(s_dv.h, epi_dv[g - 4])
                    sy.wait_ge(b.rs[g % 4].h, epi_st[g - 4])
                else:
                    sy.wait_ge(s_dv.h, b.ln_xtmp_free)
                    if b.ln_store_ring is not None:
                        ctr, val = b.ln_store_ring[g % 4]
                        sy.wait_ge(ctr.h, val)
                inst = sy.dma_start(
                    b.xtmp[:, (g % 4) * TL:(g % 4 + 1) * TL],
                    xbase_dram[o * 128:(o + 1) * 128, tc * TL:(tc + 1) * TL])
                b.rx[g % 4].inc(inst, 16)
            b.on("sync", sy_xb)
            xb_ld = b.rx[g % 4].v

            pd = (b.psA if o % 2 == 0 else b.psB)[:, 0:TL]

            def pe_d(t, tc=tc, o=o, g=g, pd=pd, qidx=qidx, gu_all=gu_all):
                if o == 0:
                    t.wait_ge(s_dv.h, gu_all)
                if o >= 2:
                    t.wait_ge(s_dv.h, epi_dv[g - 2])
                for q4 in range(4):
                    wr.wait(t, qidx[q4])
                    for kk in range(NK):
                        fb = q4 * 16 + kk
                        inst = t.matmul(pd, wr.slot(qidx[q4], kk),
                                        b.aux[:, fb * TL:(fb + 1) * TL],
                                        start=(q4 == 0 and kk == 0),
                                        stop=(q4 == 3 and kk == NK - 1))
                s_pe.inc(inst)
            b.on("tensor", pe_d)
            for _ in range(4):
                wr.consumed(s_pe.v)
            mmd = s_pe.v

            def dv_epi(v, g=g, mmd=mmd, pd=pd, xb_ld=xb_ld):
                v.wait_ge(s_pe.h, mmd)
                v.wait_ge(b.rx[g % 4].h, xb_ld)
                xs = b.xtmp[:, (g % 4) * TL:(g % 4 + 1) * TL]
                inst = v.scalar_tensor_tensor(xs, pd, 0.5, xs, ALU.mult, ALU.add)
                s_dv.inc(inst)
            b.on("vector", dv_epi)
            epi_dv[g] = s_dv.v

            def sy_st(sy, tc=tc, o=o, g=g, ev=s_dv.v):
                sy.wait_ge(s_dv.h, ev)
                inst = sy.dma_start(
                    xout_dram[o * 128:(o + 1) * 128, tc * TL:(tc + 1) * TL],
                    b.xtmp[:, (g % 4) * TL:(g % 4 + 1) * TL])
                epi_st[g] = b.rs[g % 4].inc(inst, 16)
            b.on("sync", sy_st)

    b.final_rs = [r.v for r in b.rs]



def decl_l1_ffn(nc, pfx=""):
    dr = {}
    for w in ("wq", "wk", "wo"):
        dr[w + 'f'] = nc.dram_tensor(pfx + w + "f", [16, 128, 16, 128], F32, kind="ExternalInput").ap()
        dr[w + 'c'] = nc.dram_tensor(pfx + w + "c", [16, 128, 16, 128], BF16).ap()
    dr['wvtf'] = nc.dram_tensor(pfx + "wvtf", [4, 128, 16, 512], F32, kind="ExternalInput").ap()
    dr['wvtc'] = nc.dram_tensor(pfx + "wvtc", [4, 128, 16, 512], BF16).ap()
    dr['wif'] = nc.dram_tensor(pfx + "wif", [64, 128, 16, 128], F32, kind="ExternalInput").ap()
    dr['wic'] = nc.dram_tensor(pfx + "wic", [64, 128, 16, 128], BF16).ap()
    dr['wuf'] = nc.dram_tensor(pfx + "wuf", [64, 128, 16, 128], F32, kind="ExternalInput").ap()
    dr['wuc'] = nc.dram_tensor(pfx + "wuc", [64, 128, 16, 128], BF16).ap()
    dr['wdf'] = nc.dram_tensor(pfx + "wdf", [16, 4, 128, 16, 128], F32, kind="ExternalInput").ap()
    dr['wdc'] = nc.dram_tensor(pfx + "wdc", [16, 4, 128, 16, 128], BF16).ap()
    dr['apack'] = nc.dram_tensor(pfx + "apack", [16, 128, 16], F32, kind="ExternalInput").ap()
    dr['bmat'] = nc.dram_tensor(pfx + "bmat", [16, 8192], F32, kind="ExternalInput").ap()
    dr['lnw'] = nc.dram_tensor(pfx + "lnw", [128, 32], F32, kind="ExternalInput").ap()
    dr['masks'] = nc.dram_tensor(pfx + "masks", [8, 128, 512], F32, kind="ExternalInput").ap()
    dr['_pfx'] = pfx
    return dr


def decl_l1_ffn_scratch(nc, pfx):
    dr = {}
    dr['qscr'] = nc.dram_tensor(pfx + "qscr", [2048, 2048], BF16).ap()
    dr['kscr'] = nc.dram_tensor(pfx + "kscr", [2048, 2048], BF16).ap()
    dr['vscr'] = nc.dram_tensor(pfx + "vscr", [2048, 2048], BF16).ap()
    dr['kall'] = nc.dram_tensor(pfx + "kall", [4096, 2048], BF16).ap()
    dr['vall'] = nc.dram_tensor(pfx + "vall", [4096, 2048], BF16).ap()
    dr['xnscr'] = nc.dram_tensor(pfx + "xnscr", [2048, 2048], BF16).ap()
    dr['xbscr'] = nc.dram_tensor(pfx + "xbscr", [2048, 2048], F32).ap()
    return dr


def fill_l1_ffn(dr, pfx=""):
    items = []
    for w in ("wq", "wk", "wo"):
        for m in range(16):
            items.append((pfx + w, dr[w + 'f'][m], dr[w + 'c'][m]))
    for n in range(4):
        items.append((pfx + 'wvt', dr['wvtf'][n], dr['wvtc'][n]))
    for fb in range(64):
        items.append((pfx + 'wi', dr['wif'][fb], dr['wic'][fb]))
        items.append((pfx + 'wu', dr['wuf'][fb], dr['wuc'][fb]))
    for o in range(16):
        for q4 in range(4):
            items.append((pfx + 'wd', dr['wdf'][o, q4], dr['wdc'][o, q4]))
    return items


def emit_sub_l1_ffn(b, dr, wc, xin, xout, scr):
    pfx = dr.get('_pfx', '')
    s_g = b.sem("s_g")
    s_cc = b.sem("s_cc")
    s_dv = b.sem("s_dv")
    s_pe = b.sem("s_pe")
    qscr, kscr, vscr, kall, vall = (scr[k] for k in ("qscr", "kscr", "vscr", "kall", "vall"))
    xnscr, xbscr = scr['xnscr'], scr['xbscr']

    def g_small(g):
        inst = g.dma_start(b.b1v[:, 0:32], dr['lnw'])
        s_g.inc(inst, 16)
        for s8 in range(8):
            inst = g.dma_start(b.ovl[:, s8 * 512:(s8 + 1) * 512], dr['masks'][s8])
            s_g.inc(inst, 16)
        inst = g.dma_start(b.wbc[:, 0:256].rearrange("p (k c) -> p k c", k=16),
                           dr['apack'].rearrange("k p c -> p k c"))
        s_g.inc(inst, 16)
    b.on("gpsimd", g_small)
    small_done = s_g.v
    masks_ready = small_done

    s_xn0 = b.sem("s_xn").v
    emit_ln(b, xin, xbase_dram=xbscr, xscale=0.5, xn_dram=xnscr)
    xn_stored = b.sem("s_xn").v

    emit_rms(b, lambda k: b.b1v[:, k:k + 1], b.vec[:, 63:64],
             first_wait=[(b.sem("s_dv"), b.ln_norm_done), (s_g, small_done),
                         (b.sem("s_pe"), b.ln_aux_free), (b.sem("s_xn"), xn_stored)])

    wr = WRing(b, wc)

    def rhs_xn(k, tc):
        return b.xn[:, k * 2048 + tc * TL: k * 2048 + (tc + 1) * TL]

    emit_proj_fm(b, wr, pfx + 'wq', [dr['wqc'][m] for m in range(16)], rhs_xn,
                 lambda m, tc: ('d', qscr[m * 128:(m + 1) * 128, tc * TL:(tc + 1) * TL]),
                 norm_col=lambda m: b.vec[:, m:m + 1],
                 pe_extra_wait=[(b.sem("s_dv"), b.rms_done)])
    emit_proj_fm(b, wr, pfx + 'wk', [dr['wkc'][m] for m in range(16)], rhs_xn,
                 lambda m, tc: ('d', kscr[m * 128:(m + 1) * 128, tc * TL:(tc + 1) * TL]),
                 norm_col=lambda m: b.vec[:, 16 + m:17 + m])
    ri_k = [r.v for r in b.ri]

    def g_gather_k(g):
        for p in range(8):
            g.wait_ge(b.ri[p].h, ri_k[p])
        g.collective_compute("AllGather", ALU.bypass,
                             replica_groups=[[0, 1], [2, 3], [4, 5], [6, 7]],
                             ins=[kscr.opt()], outs=[kall.opt()]).then_inc(s_cc.h)
        s_cc.v += 1
    b.on("gpsimd", g_gather_k)

    emit_proj_tm(b, wc, pfx + 'wvt', dr['wvtc'], vscr)
    proj_psum_free = b.proj_evac_done
    ri_v = [r.v for r in b.ri]

    def g_gather_v(g):
        for p in range(8):
            g.wait_ge(b.ri[p].h, ri_v[p])
        g.collective_compute("AllGather", ALU.bypass,
                             replica_groups=[[0, 1], [2, 3], [4, 5], [6, 7]],
                             ins=[vscr.opt()], outs=[vall.opt()]).then_inc(s_cc.h)
        s_cc.v += 1
    b.on("gpsimd", g_gather_v)
    gather_done = s_cc.v

    emit_attn_v2(b, qscr, kall, vall, b.xn, masks_ready, gather_done, proj_psum_free)

    emit_proj_fm(b, wr, pfx + 'wo', [dr['woc'][o] for o in range(16)], rhs_xn,
                 lambda m, tc: ('s', b.aux[:, m * 2048 + tc * TL: m * 2048 + (tc + 1) * TL]),
                 pe_extra_wait=[(s_dv, b.attn_done)])
    wo_done_dv = b.proj_evac_done

    s_ac = b.sem("s_ac")
    mod_done = {}

    def mk_mod_evac(tc, pev):
        def ac_mod(s, tc=tc, pev=pev):
            s.wait_ge(s_pe.h, pev)
            inst = s.activation(b.io[0:16, 2048 + tc * TL: 2048 + (tc + 1) * TL],
                                b.psE[0:16, :], AF.Tanh)
            s_ac.inc(inst)
        b.on("scalar", ac_mod)
        mod_done[tc] = s_ac.v
        return (s_ac, s_ac.v)

    hA2_done = emit_Amm(b, b.aux, mk_mod_evac,
                        first_wait=[(s_dv, wo_done_dv), (s_g, small_done)])
    hA2_pe = hA2_done[-1]

    sxr = emit_xn_reload(b, xnscr, first_wait=[(s_dv, wo_done_dv)])
    emit_rms(b, lambda k: b.b1v[:, 16 + k:17 + k], b.vec[:, 63:64],
             first_wait=[(b.sem("s_xr"), b.xn_reload_done), (s_pe, hA2_pe)])
    h_done = b.rms_done

    modhA_done = {}

    def mk_modha_evac(tc, pev):
        def dv_mha(v, tc=tc, pev=pev):
            v.wait_ge(s_pe.h, pev)
            v.wait_ge(s_ac.h, mod_done[NTC - 1])
            inst = v.scalar_tensor_tensor(
                b.io[0:16, tc * TL:(tc + 1) * TL], b.psE[0:16, :], 0.1,
                b.io[0:16, 2048 + tc * TL: 2048 + (tc + 1) * TL], ALU.mult, ALU.mult)
            s_dv.inc(inst)
        b.on("vector", dv_mha)
        modhA_done[tc] = s_dv.v
        return (s_dv, s_dv.v)

    emit_Amm(b, b.xn, mk_modha_evac, first_wait=[(s_dv, h_done)])

    s_gb = b.sem("s_gb")
    gb0 = s_gb.v

    def g_bmat(g):
        g.wait_ge(s_dv.h, b.attn_done)
        inst = g.dma_start(b.ovl[0:16, :], dr['bmat'])
        s_gb.inc(inst, 16)
    b.on("gpsimd", g_bmat)
    bmat_done = s_gb.v

    def pe_guard(t):
        t.wait_ge(s_gb.h, bmat_done)
        t.wait_ge(s_dv.h, modhA_done[NTC - 1])
    b.on("tensor", pe_guard)

    emit_l1_mlp(b, wr, dr, xbscr, xout, b.io, h_done)


def build_l1_ffn_nc():
    b = Bld("l1ffn")
    nc = b.nc
    dr = {}
    dr['xin'] = nc.dram_tensor("xin", [D, 2048], F32, kind="ExternalInput").ap()
    dr['xout'] = nc.dram_tensor("xout", [D, 2048], F32, kind="ExternalOutput").ap()
    dr.update(decl_l1_ffn(nc))
    scr = decl_l1_ffn_scratch(nc, "s_")

    b.start()
    alloc_common(b)
    wc = WCache(b)
    wc.fill(fill_l1_ffn(dr))
    emit_sub_l1_ffn(b, dr, wc, dr['xin'], dr['xout'], scr)
    emit_finwait(b)
    return b.finish()


def prep_l1_ffn_inputs(inputs, xs):
    wq = _tile_w(np.ascontiguousarray(np.asarray(inputs['L1_ffn_Wq']).T))
    wk = _tile_w(np.ascontiguousarray(np.asarray(inputs['L1_ffn_Wk']).T))
    wo = _tile_w(np.ascontiguousarray(np.asarray(inputs['L1_ffn_Wo']).T))
    wvt = _tile_rhs_v(np.ascontiguousarray(np.asarray(inputs['L1_ffn_Wv']).T))
    wi = _tile_w(np.ascontiguousarray(np.asarray(inputs['L1_ffn_W_in_base']).T))
    wu = _tile_w(np.ascontiguousarray(np.asarray(inputs['L1_ffn_W_up_base']).T))
    wd = _tile_w(np.ascontiguousarray(np.asarray(inputs['L1_ffn_W_down']).T))
    wd = np.ascontiguousarray(wd.reshape(16, 128, 4, 16, 128).transpose(0, 2, 1, 3, 4))
    apack = np.ascontiguousarray(np.asarray(inputs['L1_ffn_A']).reshape(16, 128, 16)).astype(np.float32)
    bmat = np.ascontiguousarray(np.asarray(inputs['L1_ffn_B'])).astype(np.float32)
    lnw = np.concatenate([_perpart(inputs['L1_ffn_ln1_w']),
                          _perpart(inputs['L1_ffn_ln2_w'])], axis=1).astype(np.float32)
    maps = []
    for c in range(8):
        maps.append({
            "xin": xs[c], "wqf": wq, "wkf": wk, "wvtf": wvt, "wof": wo,
            "wif": wi, "wuf": wu, "wdf": wd, "apack": apack, "bmat": bmat,
            "lnw": lnw, "masks": _attn_masks(c % 2),
        })
    return maps



def emit_proj_xpos(b: Bld, wr: WRing, wgroup, wsrcs, rhs_fn, cos_d, sin_d, dscr,
                   pe_extra_wait=None):
    s_pe, s_dv = b.sem("s_pe"), b.sem("s_dv")
    ri_base = [r.v for r in b.ri]
    xtmp_prev = getattr(b, '_xpos_prev_t12', None)
    idxs = [wr.push(wgroup, wsrcs[m], ford=m) for m in range(16)]
    evac_done = {}
    t12_done = {}
    rot_done = {}
    io_store = {}
    cs_ld = {}

    for m in range(16):
        wr.ensure(idxs[m] + 4)
        for tc in range(NTC):
            g = m * 4 + tc

            def sy_cs(sy, m=m, tc=tc, g=g):
                if g >= 2:
                    sy.wait_ge(s_dv.h, t12_done[g - 2])
                else:
                    if pe_extra_wait is not None:
                        for sem, val in pe_extra_wait:
                            sy.wait_ge(sem.h, val)
                    if xtmp_prev is not None:
                        sy.wait_ge(s_dv.h, xtmp_prev)
                inst = sy.dma_start(
                    b.xtmp[:, (g % 2) * TL:(g % 2 + 1) * TL],
                    cos_d[m * 128:(m + 1) * 128, tc * TL:(tc + 1) * TL])
                b.rx[g % 2].inc(inst, 16)
                inst = sy.dma_start(
                    b.xtmp[:, (2 + g % 2) * TL:(2 + g % 2 + 1) * TL],
                    sin_d[m * 128:(m + 1) * 128, tc * TL:(tc + 1) * TL])
                b.rx[2 + g % 2].inc(inst, 16)
                cs_ld[g] = (b.rx[g % 2].v, b.rx[2 + g % 2].v)
            b.on("sync", sy_cs)

            half = b.psA[:, (g % 2) * TL:(g % 2) * TL + TL]
            rhalf = b.psB[:, (g % 2) * TL:(g % 2) * TL + TL]
            qslot = b.pp[:, (g % 2) * TL:(g % 2 + 1) * TL]

            def pe_mm(t, m=m, tc=tc, g=g, half=half, idx=idxs[m]):
                if g == 0 and pe_extra_wait is not None:
                    for sem, val in pe_extra_wait:
                        t.wait_ge(sem.h, val)
                wr.wait(t, idx)
                if g >= 2:
                    t.wait_ge(s_dv.h, t12_done[g - 2])
                for k in range(NK):
                    inst = t.matmul(half, wr.slot(idxs[m], k), rhs_fn(k, tc),
                                    start=(k == 0), stop=(k == NK - 1))
                s_pe.inc(inst)
            b.on("tensor", pe_mm)
            mmv = s_pe.v
            if tc == 3:
                wr.consumed(mmv)

            def dv_ev(v, g=g, half=half, mmv=mmv, qslot=qslot):
                v.wait_ge(s_pe.h, mmv)
                inst = v.tensor_copy(qslot, half)
                s_dv.inc(inst)
            b.on("vector", dv_ev)
            evac_done[g] = s_dv.v

            def pe_rot(t, g=g, rhalf=rhalf, qslot=qslot, ev=s_dv.v):
                t.wait_ge(s_dv.h, ev)
                inst = t.matmul(rhalf, b.rt[:, :], qslot, start=True, stop=True)
                s_pe.inc(inst)
            b.on("tensor", pe_rot)
            rot_done[g] = s_pe.v

            tA = b.stat[:, (g % 2) * TL:(g % 2 + 1) * TL]
            tB = b.stat[:, (2 + g % 2) * TL:(2 + g % 2 + 1) * TL]
            ioslot = b.io[:, (g % 8) * TL:(g % 8 + 1) * TL]

            def dv_x(v, g=g, rhalf=rhalf, qslot=qslot, tA=tA, tB=tB,
                     ioslot=ioslot, rv=s_pe.v):
                v.wait_ge(b.rx[g % 2].h, cs_ld[g][0])
                v.wait_ge(b.rx[2 + g % 2].h, cs_ld[g][1])
                v.wait_ge(s_pe.h, rv)
                if g >= 8:
                    v.wait_ge(b.ri[g % 8].h, io_store[g - 8])
                else:
                    v.wait_ge(b.ri[g % 8].h, ri_base[g % 8])
                inst = v.tensor_tensor(tA, qslot, b.xtmp[:, (g % 2) * TL:(g % 2 + 1) * TL],
                                       ALU.mult)
                inst = v.tensor_tensor(tB, rhalf, b.xtmp[:, (2 + g % 2) * TL:(2 + g % 2 + 1) * TL],
                                       ALU.mult)
                b.fence(v, inst)
                inst = v.tensor_tensor(ioslot, tA, tB, ALU.add)
                s_dv.inc(inst)
            b.on("vector", dv_x)
            t12_done[g] = s_dv.v

            def sy_st(sy, m=m, tc=tc, g=g, ioslot=ioslot, ev=s_dv.v):
                sy.wait_ge(s_dv.h, ev)
                inst = sy.dma_start(
                    dscr[m * 128:(m + 1) * 128, tc * TL:(tc + 1) * TL], ioslot)
                io_store[g] = b.ri[g % 8].inc(inst, 16)
            b.on("sync", sy_st)

    b.proj_evac_done = t12_done[63]
    b._xpos_prev_t12 = t12_done[63]



def emit_attn_decay(b: Bld, qscr, kall, vall, dst_buf, masks_ready, gather_done,
                    psum_free, gammas):
    s_pe, s_dv, s_ac, s_cc = (b.sem(n) for n in ("s_pe", "s_dv", "s_ac", "s_cc"))
    kofs, vofs, qofs = 0, 8192, 16384
    head_done = {}
    kld = {}
    vld = {}
    qld = {}
    evac_rec = {}

    for h in range(NH):
        ks = h % 2

        def sy_k(sy, h=h, ks=ks):
            if h >= 2:
                sy.wait_ge(s_pe.h, head_done[h - 2])
            if h == 0:
                sy.wait_ge(s_cc.h, gather_done)
                sy.wait_ge(s_pe.h, b.ln_aux_free)
            for kt in range(32):
                r, j = kt % 2, kt // 2
                inst = sy.dma_start(
                    b.aux[:, kofs + ks * 4096 + kt * 128: kofs + ks * 4096 + (kt + 1) * 128],
                    kall[r * 2048 + h * 128:(r * 2048 + (h + 1) * 128), j * 128:(j + 1) * 128])
                b.rk[ks].inc(inst, 16)
            kld[h] = b.rk[ks].v
        b.on("sync", sy_k)

        def sc_v(s, h=h, ks=ks):
            if h >= 2:
                s.wait_ge(s_pe.h, head_done[h - 2])
            if h == 0:
                s.wait_ge(s_cc.h, gather_done)
                s.wait_ge(s_pe.h, b.ln_aux_free)
            for kt in range(32):
                r, j = kt % 2, kt // 2
                inst = s.dma_start(
                    b.aux[:, vofs + ks * 4096 + kt * 128: vofs + ks * 4096 + (kt + 1) * 128],
                    vall[r * 2048 + h * 128:(r * 2048 + (h + 1) * 128), j * 128:(j + 1) * 128],
                    transpose=True)
                b.rv[ks].inc(inst, 16)
            vld[h] = b.rv[ks].v
        b.on("scalar", sc_v)

        for jb in range(NTC):
            gq = (h * 4 + jb) % 2

            def sy_q(sy, h=h, jb=jb, gq=gq):
                gi = h * 4 + jb
                if gi >= 2:
                    sy.wait_ge(s_dv.h, evac_rec[gi - 2])
                inst = sy.dma_start(
                    b.aux[:, qofs + gq * TL: qofs + (gq + 1) * TL],
                    qscr[h * 128:(h + 1) * 128, jb * TL:(jb + 1) * TL])
                qld[(h, jb)] = b.rq[gq].inc(inst, 16)
            b.on("sync", sy_q)

            P = 4 * jb + 4
            ev_done = {}
            mask_done = {}
            spair_done = {}
            pv_done = {}

            for i in range(P + 1):
                def pe_kloop(t, h=h, jb=jb, i=i, P=P, gq=gq, ks=ks):
                    if i >= 1:
                        pi = i - 1
                        t.wait_ge(s_dv.h, mask_done.get(pi, ev_done[pi]))
                        if i == 1:
                            gi = h * 4 + jb
                            if gi >= 1:
                                t.wait_ge(s_dv.h, evac_rec[gi - 1])
                            elif psum_free is not None:
                                t.wait_ge(s_dv.h, psum_free)
                        for z in range(2):
                            kt = 2 * pi + z
                            pslot = b.pp[:, (pi % 2) * 1024 + z * TL:(pi % 2) * 1024 + (z + 1) * TL]
                            vt = b.aux[:, vofs + ks * 4096 + kt * 128: vofs + ks * 4096 + (kt + 1) * 128]
                            inst = t.matmul(b.psC, vt, pslot,
                                            start=(pi == 0 and z == 0), stop=(pi == P - 1 and z == 1))
                        s_pe.inc(inst)
                        pv_done[pi] = s_pe.v
                    if i < P:
                        if i == 0:
                            t.wait_ge(b.rq[gq].h, qld[(h, jb)])
                            if jb == 0:
                                t.wait_ge(b.rk[ks].h, kld[h])
                                t.wait_ge(b.rv[ks].h, vld[h])
                        if i >= 2:
                            t.wait_ge(s_dv.h, ev_done[i - 2])
                        ps = b.psA if i % 2 == 0 else b.psB
                        for z in range(2):
                            kt = 2 * i + z
                            ktile = b.aux[:, kofs + ks * 4096 + kt * 128: kofs + ks * 4096 + (kt + 1) * 128]
                            inst = t.matmul(ps[:, z * TL:(z + 1) * TL], ktile,
                                            b.aux[:, qofs + gq * TL: qofs + (gq + 1) * TL],
                                            start=True, stop=True)
                        s_pe.inc(inst)
                        spair_done[i] = s_pe.v
                b.on("tensor", pe_kloop)

                if i < P:
                    smin = 8 * jb

                    def dv_ev(v, h=h, jb=jb, i=i, smin=smin):
                        v.wait_ge(s_pe.h, spair_done[i])
                        if i >= 2:
                            v.wait_ge(s_pe.h, pv_done[i - 2])
                        if h == 0 and jb == 0 and i == 0:
                            v.wait_ge(b.sem("s_g").h, masks_ready)
                        ps = b.psA if i % 2 == 0 else b.psB
                        inst = None
                        for z in range(2):
                            kt = 2 * i + z
                            imm = float(gammas[h] ** (128 * (8 * jb - kt))) * RS128
                            pslot = b.pp[:, (i % 2) * 1024 + z * TL:(i % 2) * 1024 + (z + 1) * TL]
                            inst = v.tensor_scalar(pslot, ps[:, z * TL:(z + 1) * TL],
                                                   b.b1v[:, 32 + h:33 + h], imm,
                                                   ALU.mult, ALU.mult)
                        s_dv.inc(inst)
                    b.on("vector", dv_ev)
                    ev_done[i] = s_dv.v

                    kts = [2 * i, 2 * i + 1]
                    if kts[1] >= smin:
                        def dv_mask(v, i=i, kts=kts, smin=smin, ev=s_dv.v):
                            v.wait_ge(s_dv.h, ev)
                            inst = None
                            for z, kt in enumerate(kts):
                                if kt < smin:
                                    continue
                                sidx = kt - smin
                                pslot = b.pp[:, (i % 2) * 1024 + z * TL:(i % 2) * 1024 + (z + 1) * TL]
                                inst = v.tensor_tensor(pslot, pslot,
                                                       b.ovl[:, sidx * TL:(sidx + 1) * TL], ALU.mult)
                            s_dv.inc(inst)
                        b.on("vector", dv_mask)
                        mask_done[i] = s_dv.v

            ybf = b.pp[:, 2048:2048 + TL]
            ysq = b.pp[:, 2048 + TL:2048 + 2 * TL]
            mstat = b.stat[:, 0:TL]
            msq = b.stat[:, TL:2 * TL]
            tmp2 = b.stat[:, 2 * TL:3 * TL]

            def dv_y(v, h=h, jb=jb, P=P):
                v.wait_ge(s_pe.h, pv_done[P - 1])
                inst = v.tensor_copy(ybf, b.psC)
                s_dv.inc(inst)
            b.on("vector", dv_y)
            ycp = s_dv.v

            def ac_sq(s, ycp=ycp):
                s.wait_ge(s_dv.h, ycp)
                inst = s.activation(ysq, ybf, AF.Square)
                s_ac.inc(inst)
            b.on("scalar", ac_sq)
            sqd = s_ac.v

            def pe_gn(t, sqd=sqd):
                t.wait_ge(s_ac.h, sqd)
                t.matmul(b.psE, b.ones[:], ybf, start=True, stop=True)
                inst = t.matmul(b.psF, b.ones[:], ysq, start=True, stop=True)
                s_pe.inc(inst)
            b.on("tensor", pe_gn)
            gnm = s_pe.v

            def dv_st(v, gnm=gnm):
                v.wait_ge(s_pe.h, gnm)
                v.tensor_scalar_mul(mstat, b.psE, 1.0 / 128.0)
                inst = v.tensor_scalar_mul(msq, b.psF, 1.0 / 128.0)
                b.fence(v, inst)
                inst = v.tensor_tensor(tmp2, mstat, mstat, ALU.mult)
                b.fence(v, inst)
                inst = v.tensor_tensor(msq, msq, tmp2, ALU.subtract)
                s_dv.inc(inst)
            b.on("vector", dv_st)
            vard = s_dv.v

            def ac_rstd(s, vard=vard):
                s.wait_ge(s_dv.h, vard)
                inst = s.activation(msq, msq, AF.Ln, bias=b.vec[:, 62:63])
                b.fence(s, inst)
                inst = s.activation(msq, msq, AF.Exp, scale=-0.5)
                s_ac.inc(inst)
            b.on("scalar", ac_rstd)
            rsd = s_ac.v

            def dv_gn(v, h=h, jb=jb, rsd=rsd):
                v.wait_ge(s_ac.h, rsd)
                inst = v.tensor_tensor(tmp2, ybf, mstat, ALU.subtract)
                b.fence(v, inst)
                inst = v.tensor_tensor(tmp2, tmp2, msq, ALU.mult)
                b.fence(v, inst)
                inst = v.tensor_scalar(
                    dst_buf[:, h * 2048 + jb * TL: h * 2048 + (jb + 1) * TL],
                    tmp2, b.b1v[:, 48 + h:49 + h], b.b1v[:, 64 + h:65 + h],
                    ALU.mult, ALU.add)
                s_dv.inc(inst)
            b.on("vector", dv_gn)
            evac_rec[h * 4 + jb] = s_dv.v
        head_done[h] = s_pe.v

    b.attn_done = s_dv.v



def emit_gate_silu(b: Bld, wr: WRing, wgroup, wsrcs, rhs_fn, pe_extra_wait=None):
    s_pe, s_dv, s_ac = b.sem("s_pe"), b.sem("s_dv"), b.sem("s_ac")
    idxs = [wr.push(wgroup, wsrcs[m], ford=m) for m in range(16)]
    silu_done = {}
    tt_done = {}
    for m in range(16):
        wr.ensure(idxs[m] + 4)
        for tc in range(NTC):
            g = m * 4 + tc
            half = b.psA[:, (g % 2) * TL:(g % 2) * TL + TL]
            gslot = b.pp[:, (g % 2) * TL:(g % 2 + 1) * TL]

            def pe_mm(t, g=g, half=half, idx=idxs[m]):
                if g == 0 and pe_extra_wait is not None:
                    for sem, val in pe_extra_wait:
                        t.wait_ge(sem.h, val)
                wr.wait(t, idx)
                if g >= 2:
                    t.wait_ge(s_ac.h, silu_done[g - 2])
                for k in range(NK):
                    inst = t.matmul(half, wr.slot(idx, k), rhs_fn(k, tc),
                                    start=(k == 0), stop=(k == NK - 1))
                s_pe.inc(inst)
            b.on("tensor", pe_mm)
            mmv = s_pe.v
            if tc == 3:
                wr.consumed(mmv)

            def ac_si(s, g=g, half=half, mmv=mmv, gslot=gslot):
                s.wait_ge(s_pe.h, mmv)
                if g >= 2:
                    s.wait_ge(s_dv.h, tt_done[g - 2])
                inst = s.activation(gslot, half, AF.Silu)
                s_ac.inc(inst)
            b.on("scalar", ac_si)
            silu_done[g] = s_ac.v

            def dv_tt(v, m=m, tc=tc, g=g, gslot=gslot, sl=s_ac.v):
                v.wait_ge(s_ac.h, sl)
                xnk = b.xn[:, m * 2048 + tc * TL: m * 2048 + (tc + 1) * TL]
                inst = v.tensor_tensor(xnk, xnk, gslot, ALU.mult)
                s_dv.inc(inst)
            b.on("vector", dv_tt)
            tt_done[g] = s_dv.v
    b.gate_done = s_dv.v



def decl_l1_attn(nc, pfx=""):
    dr = {}
    for w in ("wq", "wk", "wg", "wp"):
        dr[w + 'f'] = nc.dram_tensor(pfx + w + "f", [16, 128, 16, 128], F32, kind="ExternalInput").ap()
        dr[w + 'c'] = nc.dram_tensor(pfx + w + "c", [16, 128, 16, 128], BF16).ap()
    dr['wvtf'] = nc.dram_tensor(pfx + "wvtf", [4, 128, 16, 512], F32, kind="ExternalInput").ap()
    dr['wvtc'] = nc.dram_tensor(pfx + "wvtc", [4, 128, 16, 512], BF16).ap()
    dr['vecs'] = nc.dram_tensor(pfx + "vecs", [128, 48], F32, kind="ExternalInput").ap()
    dr['rt'] = nc.dram_tensor(pfx + "rt", [128, 128], F32, kind="ExternalInput").ap()
    dr['masks'] = nc.dram_tensor(pfx + "masks", [8, 128, 512], F32, kind="ExternalInput").ap()
    for t in ("cosq", "sinq", "cosk", "sink"):
        dr[t] = nc.dram_tensor(pfx + t, [2048, 2048], F32, kind="ExternalInput").ap()
    dr['qscr'] = nc.dram_tensor(pfx + "qscr", [2048, 2048], BF16).ap()
    dr['kscr'] = nc.dram_tensor(pfx + "kscr", [2048, 2048], BF16).ap()
    dr['vscr'] = nc.dram_tensor(pfx + "vscr", [2048, 2048], BF16).ap()
    dr['kall'] = nc.dram_tensor(pfx + "kall", [4096, 2048], BF16).ap()
    dr['vall'] = nc.dram_tensor(pfx + "vall", [4096, 2048], BF16).ap()
    dr['xnscr'] = nc.dram_tensor(pfx + "xnscr", [2048, 2048], BF16).ap()
    dr['_pfx'] = pfx
    return dr


def fill_l1_attn(dr, pfx=""):
    items = []
    for w in ("wq", "wk", "wg", "wp"):
        for m in range(16):
            items.append((pfx + w, dr[w + 'f'][m], dr[w + 'c'][m]))
    for n in range(4):
        items.append((pfx + 'wvt', dr['wvtf'][n], dr['wvtc'][n]))
    return items


def emit_sub_l1_attn(b, dr, wc, xin, xout):
    pfx = dr.get('_pfx', '')
    s_g = b.sem("s_g")
    s_cc = b.sem("s_cc")
    s_dv = b.sem("s_dv")
    qscr, kscr, vscr, kall, vall = (dr[k] for k in ("qscr", "kscr", "vscr", "kall", "vall"))
    xnscr = dr['xnscr']

    def g_small(g):
        inst = g.dma_start(b.b1v[:, 32:80], dr['vecs'])
        s_g.inc(inst, 16)
        for s8 in range(8):
            inst = g.dma_start(b.ovl[:, s8 * 512:(s8 + 1) * 512], dr['masks'][s8])
            s_g.inc(inst, 16)
        inst = g.dma_start(b.rt[:, :], dr['rt'])
        s_g.inc(inst, 16)
    b.on("gpsimd", g_small)
    small_done = s_g.v
    masks_ready = small_done

    emit_ln(b, xin, xbase_dram=None, xn_dram=xnscr)
    xn_stored = b.sem("s_xn").v

    wr = WRing(b, wc)

    def rhs_xn(k, tc):
        return b.xn[:, k * 2048 + tc * TL: k * 2048 + (tc + 1) * TL]

    base_wait = [(b.sem("s_dv"), b.ln_norm_done), (s_g, small_done),
                 (b.sem("s_pe"), b.ln_aux_free)]
    emit_proj_xpos(b, wr, pfx + 'wq', [dr['wqc'][m] for m in range(16)], rhs_xn,
                   dr['cosq'], dr['sinq'], qscr, pe_extra_wait=base_wait)
    emit_proj_xpos(b, wr, pfx + 'wk', [dr['wkc'][m] for m in range(16)], rhs_xn,
                   dr['cosk'], dr['sink'], kscr)
    ri_k = [r.v for r in b.ri]

    def g_gather_k(g):
        for p in range(8):
            g.wait_ge(b.ri[p].h, ri_k[p])
        g.collective_compute("AllGather", ALU.bypass,
                             replica_groups=[[0, 1], [2, 3], [4, 5], [6, 7]],
                             ins=[kscr.opt()], outs=[kall.opt()]).then_inc(s_cc.h)
        s_cc.v += 1
    b.on("gpsimd", g_gather_k)

    emit_proj_tm(b, wc, pfx + 'wvt', dr['wvtc'], vscr)
    proj_psum_free = b.proj_evac_done
    ri_v = [r.v for r in b.ri]

    def g_gather_v(g):
        for p in range(8):
            g.wait_ge(b.ri[p].h, ri_v[p])
        g.collective_compute("AllGather", ALU.bypass,
                             replica_groups=[[0, 1], [2, 3], [4, 5], [6, 7]],
                             ins=[vscr.opt()], outs=[vall.opt()]).then_inc(s_cc.h)
        s_cc.v += 1
    b.on("gpsimd", g_gather_v)
    gather_done = s_cc.v

    gammas = [1.0 - 2.0 ** (-5.0 - h) for h in range(NH)]
    emit_attn_decay_v2(b, qscr, kall, vall, b.xn, masks_ready, gather_done,
                       proj_psum_free, gammas)

    s_xr = b.sem("s_xr")
    xr0 = s_xr.v

    def sy_xr(sy):
        sy.wait_ge(s_dv.h, b.attn_done)
        sy.wait_ge(b.sem("s_xn").h, xn_stored)
        for k in range(NK):
            inst = sy.dma_start(b.aux[:, k * 2048:(k + 1) * 2048],
                                xnscr[k * 128:(k + 1) * 128, :])
            s_xr.inc(inst, 16)
    b.on("sync", sy_xr)
    xr_done = s_xr.v

    def rhs_aux(k, tc):
        return b.aux[:, k * 2048 + tc * TL: k * 2048 + (tc + 1) * TL]

    emit_gate_silu(b, wr, pfx + 'wg', [dr['wgc'][m] for m in range(16)], rhs_aux,
                   pe_extra_wait=[(s_xr, xr_done), (s_dv, b.attn_done)])

    emit_outproj(b, wr, pfx + 'wp', [dr['wpc'][o] for o in range(16)], b.xn,
                 xin, xout, attn_ready=b.gate_done,
                 xtmp_free=b._xpos_prev_t12)


def build_l1_attn_nc():
    b = Bld("l1attn")
    nc = b.nc
    dr = {}
    dr['xin'] = nc.dram_tensor("xin", [D, 2048], F32, kind="ExternalInput").ap()
    dr['xout'] = nc.dram_tensor("xout", [D, 2048], F32, kind="ExternalOutput").ap()
    dr.update(decl_l1_attn(nc))

    b.start()
    alloc_common(b)
    wc = WCache(b)
    wc.fill(fill_l1_attn(dr))
    emit_sub_l1_attn(b, dr, wc, dr['xin'], dr['xout'])
    emit_finwait(b)
    return b.finish()


def _xpos_tables(hpar, with_gamma):
    C = 2048
    idx = _core_token_idx(hpar).astype(np.float64)
    i2 = np.arange(0, C, 2, dtype=np.float64)
    dim_scale = (i2 + 0.4 * C) / (1.4 * C)
    inv_freq = 1.0 / 10000.0 ** (i2 / C)
    sinus = idx[None, :] * inv_freq[:, None]
    scale = dim_scale[:, None] ** (idx[None, :] / 512.0)
    sin = np.sin(sinus) * scale
    cos = np.cos(sinus) * scale
    sin2 = np.repeat(sin, 2, axis=0)
    cos2 = np.repeat(cos, 2, axis=0)
    if with_gamma:
        col = np.arange(2048)
        pat = (col % 512) // 128 * 256 + (col % 128)
        gam = np.array([1.0 - 2.0 ** (-5.0 - h) for h in range(16)])
        gfac = gam[(np.arange(2048) // 128)][:, None] ** pat[None, :]
        sin2 = sin2 * gfac
        cos2 = cos2 * gfac
    return (np.ascontiguousarray(cos2).astype(np.float32),
            np.ascontiguousarray(sin2).astype(np.float32))


def prep_l1_attn_inputs(inputs, xs):
    wqkv = np.asarray(inputs['L1_attn_w_qkv'])
    wq = _tile_w(np.ascontiguousarray(wqkv[:, 0:2048]))
    wk = _tile_w(np.ascontiguousarray(wqkv[:, 2048:4096]))
    wvt = _tile_rhs_v(np.ascontiguousarray(wqkv[:, 4096:6144]))
    wg = _tile_w(np.asarray(inputs['L1_attn_w_g']))
    wp = _tile_w(np.asarray(inputs['L1_attn_w_proj']))
    gnw = _perpart(inputs['L1_attn_gn_weight'])
    gnb = _perpart(inputs['L1_attn_gn_bias'])
    RT = np.zeros((128, 128), dtype=np.float32)
    for i in range(64):
        RT[2 * i + 1, 2 * i] = -1.0
        RT[2 * i, 2 * i + 1] = 1.0
    gam = np.array([1.0 - 2.0 ** (-5.0 - h) for h in range(16)], dtype=np.float64)
    maps = []
    for c in range(8):
        hpar = c % 2
        row = np.arange(128, dtype=np.float64)
        dvec = (gam[None, :] ** (128.0 * hpar - row[:, None])).astype(np.float32)
        vecs = np.concatenate([dvec, gnw, gnb], axis=1).astype(np.float32)
        cosq, sinq = _xpos_tables(hpar, True)
        cosk, sink = _xpos_tables(hpar, False)
        maps.append({
            "xin": xs[c], "wqf": wq, "wkf": wk, "wvtf": wvt, "wgf": wg, "wpf": wp,
            "vecs": vecs, "rt": RT, "masks": _attn_masks(hpar),
            "cosq": cosq, "sinq": sinq, "cosk": cosk, "sink": sink,
        })
    return maps



def build_fused_nc():
    b = Bld("mkfused")
    nc = b.nc
    xin = nc.dram_tensor("xin", [D, 2048], F32, kind="ExternalInput").ap()
    xout = nc.dram_tensor("xout", [D, 2048], F32, kind="ExternalOutput").ap()
    xA = nc.dram_tensor("xbufA", [D, 2048], F32).ap()
    xB = nc.dram_tensor("xbufB", [D, 2048], F32).ap()

    d_f0 = decl_l0_ffn(nc, "f0_")
    d_a0 = decl_l0_attn(nc, "a0_")
    d_f1 = decl_l1_ffn(nc, "f1_")
    d_a1 = decl_l1_attn(nc, "a1_")
    scr4 = decl_l1_ffn_scratch(nc, "s4_")
    scr6 = decl_l1_ffn_scratch(nc, "s6_")

    b.start()
    alloc_common(b)
    wc = WCache(b)
    s_pe = b.sem("s_pe")
    wc.fill(fill_l0_ffn(d_f0, "f0_"))

    emit_sub_l0_ffn(b, d_f0, wc, xin, xA)
    wc.fill(fill_l0_attn(d_a0, "a0_"), gate=(s_pe, max(0, s_pe.v - 4000)))
    emit_boundary(b)
    emit_sub_l0_attn(b, d_a0, wc, xA, xB)
    wc.fill(fill_l1_ffn(d_f1, "f1_"), gate=(s_pe, max(0, s_pe.v - 3000)))
    emit_boundary(b)
    emit_sub_l0_ffn(b, d_f0, wc, xB, xA)
    emit_boundary(b)
    emit_sub_l1_ffn(b, d_f1, wc, xA, xB, scr4)
    wc.fill(fill_l1_attn(d_a1, "a1_"), gate=(s_pe, max(0, s_pe.v - 3000)))
    emit_boundary(b)
    emit_sub_l1_attn(b, d_a1, wc, xB, xA)
    emit_boundary(b)
    emit_sub_l1_ffn(b, d_f1, wc, xA, xout, scr6)
    emit_finwait(b)
    return b.finish()


def prep_fused_inputs(inputs, xs):
    m_f0 = prep_l0_ffn_inputs(inputs, xs)
    m_a0 = prep_l0_attn_inputs(inputs, xs)
    m_f1 = prep_l1_ffn_inputs(inputs, xs)
    m_a1 = prep_l1_attn_inputs(inputs, xs)
    maps = []
    for c in range(8):
        mp = {"xin": xs[c]}
        for pfx, mm in (("f0_", m_f0[c]), ("a0_", m_a0[c]),
                        ("f1_", m_f1[c]), ("a1_", m_a1[c])):
            for k, v in mm.items():
                if k == "xin":
                    continue
                mp[pfx + k] = v
        maps.append(mp)
    return maps


_NC_CACHE = {}


def kernel_fused(**inputs):
    xs = _shard_x(inputs['L0_attn_query'])
    if 'fused' not in _NC_CACHE:
        _NC_CACHE['fused'] = build_fused_nc()
    nc = _NC_CACHE['fused']
    maps = prep_fused_inputs(inputs, xs)
    res = run_bass_kernel_spmd(nc, maps, core_ids=list(range(8)))
    xs = [np.asarray(res.results[c]["xout"]) for c in range(8)]
    return _unshard_x(xs)


def kernel(**inputs):
    return kernel_fused(**inputs)



def _emit_attn_v2(b: Bld, qscr, kall, vall, dst_buf, masks_ready, gather_done,
                  psum_free, mode, gammas=None, q_ready=None):
    s_pe, s_dv, s_ac, s_cc = (b.sem(n) for n in ("s_pe", "s_dv", "s_ac", "s_cc"))
    s_g = b.sem("s_g")
    VOFS, KOFS, QOFS = 0, 16384, 3072

    if mode == 'softmax':
        def dv_nc1(v):
            inst = v.tensor_tensor(b.vec[:, 32:48], b.vec[:, 0:16], b.vec[:, 16:32], ALU.mult)
            s_dv.inc(inst)
        b.on("vector", dv_nc1)
        nc1 = s_dv.v

        def ac_nc(s):
            s.wait_ge(s_dv.h, nc1)
            inst = s.activation(b.vec[:, 32:48], b.vec[:, 32:48], AF.Ln)
            b.fence(s, inst)
            inst = s.activation(b.vec[:, 32:48], b.vec[:, 32:48], AF.Exp, scale=0.5)
            s_ac.inc(inst)
        b.on("scalar", ac_nc)
        nc2 = s_ac.v

        def dv_nc2(v):
            v.wait_ge(s_ac.h, nc2)
            inst = v.tensor_scalar(b.vec[:, 32:48], b.vec[:, 32:48], -RS128, -10.0,
                                   ALU.mult, ALU.add)
            s_dv.inc(inst)
        b.on("vector", dv_nc2)
        negc_done = s_dv.v
    else:
        negc_done = None

    qfree = {}
    prev_evac = None
    prev_gn = psum_free
    group_done = {}

    for hg in range(4):
        kbase = [b.rk4[hh].v for hh in range(4)]
        vbase = [b.rv4[q].v for q in range(4)]

        def sy_k(sy, hg=hg):
            if hg >= 1:
                sy.wait_ge(s_pe.h, group_done[hg - 1])
            if hg == 0:
                sy.wait_ge(s_cc.h, gather_done)
                sy.wait_ge(s_pe.h, b.ln_aux_free)
            for hh in range(4):
                h = hg * 4 + hh
                for r in range(2):
                    srcap = kall[r * 2048 + h * 128: r * 2048 + (h + 1) * 128, :]
                    dst = b.aux[:, KOFS + hh * 4096 + r * 2048:
                                KOFS + hh * 4096 + (r + 1) * 2048]
                    inst = sy.dma_start(dst, srcap)
                    b.rk4[hh].inc(inst, 16)
        b.on("sync", sy_k)

        def sc_v(s, hg=hg):
            if hg >= 1:
                s.wait_ge(s_pe.h, group_done[hg - 1])
            if hg == 0:
                s.wait_ge(s_cc.h, gather_done)
                s.wait_ge(s_pe.h, b.ln_aux_free)
            for kt in range(32):
                r, j = kt % 2, kt // 2
                srcap = vall[r * 2048 + j * 128: r * 2048 + (j + 1) * 128,
                             hg * 512:(hg + 1) * 512]
                inst = s.dma_start(b.aux[:, VOFS + kt * 512: VOFS + (kt + 1) * 512],
                                   srcap)
                b.rv4[kt // 8].inc(inst, 16)
        b.on("scalar", sc_v)

        for hh in range(4):
            h = hg * 4 + hh
            for jb in range(NTC):
                gi = h * 4 + jb
                gq = gi % 2
                P = 4 * jb + 4

                def sy_q(sy, h=h, jb=jb, gq=gq, gi=gi):
                    if gi == 0 and q_ready is not None:
                        for qc, qv in q_ready:
                            sy.wait_ge(qc.h, qv)
                    if gi >= 2:
                        sy.wait_ge(s_pe.h, qfree[gi - 2])
                    inst = sy.dma_start(
                        b.pp[:, QOFS + gq * TL: QOFS + (gq + 1) * TL],
                        qscr[h * 128:(h + 1) * 128, jb * TL:(jb + 1) * TL])
                    b.rq[gq].inc(inst, 16)
                b.on("sync", sy_q)
                qld = b.rq[gq].v

                pe0, ac0, dv0 = s_pe.v, s_ac.v, s_dv.v
                mask_val = {}
                ev_val = {}
                dvc = dv0
                if mode == 'softmax':
                    for pi in range(4 * jb, P):
                        dvc += 1
                        mask_val[pi] = dvc
                    evac_val = dvc + 1
                else:
                    for i in range(P):
                        dvc += 1
                        ev_val[i] = dvc
                        if i >= 4 * jb:
                            dvc += 1
                            mask_val[i] = dvc
                    ycp_val = dvc + 1
                    vard_val = dvc + 2
                    evac_val = dvc + 3

                def pe_sec(t, h=h, hh=hh, jb=jb, gi=gi, gq=gq, P=P,
                           pe0=pe0, ac0=ac0, dv0=dv0, qld=qld,
                           kbase=list(kbase), vbase=list(vbase), prev_evac=prev_evac,
                           mask_val=dict(mask_val), ev_val=dict(ev_val)):
                    for i in range(P + 1):
                        if i >= 1:
                            pi = i - 1
                            if mode == 'softmax':
                                t.wait_ge(s_ac.h, ac0 + pi + 1)
                                if pi in mask_val:
                                    t.wait_ge(s_dv.h, mask_val[pi])
                            else:
                                t.wait_ge(s_dv.h, mask_val.get(pi, ev_val[pi]))
                            if i == 1 and prev_evac is not None:
                                t.wait_ge(s_dv.h, prev_evac)
                            if i == 1:
                                for q in range(jb + 1):
                                    t.wait_ge(b.rv4[q].h, vbase[q] + 16 * 8)
                            for z in range(2):
                                kt = 2 * pi + z
                                pslot = b.pp[:, (pi % 2) * 1024 + z * TL:(pi % 2) * 1024 + (z + 1) * TL]
                                vt = b.aux[:, VOFS + kt * 512 + hh * 128: VOFS + kt * 512 + (hh + 1) * 128]
                                inst = t.matmul(b.psC, vt, pslot,
                                                start=(pi == 0 and z == 0),
                                                stop=(pi == P - 1 and z == 1))
                                if mode == 'softmax':
                                    inst = t.matmul(b.psD, b.ones[:], pslot,
                                                    start=(pi == 0 and z == 0),
                                                    stop=(pi == P - 1 and z == 1))
                        if i < P:
                            if i == 0:
                                t.wait_ge(b.rq[gq].h, qld)
                                if jb == 0 and hh == 0:
                                    if gi == 0 and psum_free is not None:
                                        t.wait_ge(s_dv.h, psum_free)
                                if jb == 0:
                                    t.wait_ge(b.rk4[hh].h, kbase[hh] + 32)
                            if i >= 2:
                                t.wait_ge(s_ac.h, ac0 + i - 1) if mode == 'softmax' \
                                    else t.wait_ge(s_dv.h, ev_val[i - 2])
                            ps = b.psA if i % 2 == 0 else b.psB
                            for z in range(2):
                                kt = 2 * i + z
                                kcol = (kt % 2) * 2048 + (kt // 2) * 128
                                ktile = b.aux[:, KOFS + hh * 4096 + kcol: KOFS + hh * 4096 + kcol + 128]
                                inst = t.matmul(ps[:, z * TL:(z + 1) * TL], ktile,
                                                b.pp[:, QOFS + gq * TL: QOFS + (gq + 1) * TL],
                                                start=True, stop=True)
                            s_pe.inc(inst)
                        else:
                            s_pe.inc(inst)
                b.on("tensor", pe_sec)
                assert s_pe.v == pe0 + P + 1
                pe_end = s_pe.v
                qfree[gi] = pe_end

                if mode == 'softmax':
                    def ac_sec(s, h=h, jb=jb, P=P, pe0=pe0, dv0=dv0, gi=gi):
                        if gi == 0 and negc_done is not None:
                            s.wait_ge(s_dv.h, negc_done)
                        for i in range(P):
                            s.wait_ge(s_pe.h, pe0 + i + 1)
                            ps = b.psA if i % 2 == 0 else b.psB
                            inst = s.activation(b.pp[:, (i % 2) * 1024:(i % 2 + 1) * 1024],
                                                ps[:], AF.Exp, scale=RS128,
                                                bias=b.vec[:, 32 + h:33 + h])
                            s_ac.inc(inst)
                    b.on("scalar", ac_sec)

                    def dv_sec(v, h=h, jb=jb, P=P, pe0=pe0, ac0=ac0, gi=gi,
                               mask_val=dict(mask_val)):
                        if gi == 0:
                            v.wait_ge(s_g.h, masks_ready)
                        for pi in range(4 * jb, P):
                            v.wait_ge(s_ac.h, ac0 + pi + 1)
                            inst = None
                            for z in range(2):
                                kt = 2 * pi + z
                                if kt < 8 * jb:
                                    continue
                                sidx = kt - 8 * jb
                                pslot = b.pp[:, (pi % 2) * 1024 + z * TL:(pi % 2) * 1024 + (z + 1) * TL]
                                inst = v.tensor_tensor(pslot, pslot,
                                                       b.ovl[:, sidx * TL:(sidx + 1) * TL], ALU.mult)
                            s_dv.inc(inst)
                        v.wait_ge(s_pe.h, pe0 + P + 1)
                        inst = v.reciprocal(b.stat[:, 0:TL], b.psD)
                        b.fence(v, inst)
                        inst = v.tensor_tensor(
                            dst_buf[:, h * 2048 + jb * TL: h * 2048 + (jb + 1) * TL],
                            b.psC, b.stat[:, 0:TL], ALU.mult)
                        s_dv.inc(inst)
                    b.on("vector", dv_sec)
                else:
                    def dv_sec(v, h=h, hh=hh, jb=jb, P=P, pe0=pe0, gi=gi,
                               mask_val=dict(mask_val), ev_val=dict(ev_val)):
                        if gi == 0:
                            v.wait_ge(s_g.h, masks_ready)
                        for i in range(P):
                            v.wait_ge(s_pe.h, pe0 + i + 1)
                            ps = b.psA if i % 2 == 0 else b.psB
                            inst = None
                            for z in range(2):
                                kt = 2 * i + z
                                imm = float(gammas[h] ** (128 * (8 * jb - kt))) * RS128
                                pslot = b.pp[:, (i % 2) * 1024 + z * TL:(i % 2) * 1024 + (z + 1) * TL]
                                inst = v.tensor_scalar(pslot, ps[:, z * TL:(z + 1) * TL],
                                                       b.b1v[:, 32 + h:33 + h], imm,
                                                       ALU.mult, ALU.mult)
                            s_dv.inc(inst)
                            if i >= 4 * jb:
                                inst2 = None
                                for z in range(2):
                                    kt = 2 * i + z
                                    if kt < 8 * jb:
                                        continue
                                    sidx = kt - 8 * jb
                                    pslot = b.pp[:, (i % 2) * 1024 + z * TL:(i % 2) * 1024 + (z + 1) * TL]
                                    inst2 = v.tensor_tensor(pslot, pslot,
                                                            b.ovl[:, sidx * TL:(sidx + 1) * TL], ALU.mult)
                                s_dv.inc(inst2)
                        ybf = b.pp[:, 2048:2048 + TL]
                        v.wait_ge(s_pe.h, pe0 + P + 1)
                        inst = v.tensor_copy(ybf, b.psC)
                        s_dv.inc(inst)
                    b.on("vector", dv_sec)

                    ycp = s_dv.v

                    def ac_gn1(s, ycp=ycp):
                        s.wait_ge(s_dv.h, ycp)
                        inst = s.activation(b.pp[:, 2048 + TL:2048 + 2 * TL],
                                            b.pp[:, 2048:2048 + TL], AF.Square)
                        s_ac.inc(inst)
                    b.on("scalar", ac_gn1)
                    sqd = s_ac.v

                    def pe_gn(t, sqd=sqd, prev_gn=prev_gn):
                        t.wait_ge(s_ac.h, sqd)
                        if prev_gn is not None:
                            t.wait_ge(s_dv.h, prev_gn)
                        t.matmul(b.psE, b.ones[:], b.pp[:, 2048:2048 + TL],
                                 start=True, stop=True)
                        inst = t.matmul(b.psF, b.ones[:], b.pp[:, 2048 + TL:2048 + 2 * TL],
                                        start=True, stop=True)
                        s_pe.inc(inst)
                    b.on("tensor", pe_gn)
                    gnm = s_pe.v
                    qfree[gi] = gnm

                    def dv_gn2(v, gnm=gnm):
                        v.wait_ge(s_pe.h, gnm)
                        v.tensor_scalar_mul(b.stat[:, 0:TL], b.psE, 1.0 / 128.0)
                        inst = v.tensor_scalar_mul(b.stat[:, TL:2 * TL], b.psF, 1.0 / 128.0)
                        b.fence(v, inst)
                        inst = v.tensor_tensor(b.stat[:, 2 * TL:3 * TL],
                                               b.stat[:, 0:TL], b.stat[:, 0:TL], ALU.mult)
                        b.fence(v, inst)
                        inst = v.tensor_tensor(b.stat[:, TL:2 * TL], b.stat[:, TL:2 * TL],
                                               b.stat[:, 2 * TL:3 * TL], ALU.subtract)
                        s_dv.inc(inst)
                    b.on("vector", dv_gn2)
                    vard = s_dv.v
                    prev_gn = vard

                    def ac_gn2(s, vard=vard):
                        s.wait_ge(s_dv.h, vard)
                        inst = s.activation(b.stat[:, TL:2 * TL], b.stat[:, TL:2 * TL],
                                            AF.Ln, bias=b.vec[:, 62:63])
                        b.fence(s, inst)
                        inst = s.activation(b.stat[:, TL:2 * TL], b.stat[:, TL:2 * TL],
                                            AF.Exp, scale=-0.5)
                        s_ac.inc(inst)
                    b.on("scalar", ac_gn2)
                    rsd = s_ac.v

                    def dv_gn3(v, h=h, jb=jb, rsd=rsd):
                        v.wait_ge(s_ac.h, rsd)
                        inst = v.tensor_tensor(b.stat[:, 2 * TL:3 * TL],
                                               b.pp[:, 2048:2048 + TL],
                                               b.stat[:, 0:TL], ALU.subtract)
                        b.fence(v, inst)
                        inst = v.tensor_tensor(b.stat[:, 2 * TL:3 * TL],
                                               b.stat[:, 2 * TL:3 * TL],
                                               b.stat[:, TL:2 * TL], ALU.mult)
                        b.fence(v, inst)
                        inst = v.tensor_scalar(
                            dst_buf[:, h * 2048 + jb * TL: h * 2048 + (jb + 1) * TL],
                            b.stat[:, 2 * TL:3 * TL],
                            b.b1v[:, 48 + h:49 + h], b.b1v[:, 64 + h:65 + h],
                            ALU.mult, ALU.add)
                        s_dv.inc(inst)
                    b.on("vector", dv_gn3)

                prev_evac = s_dv.v
        group_done[hg] = s_pe.v

    b.attn_done = s_dv.v


def emit_attn_v2(b, qscr, kall, vall, dst_buf, masks_ready, gather_done, psum_free,
                 q_ready=None):
    _emit_attn_v2(b, qscr, kall, vall, dst_buf, masks_ready, gather_done,
                  psum_free, 'softmax', q_ready=q_ready)


def emit_attn_decay_v2(b, qscr, kall, vall, dst_buf, masks_ready, gather_done,
                       psum_free, gammas, q_ready=None):
    _emit_attn_v2(b, qscr, kall, vall, dst_buf, masks_ready, gather_done,
                  psum_free, 'decay', gammas, q_ready=q_ready)



def emit_proj_tm(b: Bld, wc: WCache, wgroup, wsrc, dscr, bvrow_sb=None,
                 pe_extra_wait=None):
    s_pe, s_dv = b.sem("s_pe"), b.sem("s_dv")
    s_k = b.sem("s_k2")
    ri_base = [r.v for r in b.ri]
    pe_base = s_pe.v
    dv_base = s_dv.v
    io_store = {}
    evac_done = {}
    prev_n_pe = None
    for n in range(4):
        rw_t = [b.rw[0].v + 16]

        def sc_w(s, n=n, prev_n_pe=prev_n_pe):
            if prev_n_pe is not None:
                s.wait_ge(s_pe.h, prev_n_pe)
            else:
                s.wait_ge(s_pe.h, pe_base)
            wc.wait(s, wgroup, n)
            inst = s.dma_start(b.wring[:, :].rearrange("p (k c) -> p k c", k=16),
                               wsrc[n])
            b.rw[0].inc(inst, 16)
        b.on("scalar", sc_w)

        for tt in range(NTT):
            g = n * 16 + tt
            half = (b.psA if (g % 4) < 2 else b.psB)[:, (g % 2) * TL:(g % 2) * TL + TL]

            def pe_mm(t, n=n, tt=tt, g=g, half=half, rw_t=list(rw_t)):
                if tt == 0:
                    if g == 0 and pe_extra_wait is not None:
                        for sem, val in pe_extra_wait:
                            t.wait_ge(sem.h, val)
                    t.wait_ge(b.rw[0].h, rw_t[0])
                if g >= 4:
                    t.wait_ge(s_dv.h, evac_done[g - 4])
                else:
                    t.wait_ge(s_dv.h, dv_base)
                for k in range(NK):
                    sl, kk = k // 4, k % 4
                    inst = t.matmul(half,
                                    b.xn[:, k * 2048 + tt * 128: k * 2048 + (tt + 1) * 128],
                                    b.wring[:, sl * 2048 + kk * 512: sl * 2048 + (kk + 1) * 512],
                                    start=(k == 0), stop=(bvrow_sb is None and k == NK - 1))
                if bvrow_sb is not None:
                    inst = t.matmul(half, b.onesrow[0:1, :],
                                    bvrow_sb[0:1, n * TL:(n + 1) * TL],
                                    start=False, stop=True)
                s_pe.inc(inst)
            b.on("tensor", pe_mm)
            mmv = s_pe.v

            slot = b.io[:, (g % 8) * TL:(g % 8 + 1) * TL]

            def dv_ev(v, g=g, half=half, mmv=mmv, slot=slot):
                v.wait_ge(s_pe.h, mmv)
                if g >= 8:
                    v.wait_ge(b.ri[g % 8].h, io_store[g - 8])
                else:
                    v.wait_ge(b.ri[g % 8].h, ri_base[g % 8])
                inst = v.tensor_copy(slot, half)
                s_dv.inc(inst)
            b.on("vector", dv_ev)
            evac_done[g] = s_dv.v

            def sy_st(sy, n=n, tt=tt, g=g, slot=slot, ev=s_dv.v):
                sy.wait_ge(s_dv.h, ev)
                inst = sy.dma_start(
                    dscr[tt * 128:(tt + 1) * 128, n * TL:(n + 1) * TL], slot)
                io_store[g] = b.ri[g % 8].inc(inst, 16)
            b.on("sync", sy_st)
        prev_n_pe = s_pe.v

    b.proj_evac_done = evac_done[63]
